# version 16
# baseline (speedup 1.0000x reference)
import numpy as np
import ml_dtypes
import concourse.bass as bass
from concourse.bass import IndirectOffsetOnAxis
import concourse.mybir as mybir
from concourse.bass_utils import run_bass_kernel_spmd
from contextlib import ExitStack

F32 = mybir.dt.float32
BF16 = mybir.dt.bfloat16
AF = mybir.ActivationFunctionType
ALU = mybir.AluOpType
AX = mybir.AxisListType.X

T, C, H, D = 4096, 1024, 16, 64
NCC = 8
TCA = 256
NCHA = T // TCA
NTA = TCA // 128
TCB = 256
NCHB = T // TCB
NTB = TCB // 128
E = 8
J = 4096
NJG = J // 512
EXM05 = float(np.exp(np.float32(-0.5)))


def _cap(t):
    return max(128, -(-3 * t // 16 // 128) * 128)


CAP = _cap(T)


def set_dims(t=4096, j=4096):
    global T, NCHA, NCHB, J, NJG, CAP
    T, J = t, j
    NCHA = T // TCA
    NCHB = T // TCB
    NJG = J // 512
    CAP = _cap(T)


class Eng:

    def __init__(self, name, queue=False, ops=None):
        self.name = name
        self.queue = queue
        self.n = 0
        self.ops = [] if ops is None else ops
        self.seen = {}

    def self_wait(self):
        if self.n > 0:
            nm, vv = self.name, self.n
            self.ops.append(lambda e, S: e.wait_ge(S[nm], vv))

    def wait(self, other, val):
        if val <= 0:
            return
        if self.seen.get(other.name, 0) >= val:
            return
        self.seen[other.name] = val
        on, vv = other.name, int(val)
        self.ops.append(lambda e, S: e.wait_ge(S[on], vv))

    def emit(self, fn, tag=0):
        if tag:
            self.n += tag
            sname, step = self.name, tag

            def run(e, S, fn=fn, sname=sname, step=step):
                r = fn(e, S)
                if isinstance(r, (list, tuple)):
                    u = step // len(r)
                    assert u * len(r) == step, (step, len(r))
                    for inst in r:
                        inst.then_inc(S[sname], u)
                else:
                    r.then_inc(S[sname], step)

            self.ops.append(run)
        else:
            self.ops.append(lambda e, S, fn=fn: fn(e, S))
        return self.n


class Buf:

    def __init__(self, name=""):
        self.name = name
        self.w = (None, 0)
        self.r = []


def group(eng, reads, writes, fn, tag):
    if eng.queue:
        eng.self_wait()
    for b in reads:
        if b.w[0] is not None:
            eng.wait(b.w[0], b.w[1])
    for b in writes:
        for (oe, ov) in b.r:
            eng.wait(oe, ov)
        if b.w[0] is not None:
            eng.wait(b.w[0], b.w[1])
    eng.emit(fn, tag=tag)
    for b in reads:
        b.r.append((eng, eng.n))
    for b in writes:
        b.w = (eng, eng.n)
        b.r = []


def colmaj(v):
    return np.ascontiguousarray(np.asarray(v, np.float32).reshape(NCC, 128).T)


def prep_core_inputs(b, I):
    _ = (I["L0_attn_v0"], I["L0_attn_v1"], I["L0_attn_v2"])
    g = lambda k: np.asarray(I[k], np.float32)
    w0 = g("L0_attn_w0").reshape(H)
    a0 = g("L0_attn_a0").reshape(H)
    w2 = g("L0_attn_w2")
    a2 = g("L0_attn_a2")
    selx = np.kron(np.eye(2, dtype=np.float32), np.ones((64, 64), np.float32))
    m = {
        "x": np.ascontiguousarray(g("L0_attn_x")[b]),
        "wr": g("L0_attn_Wr"), "wk": g("L0_attn_Wk"),
        "wv": g("L0_attn_Wv"), "wo": g("L0_attn_Wo"),
        "w1": g("L0_attn_w1"), "a1": g("L0_attn_a1"), "g1": g("L0_attn_g1"),
        "g2": g("L0_attn_g2"),
        "w2x": np.ascontiguousarray(np.repeat(w2, D, axis=1)),
        "a2x": np.ascontiguousarray(np.repeat(a2, D, axis=1)),
        "gwt": np.ascontiguousarray(g("L0_ffn_gate_w").T),
        "wupt": np.ascontiguousarray(g("L0_ffn_w_up")[b % 8].T),
        "wdn": np.ascontiguousarray(g("L0_ffn_w_down")[b % 8]),
        "mixr": colmaj(g("L0_attn_x_r")), "mixw": colmaj(g("L0_attn_x_w")),
        "mixk": colmaj(g("L0_attn_x_k")), "mixv": colmaj(g("L0_attn_x_v")),
        "mixa": colmaj(g("L0_attn_x_a")), "mixg": colmaj(g("L0_attn_x_g")),
        "kkc": colmaj(g("L0_attn_k_k")),
        "lnw": colmaj(g("L0_attn_ln_w")), "lnb": colmaj(g("L0_attn_ln_b")),
        "rkc": colmaj(g("L0_attn_r_k").reshape(C)),
        "w0x": colmaj(np.repeat(w0, D)), "a0x": colmaj(np.repeat(a0, D)),
        "selx": selx.astype(ml_dtypes.bfloat16),
        "selx64": (selx / 64.0).astype(ml_dtypes.bfloat16),
        "w0row": np.ascontiguousarray(
            colmaj(np.repeat(w0, D)).T.reshape(1, C)).astype(ml_dtypes.bfloat16),
        "a0row": np.ascontiguousarray(
            colmaj(np.repeat(a0, D)).T.reshape(1, C)).astype(ml_dtypes.bfloat16),
        "kkw": np.ascontiguousarray(
            np.tile(g("L0_attn_k_k").reshape(1, C), (128, 1))).astype(ml_dtypes.bfloat16),
        "rkexp": np.ascontiguousarray(
            np.repeat(colmaj(g("L0_attn_r_k").reshape(C)), TCA, axis=1)).astype(
                ml_dtypes.bfloat16),
        "idnf": np.eye(128, dtype=np.float32),
        "basec": np.ascontiguousarray(
            np.tile((np.arange(16, dtype=np.float32) * CAP - 1.0)[None, :],
                    (128, 1))),
    }
    return m


def build(taps=(), ncores=8):
    nc = bass.Bass("TRN2", target_bir_lowering=False, debug=False)
    dt_in = {}
    def din(name, shape, dt=F32):
        dt_in[name] = nc.dram_tensor(name, list(shape), dt, kind="ExternalInput").ap()
        return dt_in[name]

    x_d = din("x", (T, C))
    wr_d = din("wr", (C, C)); wk_d = din("wk", (C, C))
    wv_d = din("wv", (C, C)); wo_d = din("wo", (C, C))
    w1_d = din("w1", (C, 64)); a1_d = din("a1", (C, 64)); g1_d = din("g1", (C, 64))
    g2_d = din("g2", (64, C)); w2x_d = din("w2x", (64, C)); a2x_d = din("a2x", (64, C))
    gwt_d = din("gwt", (C, E))
    wupt_d = din("wupt", (J, C)); wdn_d = din("wdn", (J, C))
    cols_d = {k: din(k, (128, NCC)) for k in
              ["mixr", "mixw", "mixk", "mixv", "mixa", "mixg",
               "kkc", "lnw", "lnb", "rkc", "w0x", "a0x"]}
    selx_d = din("selx", (128, 128), BF16)
    selx64_d = din("selx64", (128, 128), BF16)
    idnf_d = din("idnf", (128, 128))
    basec_d = din("basec", (128, 16))
    w0row_d = din("w0row", (1, C), BF16)
    a0row_d = din("a0row", (1, C), BF16)
    kkw_d = din("kkw", (128, C), BF16)
    rkexp_d = din("rkexp", (128, NCC * TCA), BF16)

    xn_d = nc.dram_tensor("xn_d", [T, C], BF16, kind="Internal").ap()
    wcown_d = nc.dram_tensor("wcown_d", [128, NCC * 1024], BF16, kind="Internal").ap()
    xng_d = nc.dram_tensor("xng_d", [16 * CAP, C], BF16, kind="Internal").ap()
    yext_d = nc.dram_tensor("yext_d", [16 * CAP, C], BF16, kind="Internal").ap()
    wcall_d = nc.dram_tensor("wcall_d", [128 * E, NCC * 1024], BF16, kind="Internal").ap()
    s_d = nc.dram_tensor("s_d", [T, C], BF16, kind="Internal").ap()
    out_d = nc.dram_tensor("out", [T, C], F32, kind="ExternalOutput").ap()
    tap_d = {}
    if "xn" in taps:
        tap_d["tap_xn"] = nc.dram_tensor("tap_xn", [T, C], BF16, kind="ExternalOutput").ap()
    if "s" in taps:
        tap_d["tap_s"] = nc.dram_tensor("tap_s", [T, C], BF16, kind="ExternalOutput").ap()
    if "wc" in taps:
        tap_d["tap_wc"] = nc.dram_tensor("tap_wc", [128, E * NCC * 1024], BF16,
                                         kind="ExternalOutput").ap()
    if "ffn" in taps:
        tap_d["tap_ffn"] = nc.dram_tensor("tap_ffn", [T, C], BF16,
                                          kind="ExternalOutput").ap()
    if "mask" in taps:
        tap_d["tap_mask"] = nc.dram_tensor("tap_mask", [T, E], F32,
                                           kind="ExternalOutput").ap()
    if "lg" in taps:
        tap_d["tap_lg"] = nc.dram_tensor("tap_lg", [T, E], F32,
                                         kind="ExternalOutput").ap()

    QS, QG, PE, ACT, DVE = (Eng("QS", queue=True), Eng("QG", queue=True), Eng("PE"), Eng("ACT"), Eng("DVE"))
    PL = Eng("PL", ops=QG.ops)
    CC = Eng("CC", ops=QG.ops)
    ENGS = [QS, QG, PL, CC, PE, ACT, DVE]

    def barrier(exclude=()):
        snap = [(e, e.n) for e in ENGS if e.name not in exclude]
        for e in ENGS:
            if e.name in exclude:
                continue
            if e.queue:
                e.self_wait()
            for (o, v) in snap:
                e.wait(o, v)

    es = ExitStack()
    sbuf = lambda name, shape, dt: es.enter_context(nc.sbuf_tensor(name, shape, dt))
    with es:
        sems = {}
        for e in ENGS:
            sems[e.name] = es.enter_context(nc.semaphore(e.name))

        cols_sb = {k: sbuf(k + "_sb", [128, NCC], F32) for k in cols_d}
        selx_sb = sbuf("selx_sb", [128, 128], BF16)
        selx64_sb = sbuf("selx64_sb", [128, 128], BF16)
        gwt_sb = sbuf("gwt_sb", [128, NCC * E], BF16)
        carry = sbuf("carry", [128, NCC], F32)
        prevc = sbuf("prevc", [128, NCC], F32)
        sumc = sbuf("sumc", [128, NTA], F32)
        sqc = sbuf("sqc", [128, NTA], F32)
        meanc = sbuf("meanc", [128, NTA], F32)
        varc = sbuf("varc", [128, NTA], F32)
        tmpc = sbuf("tmpc", [128, NTA], F32)
        rstdc = sbuf("rstdc", [128, NTA], F32)
        nmrc = sbuf("nmrc", [128, NTA], F32)
        eps5 = sbuf("eps5", [128, 1], F32)
        eps24 = sbuf("eps24", [128, 1], F32)
        mask12_sb = sbuf("mask12_sb", [128, (T // 128) * 16], BF16)
        lgs = sbuf("lgs", [128, 8], F32)
        m1s = sbuf("m1s", [128, 1], F32)
        eqs = sbuf("eqs", [128, 8], F32)
        lg2s = sbuf("lg2s", [128, 8], F32)
        m2s = sbuf("m2s", [128, 1], F32)

        pb = [es.enter_context(nc.psum_tensor(f"pb{i}", [128, 512], F32)) for i in range(8)]
        BK = [Buf(f"bank{i}") for i in range(8)]

        b_cols = Buf("cols"); b_selx = Buf("selx"); b_gwt = Buf("gwt")
        b_carry = Buf("carry"); b_prevc = Buf("prevc")
        b_stat = Buf("stat")
        b_xnd = [Buf(f"xnd{c}") for c in range(NCHA)]
        b_sdd = [Buf(f"sdd{c}") for c in range(NCHA)]

        def ld_smalls(e, S):
            L = [e.dma_start(out=cols_sb[k][:], in_=cols_d[k]) for k in cols_d]
            L.append(e.dma_start(out=selx_sb[:], in_=selx_d))
            L.append(e.dma_start(out=selx64_sb[:], in_=selx64_d))
            return L
        group(QS, [], [b_cols, b_selx], ld_smalls, tag=16 * (len(cols_d) + 2))
        group(QG, [], [b_gwt],
              lambda e, S: e.dma_start(
                  out=gwt_sb[:].rearrange("p (cc e) -> p cc e", e=E),
                  in_=gwt_d.rearrange("(cc p) e -> p cc e", p=128)), tag=16)
        b_eps = Buf("eps")
        b_m12 = Buf("m12"); b_lgx = Buf("lgx")
        group(DVE, [], [b_carry, b_prevc, b_eps],
              lambda e, S: [e.memset(carry[:], 0.0), e.memset(prevc[:], 0.0),
                            e.memset(eps5[:], 1e-5), e.memset(eps24[:], 1e-24)], tag=4)

        esW = ExitStack()
        sbw = lambda name, shape, dt: esW.enter_context(nc.sbuf_tensor(name, shape, dt))
        wr_sb = sbw("wr_sb", [128, NCC * C], BF16)
        wk_sb = sbw("wk_sb", [128, NCC * C], BF16)
        wv_sb = sbw("wv_sb", [128, NCC * C], BF16)
        wo_sb = sbw("wo_sb", [128, NCC * C], BF16)
        w1_sb = sbw("w1_sb", [128, NCC * 64], BF16)
        a1_sb = sbw("a1_sb", [128, NCC * 64], BF16)
        g1_sb = sbw("g1_sb", [128, NCC * 64], BF16)
        g2_sb = sbw("g2_sb", [64, C], BF16)
        w2x_sb = sbw("w2x_sb", [64, C], BF16)
        a2x_sb = sbw("a2x_sb", [64, C], BF16)
        x_sb = sbw("x_sb", [128, 2 * NTA * C], BF16)
        b_w = {k: Buf(k) for k in ["wr", "wk", "wv", "wo", "w1", "a1", "g1",
                                   "g2", "w2x", "a2x", "w1v", "a1v", "g1v",
                                   "rows", "kkw", "rkexp", "mxc"]}
        b_x = [Buf(f"x{p}") for p in range(2)]

        def wcast(dst, src_d, n):
            def fn(e, S, d=dst, s=src_d, n=n):
                return e.dma_start(
                    out=d[:].rearrange("p (cc n) -> p cc n", n=n),
                    in_=s.rearrange("(cc p) n -> p cc n", p=128))
            return fn
        group(QG, [], [b_w["w1"]], wcast(w1_sb, w1_d, 64), 16)
        group(QG, [], [b_w["a1"]], wcast(a1_sb, a1_d, 64), 16)
        group(QG, [], [b_w["g1"]], wcast(g1_sb, g1_d, 64), 16)
        group(QG, [], [b_w["g2"]], lambda e, S: e.dma_start(out=g2_sb[:], in_=g2_d), 16)
        group(QG, [], [b_w["w2x"]], lambda e, S: e.dma_start(out=w2x_sb[:], in_=w2x_d), 16)
        group(QG, [], [b_w["a2x"]], lambda e, S: e.dma_start(out=a2x_sb[:], in_=a2x_d), 16)

        def load_x(c):
            def ld(e, S, c=c):
                return [e.dma_start(
                    out=x_sb[:, ((c % 2) * NTA + tt) * C:((c % 2) * NTA + tt + 1) * C],
                    in_=x_d[c * TCA + tt * 128:c * TCA + (tt + 1) * 128, :])
                    for tt in range(NTA)]
            group(QG, [], [b_x[c % 2]], ld, 16 * NTA)
        for cc_ in range(min(2, NCHA)):
            load_x(cc_)
        group(QG, [], [b_w["wr"]], wcast(wr_sb, wr_d, C), 16)
        group(QG, [], [b_w["wk"]], wcast(wk_sb, wk_d, C), 16)
        group(QG, [], [b_w["wv"]], wcast(wv_sb, wv_d, C), 16)
        group(QG, [], [b_w["wo"]], wcast(wo_sb, wo_d, C), 16)

        b_wcall = Buf("wcall")
        esC = ExitStack()
        with esC:
            wcst = esC.enter_context(nc.sbuf_tensor("wcst", [128, NCC * 1024], BF16))
            wdn_bf = esC.enter_context(nc.sbuf_tensor("wdn_bf", [128, 32 * 1024], BF16))
            wup_q = esC.enter_context(nc.sbuf_tensor("wup_q", [128, 2 * 8 * 512], BF16))
            b_wupq = [Buf("wupq0"), Buf("wupq1")]
            b_wdnq = [Buf(f"wdnq{q}") for q in range(4)]
            b_wcown = Buf("wcown"); b_wcst = Buf("wcst")
            NJC = J // 128

            def c0_lddn(q):
                def fn(e, S, q=q):
                    jgs = NJC // 4
                    src_d2 = wdn_d.rearrange("(jg p) n -> jg p n", p=128)[
                        q * jgs:(q + 1) * jgs].rearrange("jg p n -> p jg n")
                    return e.dma_start(
                        out=wdn_bf[:, q * jgs * 1024:(q + 1) * jgs * 1024].rearrange(
                            "p (jg n) -> p jg n", n=1024),
                        in_=src_d2)
                group(QG, [], [b_wdnq[q]], fn, 16)

            for cinh in range(2):
                for q in range(4):
                    jgs = NJC // 4
                    p = q % 2
                    if cinh == 0:
                        c0_lddn(q)
                    def c0_ldup(e, S, cinh=cinh, q=q, p=p, jgs=jgs):
                        src_u = wupt_d.rearrange("(jg p) n -> jg p n", p=128)[
                            q * jgs:(q + 1) * jgs, :,
                            cinh * 512:(cinh + 1) * 512].rearrange("jg p n -> p jg n")
                        return e.dma_start(
                            out=wup_q[:, p * jgs * 512:(p + 1) * jgs * 512].rearrange(
                                "p (jg n) -> p jg n", n=512),
                            in_=src_u)
                    group(QG, [], [b_wupq[p]], c0_ldup, 16)
                    def c0_mm(e, S, cinh=cinh, q=q, p=p, jgs=jgs):
                        for jq in range(jgs):
                            jg = q * jgs + jq
                            for cc in range(4):
                                for half in range(2):
                                    inst = e.matmul(
                                        pb[cc * 2 + half][:, 0:512],
                                        wup_q[:, (p * jgs + jq) * 512 + cc * 128:
                                              (p * jgs + jq) * 512 + cc * 128 + 128],
                                        wdn_bf[:, jg * 1024 + half * 512:
                                               jg * 1024 + half * 512 + 512],
                                        start=(jg == 0), stop=(jg == NJC - 1))
                        return inst
                    group(PE, [b_wupq[p], b_wdnq[q]],
                          [BK[i] for i in range(8)] if q == 0 else [], c0_mm, 1)
                    if q != 0:
                        for i in range(8):
                            BK[i].w = (PE, PE.n)
                def c0_ev(e, S, cinh=cinh):
                    return [e.copy(
                        out=wcst[:, (cinh * 4 + cc) * 1024 + half * 512:
                                 (cinh * 4 + cc) * 1024 + half * 512 + 512],
                        in_=pb[cc * 2 + half][:, 0:512])
                        for cc in range(4) for half in range(2)]
                group(ACT, [BK[i] for i in range(8)],
                      [b_wcst] + [BK[i] for i in range(8)], c0_ev, 8)

            group(QG, [b_wcst], [b_wcown],
                  lambda e, S: e.dma_start(out=wcown_d, in_=wcst[:]), 16)
            group(CC, [b_wcown], [b_wcall],
                  lambda e, S: e.collective_compute(
                      "AllGather", ALU.bypass,
                      [list(range(ncores))], [wcown_d], [wcall_d]), 1)
        barrier(exclude=("CC",))

        TW = TCA + 16
        PW = 2 * TCA
        esA = ExitStack()
        sbufA = lambda name, shape, dt: esA.enter_context(nc.sbuf_tensor(name, shape, dt))
        with esA:
            w1v_sb = sbufA("w1v_sb", [128, 2 * NCC * 64], BF16)
            a1v_sb = sbufA("a1v_sb", [128, 2 * NCC * 64], BF16)
            g1v_sb = sbufA("g1v_sb", [128, 2 * NCC * 64], BF16)
            w0row_sb = sbufA("w0row_sb", [1, C], BF16)
            a0row_sb = sbufA("a0row_sb", [1, C], BF16)
            ones_sb = sbufA("ones_sb", [1, TCA], BF16)
            kkw_sb = sbufA("kkw_sb", [128, C], BF16)
            rkexp_sb = sbufA("rkexp_sb", [128, NCC * TCA], BF16)
            mxc_sb = sbufA("mxc_sb", [128, 6 * NCC], F32)
            xntm_sb = sbufA("xntm_sb", [128, 2 * NTA * C], BF16)
            xnfm_sb = sbufA("xnfm_sb", [128, 3 * NCC * TW], BF16)
            sx_sb = sbufA("sx_sb", [128, NCC * TCA], BF16)
            mix_sb = sbufA("mix_sb", [128, 2 * NCC * TCA], BF16)
            kscl_sb = sbufA("kscl_sb", [128, NCC * TCA], BF16)
            vh_sb = sbufA("vh_sb", [128, NCC * TCA], BF16)
            r_sb = sbufA("r_sb", [128, NCC * TCA], BF16)
            g_sb = sbufA("g_sb", [128, NCC * TCA], BF16)
            ab_sb = sbufA("ab_sb", [128, NCC * TCA], BF16)
            og_sb = sbufA("og_sb", [128, NCC * TCA], BF16)
            wb_sb = sbufA("wb_sb", [128, NCC * TCA], F32)
            tw_sb = sbufA("tw_sb", [64, TCA], BF16)
            ta_sb = sbufA("ta_sb", [64, TCA], BF16)
            tg_sb = sbufA("tg_sb", [64, TCA], BF16)
            lnscr = sbufA("lnscr", [128, C], F32)
            stm_sb = sbufA("stm_sb", [128, NTA * C], BF16)
            scr = {}
            for nm in ["rsn", "kk", "kv", "st", "t1", "xw"]:
                scr[nm] = sbufA("scr_" + nm, [128, 2 * PW], F32)
            scrb = {}
            for nm in ["z", "xwb", "xw2"]:
                scrb[nm] = sbufA("scrb_" + nm, [128, 2 * PW], BF16)

            b_xntm = [Buf(f"xntm{p}") for p in range(2)]
            b_xnfm = [Buf(f"xnfm{p}") for p in range(3)]
            b_sx = Buf("sx")
            b_mix = [Buf("mix0"), Buf("mix1")]
            b_kscl = Buf("kscl"); b_vh = Buf("vh"); b_r = Buf("r")
            b_g = Buf("g"); b_ab = Buf("ab"); b_og = Buf("og"); b_wb = Buf("wb")
            b_tw = Buf("t1lr"); b_ta = Buf("t2lr"); b_tg = Buf("t3lr")
            b_lnscr = Buf("lnscr")
            b_stm = [Buf("stm"), Buf("stm")]
            b_stm[1] = b_stm[0]
            b_scr = {nm: [Buf(nm + "0"), Buf(nm + "1")] for nm in
                     list(scr) + list(scrb)}

            group(QS, [], [b_w["rows"]], lambda e, S: [
                e.dma_start(out=w0row_sb[:], in_=w0row_d),
                e.dma_start(out=a0row_sb[:], in_=a0row_d),
                e.dma_start(out=kkw_sb[:], in_=kkw_d),
                e.dma_start(out=rkexp_sb[:], in_=rkexp_d)], 64)
            group(DVE, [], [b_w["rows"]],
                  lambda e, S: e.memset(ones_sb[:], 1.0), 1)

            def kfold(e, S):
                return [e.tensor_tensor(wk_sb[:, cc * C:(cc + 1) * C],
                                        wk_sb[:, cc * C:(cc + 1) * C], kkw_sb[:],
                                        ALU.mult) for cc in range(NCC)]
            group(DVE, [b_w["wk"], b_w["rows"]], [b_w["wk"]], kfold, NCC)
            def mxc(e, S):
                L = []
                for i, mk in enumerate(["mixw", "mixa", "mixg"]):
                    L.append(e.tensor_scalar(mxc_sb[:, (2 * i) * NCC:(2 * i + 1) * NCC],
                                             cols_sb[mk][:], 1.0, None, ALU.add))
                    L.append(e.tensor_scalar(mxc_sb[:, (2 * i + 1) * NCC:(2 * i + 2) * NCC],
                                             cols_sb[mk][:], -1.0, None, ALU.mult))
                return L
            group(DVE, [b_cols], [b_w["mxc"]], mxc, 6)
            def lrv(e, S):
                L = []
                for i, (wsb, vsb) in enumerate([(w1_sb, w1v_sb), (a1_sb, a1v_sb),
                                                (g1_sb, g1v_sb)]):
                    for v in range(2):
                        for cc in range(NCC):
                            L.append(e.tensor_scalar(
                                vsb[:, (v * NCC + cc) * 64:(v * NCC + cc + 1) * 64],
                                wsb[:, cc * 64:(cc + 1) * 64],
                                mxc_sb[:, (2 * i + v) * NCC + cc:
                                       (2 * i + v) * NCC + cc + 1],
                                None, ALU.mult))
                return L
            group(DVE, [b_w["w1"], b_w["a1"], b_w["g1"], b_w["mxc"]],
                  [b_w["w1v"], b_w["a1v"], b_w["g1v"]], lrv, 6 * NCC)

            def xt(c, tt):
                return x_sb[:, ((c % 2) * NTA + tt) * C:((c % 2) * NTA + tt + 1) * C]
            def xntm(c, tt):
                return xntm_sb[:, ((c % 2) * NTA + tt) * C:((c % 2) * NTA + tt + 1) * C]
            def fmtile(c, ci):
                off = ((c % 3) * NCC + ci) * TW
                return xnfm_sb[:, off:off + TW]
            def xnv(c, ci):
                return fmtile(c, ci)[:, 16:TW]
            def xnsh(c, ci):
                return fmtile(c, ci)[:, 15:TW - 1]
            def sxv(ci):
                return sx_sb[:, ci * TCA:(ci + 1) * TCA]
            def mixv(sl, ci):
                return mix_sb[:, (sl * NCC + ci) * TCA:(sl * NCC + ci + 1) * TCA]
            def bchunk(buf, ci, w=TCA):
                return buf[:, ci * w:(ci + 1) * w]
            def bpair(buf, pp):
                return buf[:, pp * PW:(pp + 1) * PW]
            def scrv(nm, pp):
                t = scr[nm] if nm in scr else scrb[nm]
                return t[:, (pp % 2) * PW:((pp % 2) + 1) * PW]
            def scrh(nm, pp, h):
                return scrv(nm, pp)[:, h * TCA:(h + 1) * TCA]
            def col(k, ci):
                return cols_sb[k][:, ci:ci + 1]
            def wslice(wsb, cin, j0, jw):
                return wsb[:, cin * C + j0:cin * C + j0 + jw]

            def ln0_tail(c):
                bx, bxn, bfm = b_x[c % 2], b_xntm[c % 2], b_xnfm[c % 3]
                bst = b_stat
                for tt in range(NTA):
                    group(ACT, [bx], [b_lnscr, bst],
                          lambda e, S, tt=tt: e.activation(
                              out=lnscr[:], in_=xt(c, tt), func=AF.Square,
                              accum_out=sqc[:, tt:tt + 1]), 1)
                    group(ACT, [bx], [b_lnscr, bst],
                          lambda e, S, tt=tt: e.activation(
                              out=lnscr[:], in_=xt(c, tt), func=AF.Identity,
                              accum_out=sumc[:, tt:tt + 1]), 1)
                group(DVE, [bst], [bst], lambda e, S: [
                    e.tensor_scalar(meanc[:], sumc[:], 1.0 / C, None, ALU.mult),
                    e.tensor_scalar(varc[:], sqc[:], 1.0 / C, None, ALU.mult)], 2)
                group(DVE, [bst], [bst],
                      lambda e, S: e.tensor_tensor(tmpc[:], meanc[:], meanc[:], ALU.mult), 1)
                group(DVE, [bst], [bst],
                      lambda e, S: e.tensor_tensor(varc[:], varc[:], tmpc[:], ALU.subtract), 1)
                group(ACT, [bst, b_eps], [bst],
                      lambda e, S: e.activation(out=tmpc[:], in_=varc[:], func=AF.Ln,
                                                bias=eps5[:, 0:1]), 1)
                group(ACT, [bst], [bst],
                      lambda e, S: e.activation(out=rstdc[:], in_=tmpc[:], func=AF.Exp,
                                                scale=-0.5), 1)
                group(DVE, [bst], [bst],
                      lambda e, S: e.scalar_tensor_tensor(
                          nmrc[:], meanc[:], -1.0, rstdc[:], ALU.mult, ALU.mult), 1)
                def apply(e, S):
                    return [e.activation(out=xntm(c, tt), in_=xt(c, tt), func=AF.Identity,
                                         scale=rstdc[:, tt:tt + 1], bias=nmrc[:, tt:tt + 1])
                            for tt in range(NTA)]
                group(ACT, [bx, bst], [bxn], apply, NTA)
                def store(e, S):
                    return [e.dma_start(out=xn_d[c * TCA + tt * 128:c * TCA + (tt + 1) * 128, :],
                                        in_=xntm(c, tt)) for tt in range(NTA)]
                group(QS, [bxn], [b_xnd[c]], store, 16 * NTA)
                def tpr(e, S):
                    return [e.dma_start(out=xnv(c, ci),
                                        in_=xn_d[c * TCA:(c + 1) * TCA, ci * 128:(ci + 1) * 128],
                                        transpose=True) for ci in range(NCC)]
                group(QS, [b_xnd[c]], [bfm], tpr, 16 * NCC)

            for cc_ in range(min(2, NCHA)):
                ln0_tail(cc_)

            for c in range(NCHA):
                if c + 2 < NCHA:
                    load_x(c + 2)
                bfm = b_xnfm[c % 3]
                if c == 0:
                    group(DVE, [], [bfm], lambda e, S: [
                        e.memset(fmtile(0, ci)[:, 0:16], 0.0) for ci in range(NCC)], NCC)
                else:
                    bprev = b_xnfm[(c - 1) % 3]
                    group(DVE, [bprev], [bfm], lambda e, S, c=c: [
                        e.tensor_copy(fmtile(c, ci)[:, 0:16],
                                      fmtile(c - 1, ci)[:, TW - 16:TW])
                        for ci in range(NCC)], NCC)
                group(DVE, [bfm], [b_sx], lambda e, S, c=c: [
                    e.tensor_tensor(sxv(ci), xnv(c, ci), xnsh(c, ci), ALU.subtract)
                    for ci in range(NCC)], NCC)

                for pi, (vsb, vbuf, tdst, tb) in enumerate(
                        [(w1v_sb, b_w["w1v"], tw_sb, b_tw),
                         (a1v_sb, b_w["a1v"], ta_sb, b_ta),
                         (g1v_sb, b_w["g1v"], tg_sb, b_tg)]):
                    def lr_mm(e, S, c=c, vsb=vsb):
                        L = []
                        for cin in range(NCC):
                            L.append(e.matmul(
                                pb[0][0:64, 0:TCA],
                                vsb[:, cin * 64:(cin + 1) * 64],
                                xnv(c, cin), start=(cin == 0), stop=False))
                            L.append(e.matmul(
                                pb[0][0:64, 0:TCA],
                                vsb[:, (NCC + cin) * 64:(NCC + cin + 1) * 64],
                                xnsh(c, cin), start=False,
                                stop=(cin == NCC - 1)))
                        return L[-1]
                    group(PE, [bfm, vbuf], [BK[0]], lr_mm, 1)
                    group(ACT, [BK[0]], [tb],
                          lambda e, S, tdst=tdst: e.activation(
                              out=tdst[:], in_=pb[0][0:64, 0:TCA], func=AF.Tanh), 1)
                for pi, (xsb, xb, tsrc, tsb, rowsb, dst, db) in enumerate(
                        [(w2x_sb, b_w["w2x"], tw_sb, b_tw, w0row_sb, wb_sb, b_wb),
                         (a2x_sb, b_w["a2x"], ta_sb, b_ta, a0row_sb, ab_sb, b_ab),
                         (g2_sb, b_w["g2"], tg_sb, b_tg, None, g_sb, b_g)]):
                    for pp in range(4):
                        bk = pp % 2
                        def s2_mm(e, S, pp=pp, bk=bk, xsb=xsb, tsrc=tsrc, rowsb=rowsb):
                            L = []
                            for h in range(2):
                                ci = 2 * pp + h
                                L.append(e.matmul(pb[bk][:, h * TCA:(h + 1) * TCA],
                                                  xsb[:, ci * 128:(ci + 1) * 128],
                                                  tsrc[:], start=True,
                                                  stop=(rowsb is None)))
                                if rowsb is not None:
                                    L.append(e.matmul(
                                        pb[bk][:, h * TCA:(h + 1) * TCA],
                                        rowsb[0:1, ci * 128:(ci + 1) * 128],
                                        ones_sb[0:1, :], start=False, stop=True))
                            return L[-1]
                        group(PE, [tsb, xb, b_w["rows"]], [BK[bk]], s2_mm, 1)
                        def sig_ev(e, S, pp=pp, bk=bk, dst=dst):
                            return e.activation(out=bpair(dst, pp), in_=pb[bk][:, 0:PW],
                                                func=AF.Sigmoid)
                        group(ACT, [BK[bk]], [db], sig_ev, 1)
                group(DVE, [b_wb], [b_wb],
                      lambda e, S: e.tensor_scalar(wb_sb[:], wb_sb[:], EXM05,
                                                   None, ALU.mult), 1)

                MIX3 = [("mixk", "k"), ("mixv", "v"), ("mixr", "r")]
                WMAP = {"k": (wk_sb, b_w["wk"], kscl_sb, b_kscl),
                        "v": (wv_sb, b_w["wv"], vh_sb, b_vh),
                        "r": (wr_sb, b_w["wr"], r_sb, b_r)}
                def emit_mix(mi):
                    mk, tag = MIX3[mi]
                    sl = mi % 2
                    def mix_fn(e, S, c=c, mk=mk, sl=sl):
                        return [e.scalar_tensor_tensor(
                            mixv(sl, ci), sxv(ci), col(mk, ci), xnv(c, ci),
                            ALU.mult, ALU.add) for ci in range(NCC)]
                    group(DVE, [b_sx, bfm, b_cols], [b_mix[sl]], mix_fn, NCC)
                def emit_bigmm(mi):
                    mk, tag = MIX3[mi]
                    sl = mi % 2
                    wsb, wbf, dst, db = WMAP[tag]
                    for pp in range(4):
                        bk = pp % 2
                        def big_mm(e, S, sl=sl, pp=pp, bk=bk, wsb=wsb):
                            for h in range(2):
                                co = 2 * pp + h
                                for cin in range(NCC):
                                    inst = e.matmul(pb[bk][:, h * TCA:(h + 1) * TCA],
                                                    wslice(wsb, cin, co * 128, 128),
                                                    mixv(sl, cin),
                                                    start=(cin == 0), stop=(cin == NCC - 1))
                            return inst
                        group(PE, [b_mix[sl], wbf], [BK[bk]], big_mm, 1)
                        def c_ev(e, S, pp=pp, bk=bk, dst=dst):
                            return e.copy(out=bpair(dst, pp), in_=pb[bk][:, 0:PW])
                        group(ACT, [BK[bk]], [db], c_ev, 1)
                emit_mix(0); emit_mix(1)
                emit_bigmm(0)
                emit_mix(2)
                emit_bigmm(1)
                emit_bigmm(2)

                def st_ksq(pp, s2, bkA, bkB, bkC):
                    group(ACT, [b_kscl], [b_scr["z"][s2]],
                          lambda e, S, pp=pp: e.activation(out=scrv("z", pp),
                                                           in_=bpair(kscl_sb, pp),
                                                           func=AF.Square), 1)
                def st_ssq(pp, s2, bkA, bkB, bkC):
                    group(PE, [b_scr["z"][s2], b_selx], [BK[bkA]],
                          lambda e, S, pp=pp, bkA=bkA: [
                              e.matmul(pb[bkA][:, h * TCA:(h + 1) * TCA], selx_sb[:],
                                       scrh("z", pp, h), start=True, stop=True)
                              for h in range(2)][-1], 1)
                def st_rsnln(pp, s2, bkA, bkB, bkC):
                    group(ACT, [BK[bkA], b_eps], [b_scr["rsn"][s2]],
                          lambda e, S, pp=pp, bkA=bkA: e.activation(
                              out=scrv("rsn", pp), in_=pb[bkA][:, 0:PW], func=AF.Ln,
                              bias=eps24[:, 0:1]), 1)
                def st_rsnexp(pp, s2, bkA, bkB, bkC):
                    group(ACT, [b_scr["rsn"][s2]], [b_scr["rsn"][s2]],
                          lambda e, S, pp=pp: e.activation(
                              out=scrv("rsn", pp), in_=scrv("rsn", pp),
                              func=AF.Exp, scale=-0.5), 1)
                def st_kk(pp, s2, bkA, bkB, bkC):
                    group(PL, [b_scr["rsn"][s2], b_kscl], [b_scr["kk"][s2]],
                          lambda e, S, pp=pp: e.tensor_tensor(
                              scrv("kk", pp), bpair(kscl_sb, pp), scrv("rsn", pp),
                              ALU.mult), 1)
                def st_kv1(pp, s2, bkA, bkB, bkC):
                    group(PL, [b_scr["kk"][s2], b_vh], [b_scr["kv"][s2]],
                          lambda e, S, pp=pp: e.tensor_tensor(
                              scrv("kv", pp), scrv("kk", pp), bpair(vh_sb, pp),
                              ALU.mult), 1)
                def st_kv2(pp, s2, bkA, bkB, bkC):
                    group(PL, [b_scr["kv"][s2], b_ab], [b_scr["kv"][s2]],
                          lambda e, S, pp=pp: e.tensor_tensor(
                              scrv("kv", pp), scrv("kv", pp), bpair(ab_sb, pp),
                              ALU.mult), 1)
                def st_scan(pp, s2, bkA, bkB, bkC):
                    for h in range(2):
                        ci = 2 * pp + h
                        group(DVE, [b_wb, b_scr["kv"][s2], b_carry], [b_scr["st"][s2]],
                              lambda e, S, pp=pp, h=h, ci=ci: e.tensor_tensor_scan(
                                  scrh("st", pp, h), bchunk(wb_sb, ci),
                                  scrh("kv", pp, h), carry[:, ci:ci + 1],
                                  ALU.mult, ALU.add), 1)
                        group(DVE, [b_scr["st"][s2]], [b_carry],
                              lambda e, S, pp=pp, h=h, ci=ci: e.tensor_copy(
                                  carry[:, ci:ci + 1],
                                  scrh("st", pp, h)[:, TCA - 1:TCA]), 1)
                def st_z1(pp, s2, bkA, bkB, bkC):
                    group(PL, [b_r, b_scr["kk"][s2]], [b_scr["z"][s2]],
                          lambda e, S, pp=pp: e.tensor_tensor(
                              scrv("z", pp), bpair(r_sb, pp), scrv("kk", pp),
                              ALU.mult), 1)
                def st_z2(pp, s2, bkA, bkB, bkC):
                    group(PL, [b_scr["z"][s2], b_w["rkexp"]], [b_scr["z"][s2]],
                          lambda e, S, pp=pp: e.tensor_tensor(
                              scrv("z", pp), scrv("z", pp), bpair(rkexp_sb, pp),
                              ALU.mult), 1)
                def st_bb(pp, s2, bkA, bkB, bkC):
                    group(PE, [b_scr["z"][s2], b_selx], [BK[bkA]],
                          lambda e, S, pp=pp, bkA=bkA: [
                              e.matmul(pb[bkA][:, h * TCA:(h + 1) * TCA], selx_sb[:],
                                       scrh("z", pp, h), start=True, stop=True)
                              for h in range(2)][-1], 1)
                def st_t1(pp, s2, bkA, bkB, bkC):
                    group(PL, [b_scr["st"][s2], b_r], [b_scr["t1"][s2]],
                          lambda e, S, pp=pp: e.tensor_tensor(
                              scrv("t1", pp), scrv("st", pp), bpair(r_sb, pp),
                              ALU.mult), 1)
                def st_xw1(pp, s2, bkA, bkB, bkC):
                    group(DVE, [b_vh, BK[bkA]], [b_scr["xw"][s2]],
                          lambda e, S, pp=pp, bkA=bkA: e.tensor_tensor(
                              scrv("xw", pp), bpair(vh_sb, pp), pb[bkA][:, 0:PW],
                              ALU.mult), 1)
                def st_xw2(pp, s2, bkA, bkB, bkC):
                    group(PL, [b_scr["t1"][s2], b_scr["xw"][s2]], [b_scr["xw"][s2]],
                          lambda e, S, pp=pp: e.tensor_tensor(
                              scrv("xw", pp), scrv("xw", pp), scrv("t1", pp),
                              ALU.add), 1)
                def st_xwb(pp, s2, bkA, bkB, bkC):
                    group(ACT, [b_scr["xw"][s2]], [b_scr["xwb"][s2], b_scr["xw2"][s2]],
                          lambda e, S, pp=pp: [
                              e.copy(out=scrv("xwb", pp), in_=scrv("xw", pp)),
                              e.activation(out=scrv("xw2", pp), in_=scrv("xw", pp),
                                           func=AF.Square)], 2)
                def st_mb(pp, s2, bkA, bkB, bkC):
                    group(PE, [b_scr["xwb"][s2], b_selx], [BK[bkB]],
                          lambda e, S, pp=pp, bkB=bkB: [
                              e.matmul(pb[bkB][:, h * TCA:(h + 1) * TCA], selx64_sb[:],
                                       scrh("xwb", pp, h), start=True, stop=True)
                              for h in range(2)][-1], 1)
                def st_s64(pp, s2, bkA, bkB, bkC):
                    group(PE, [b_scr["xw2"][s2], b_selx], [BK[bkC]],
                          lambda e, S, pp=pp, bkC=bkC: [
                              e.matmul(pb[bkC][:, h * TCA:(h + 1) * TCA], selx64_sb[:],
                                       scrh("xw2", pp, h), start=True, stop=True)
                              for h in range(2)][-1], 1)
                def st_mbsq(pp, s2, bkA, bkB, bkC):
                    group(ACT, [BK[bkB]], [b_scr["rsn"][s2]],
                          lambda e, S, pp=pp, bkB=bkB: e.activation(
                              out=scrv("rsn", pp), in_=pb[bkB][:, 0:PW],
                              func=AF.Square), 1)
                def st_var(pp, s2, bkA, bkB, bkC):
                    group(DVE, [BK[bkC], b_scr["rsn"][s2]], [b_scr["kv"][s2]],
                          lambda e, S, pp=pp, bkC=bkC: e.tensor_tensor(
                              scrv("kv", pp), pb[bkC][:, 0:PW], scrv("rsn", pp),
                              ALU.subtract), 1)
                def st_var2(pp, s2, bkA, bkB, bkC):
                    group(DVE, [b_scr["kv"][s2]], [b_scr["kv"][s2]],
                          lambda e, S, pp=pp: e.tensor_scalar(
                              scrv("kv", pp), scrv("kv", pp), 0.0, 1e-5,
                              ALU.max, ALU.add), 1)
                def st_gnln(pp, s2, bkA, bkB, bkC):
                    group(ACT, [b_scr["kv"][s2]], [b_scr["rsn"][s2]],
                          lambda e, S, pp=pp: e.activation(
                              out=scrv("rsn", pp), in_=scrv("kv", pp), func=AF.Ln), 1)
                def st_gnexp(pp, s2, bkA, bkB, bkC):
                    group(ACT, [b_scr["rsn"][s2]], [b_scr["kk"][s2]],
                          lambda e, S, pp=pp: e.activation(
                              out=scrv("kk", pp), in_=scrv("rsn", pp),
                              func=AF.Exp, scale=-0.5), 1)
                def st_gsub(pp, s2, bkA, bkB, bkC):
                    group(DVE, [b_scr["xw"][s2], BK[bkB]], [b_scr["t1"][s2]],
                          lambda e, S, pp=pp, bkB=bkB: e.tensor_tensor(
                              scrv("t1", pp), scrv("xw", pp), pb[bkB][:, 0:PW],
                              ALU.subtract), 1)
                def st_gmul(pp, s2, bkA, bkB, bkC):
                    group(PL, [b_scr["t1"][s2], b_scr["kk"][s2]], [b_scr["t1"][s2]],
                          lambda e, S, pp=pp: e.tensor_tensor(
                              scrv("t1", pp), scrv("t1", pp), scrv("kk", pp),
                              ALU.mult), 1)
                def st_lnr(pp, s2, bkA, bkB, bkC):
                    group(ACT, [b_scr["t1"][s2], b_cols], [b_scr["st"][s2]],
                          lambda e, S, pp=pp: [
                              e.activation(out=scrh("st", pp, h),
                                           in_=scrh("t1", pp, h),
                                           func=AF.Identity,
                                           scale=col("lnw", 2 * pp + h),
                                           bias=col("lnb", 2 * pp + h))
                              for h in range(2)], 2)
                def st_og(pp, s2, bkA, bkB, bkC):
                    group(PL, [b_scr["st"][s2], b_g], [b_og],
                          lambda e, S, pp=pp: e.tensor_tensor(
                              bpair(og_sb, pp), scrv("st", pp), bpair(g_sb, pp),
                              ALU.mult), 1)

                STAGES = [st_ksq, st_ssq, st_rsnln, st_rsnexp, st_kk, st_kv1,
                          st_kv2, st_scan, st_z1, st_z2, st_bb, st_t1, st_xw1,
                          st_xw2, st_xwb, st_mb, st_s64, st_mbsq, st_var,
                          st_var2, st_gnln, st_gnexp, st_gsub, st_gmul,
                          st_lnr, st_og]
                for wv in range(2):
                    for st_fn in STAGES:
                        for pp in (2 * wv, 2 * wv + 1):
                            s2 = pp % 2
                            st_fn(pp, s2, [2, 5][s2], [3, 6][s2], [4, 7][s2])

                for tt in range(NTA):
                    for half in range(2):
                        bk = [2, 5, 3, 6][(tt * 2 + half) % 4]
                        def s_mm(e, S, tt=tt, half=half, bk=bk):
                            for cin in range(NCC):
                                inst = e.matmul(pb[bk][:, 0:512],
                                                bchunk(og_sb, cin)[:, tt * 128:(tt + 1) * 128],
                                                wslice(wo_sb, cin, half * 512, 512),
                                                start=(cin == 0), stop=(cin == NCC - 1))
                            return inst
                        group(PE, [b_og, b_w["wo"]], [BK[bk]], s_mm, 1)
                        def s_ev(e, S, c=c, tt=tt, half=half, bk=bk):
                            return e.copy(
                                out=stm_sb[:, tt * C + half * 512:
                                           tt * C + half * 512 + 512],
                                in_=pb[bk][:, 0:512])
                        group(ACT, [BK[bk]], [b_stm[c % 2]], s_ev, 1)
                def s_store(e, S, c=c):
                    return [e.dma_start(
                        out=s_d[c * TCA + tt * 128:c * TCA + (tt + 1) * 128, :],
                        in_=stm_sb[:, tt * C:(tt + 1) * C]) for tt in range(NTA)]
                group(QS, [b_stm[c % 2]], [b_sdd[c]], s_store, 16 * NTA)

                for tt in range(NTA):
                    gt = c * NTA + tt
                    def lg_mm(e, S, c=c, tt=tt):
                        for cin in range(NCC):
                            inst = e.matmul(pb[1][:, 0:E],
                                            xnv(c, cin)[:, tt * 128:(tt + 1) * 128],
                                            gwt_sb[:, cin * E:(cin + 1) * E],
                                            start=(cin == 0), stop=(cin == NCC - 1))
                        return inst
                    group(PE, [bfm, b_gwt], [BK[1]], lg_mm, 1)
                    group(DVE, [BK[1]], [b_lgx, BK[1]],
                          lambda e, S: e.tensor_copy(lgs[:], pb[1][:, 0:E]), 1)
                    group(DVE, [b_lgx], [b_lgx],
                          lambda e, S: e.reduce_max(m1s[:, 0:1], lgs[:], axis=AX), 1)
                    group(DVE, [b_lgx], [b_lgx],
                          lambda e, S: e.tensor_scalar(eqs[:], lgs[:],
                                                       m1s[:, 0:1], None, ALU.is_ge), 1)
                    group(DVE, [b_lgx], [b_lgx],
                          lambda e, S: e.scalar_tensor_tensor(
                              lg2s[:], eqs[:], -1e30, lgs[:], ALU.mult, ALU.add), 1)
                    group(DVE, [b_lgx], [b_lgx],
                          lambda e, S: e.reduce_max(m2s[:, 0:1], lg2s[:], axis=AX), 1)
                    def m12_fn(e, S, gt=gt):
                        return [
                            e.tensor_copy(mask12_sb[:, gt * 16:gt * 16 + 8], eqs[:]),
                            e.tensor_scalar(mask12_sb[:, gt * 16 + 8:gt * 16 + 16],
                                            lgs[:], m2s[:, 0:1], None, ALU.is_ge)]
                    group(DVE, [b_lgx], [b_m12, b_lgx], m12_fn, 2)
                    group(DVE, [b_lgx, b_m12], [b_m12],
                          lambda e, S, gt=gt: e.tensor_tensor(
                              mask12_sb[:, gt * 16 + 8:gt * 16 + 16],
                              mask12_sb[:, gt * 16 + 8:gt * 16 + 16],
                              eqs[:], ALU.subtract), 1)

                if c + 2 < NCHA:
                    ln0_tail(c + 2)

            barrier()

        esW.close()
        esCB = ExitStack()
        with esCB:
            wc_sb = esCB.enter_context(nc.sbuf_tensor("wc_sb", [128, E * NCC * 1024], BF16))
            b_wc = Buf("wc")

            def wc(e_, cin):
                off = (e_ * NCC + cin) * 1024
                return wc_sb[:, off:off + 1024]

            def wc_ld(e, S):
                return [e.dma_start(
                    out=wc_sb[:, ee * NCC * 1024:(ee + 1) * NCC * 1024],
                    in_=wcall_d[ee * 128:(ee + 1) * 128, :]) for ee in range(E)]
            group(QS, [b_wcall], [b_wc], wc_ld, 16 * E)

            if "wc" in taps:
                group(QG, [b_wc], [],
                      lambda e, S: e.dma_start(out=tap_d["tap_wc"], in_=wc_sb[:]), 16)

            idx_i = esCB.enter_context(nc.sbuf_tensor("idx_i", [128, 2 * (T // 128)],
                                                      mybir.dt.int32))
            esR = ExitStack()
            idnf_sb = esR.enter_context(nc.sbuf_tensor("idnf_sb", [128, 128], F32))
            basec_sb = esR.enter_context(nc.sbuf_tensor("basec_sb", [128, 16], F32))
            ind_fm = esR.enter_context(nc.sbuf_tensor("ind_fm", [16, T], F32))
            zer_fm = esR.enter_context(nc.sbuf_tensor("zer_fm", [16, T], BF16))
            cnt_fm = esR.enter_context(nc.sbuf_tensor("cnt_fm", [16, T], F32))
            cnt_tm = esR.enter_context(nc.sbuf_tensor("cnt_tm", [128, (T // 128) * 16], F32))
            idxf = esR.enter_context(nc.sbuf_tensor("idxf", [128, 16], F32))
            idxr = esR.enter_context(nc.sbuf_tensor("idxr", [128, 2], F32))
            mstg = esR.enter_context(nc.sbuf_tensor("mstg", [128, 16], F32))
            b_route = Buf("route"); b_idx = Buf("idx")
            NTT = T // 128

            group(QS, [], [b_route], lambda e, S: [
                e.dma_start(out=idnf_sb[:], in_=idnf_d),
                e.dma_start(out=basec_sb[:], in_=basec_d)], 32)
            group(DVE, [], [b_route],
                  lambda e, S: e.memset(zer_fm[:], 0.0), 1)
            for gt in range(NTT):
                bk = gt % 2
                group(DVE, [b_m12], [b_route],
                      lambda e, S, gt=gt: e.tensor_copy(
                          mstg[:], mask12_sb[:, gt * 16:(gt + 1) * 16]), 1)
                group(PE, [b_route], [BK[bk]],
                      lambda e, S, bk=bk: e.transpose(
                          pb[bk][0:16, 0:128], mstg[:], idnf_sb[:]), 1)
                group(ACT, [BK[bk]], [b_route, BK[bk]],
                      lambda e, S, gt=gt, bk=bk: e.copy(
                          out=ind_fm[:, gt * 128:(gt + 1) * 128],
                          in_=pb[bk][0:16, 0:128]), 1)
            group(DVE, [b_route], [b_route],
                  lambda e, S: e.tensor_tensor_scan(
                      cnt_fm[:], zer_fm[:], ind_fm[:], 0.0, ALU.add, ALU.add), 1)
            group(DVE, [b_route], [b_route],
                  lambda e, S: e.tensor_scalar(cnt_fm[:], cnt_fm[:], float(CAP - 1),
                                               None, ALU.min), 1)
            for gt in range(NTT):
                bk = gt % 2
                group(PE, [b_route], [BK[bk]],
                      lambda e, S, gt=gt, bk=bk: e.transpose(
                          pb[bk][:, 0:16], cnt_fm[:, gt * 128:(gt + 1) * 128],
                          idnf_sb[0:16, 0:16]), 1)
                group(ACT, [BK[bk]], [b_route, BK[bk]],
                      lambda e, S, gt=gt, bk=bk: e.copy(
                          out=cnt_tm[:, gt * 16:(gt + 1) * 16], in_=pb[bk][:, 0:16]), 1)
            for gt in range(NTT):
                group(DVE, [b_route, b_m12], [b_route],
                      lambda e, S, gt=gt: e.tensor_tensor(
                          idxf[:], cnt_tm[:, gt * 16:(gt + 1) * 16], basec_sb[:],
                          ALU.add), 1)
                group(DVE, [b_route, b_m12], [b_route],
                      lambda e, S, gt=gt: e.tensor_tensor(
                          idxf[:], idxf[:], mask12_sb[:, gt * 16:(gt + 1) * 16],
                          ALU.mult), 1)
                group(DVE, [b_route], [b_route],
                      lambda e, S, gt=gt: [
                          e.reduce_sum(idxr[:, 0:1], idxf[:, 0:8], axis=AX),
                          e.reduce_sum(idxr[:, 1:2], idxf[:, 8:16], axis=AX)], 2)
                group(DVE, [b_route], [b_idx, b_route],
                      lambda e, S, gt=gt: [
                          e.tensor_copy(idx_i[:, gt:gt + 1], idxr[:, 0:1]),
                          e.tensor_copy(idx_i[:, NTT + gt:NTT + gt + 1],
                                        idxr[:, 1:2])], 2)
            xstg = esR.enter_context(nc.sbuf_tensor("xstg", [128, 2 * C], BF16))
            b_xstg = [Buf("xstg0"), Buf("xstg1")]
            for gt in range(NTT):
                par = gt % 2
                group(QS, [b_xnd[NCHA - 1]], [b_xstg[par]],
                      lambda e, S, gt=gt, par=par: e.dma_start(
                          out=xstg[:, par * C:(par + 1) * C],
                          in_=xn_d[gt * 128:(gt + 1) * 128, :]), 16)
                def xsc(e, S, gt=gt, par=par):
                    return [e.indirect_dma_start(
                        out=xng_d,
                        out_offset=IndirectOffsetOnAxis(
                            idx_i[:, rk * NTT + gt:rk * NTT + gt + 1], 0),
                        in_=xstg[:, par * C:(par + 1) * C],
                        in_offset=None) for rk in range(2)]
                group(QG, [b_idx, b_xstg[par]], [b_route], xsc, 32)

            barrier(exclude=("CC",))
            esR.close()

            esB = ExitStack()
            with esB:
                sbufB = lambda name, shape, dt: esB.enter_context(nc.sbuf_tensor(name, shape, dt))
                xngf_sb = sbufB("xngf_sb", [128, 2 * NCC * CAP // 2], BF16)
                y_sb = sbufB("y_sb", [128, 2 * C], BF16)
                xb_sb = sbufB("xb_sb", [128, 2 * NTB * C], F32)
                sb_sb = sbufB("sb_sb", [128, 2 * NTB * C], BF16)
                f1_sb = sbufB("f1_sb", [128, C], BF16)
                f2_sb = sbufB("f2_sb", [128, C], BF16)
                acc_sb = sbufB("acc_sb", [128, C], BF16)
                sg_sb = sbufB("sg_sb", [128, C], BF16)
                prod_sb = sbufB("prod_sb", [128, C], BF16)
                b_xngf = [Buf("xngf0"), Buf("xngf1")]
                b_y = [Buf("y0"), Buf("y1")]
                b_xb = [Buf(f"xb{p}") for p in range(2)]
                b_sb = [Buf(f"sb{p}") for p in range(2)]
                b_f1 = Buf("f1"); b_f2 = Buf("f2")
                b_acc = Buf("accB"); b_sg = Buf("sg"); b_prod = Buf("prod")
                b_yd = Buf("yd")
                b_out = [Buf(f"outb{c}") for c in range(NCHB)]
                NTE = CAP // 128

                HB = CAP // 2

                def xngf(par, cin):
                    off = (par * NCC + cin) * HB
                    return xngf_sb[:, off:off + HB]

                for er in range(16):
                    for hb in range(2):
                        par = (er * 2 + hb) % 2
                        def gload(e, S, er=er, hb=hb, par=par):
                            return [e.dma_start(
                                out=xngf(par, cin),
                                in_=xng_d[er * CAP + hb * HB:er * CAP + (hb + 1) * HB,
                                          cin * 128:(cin + 1) * 128],
                                transpose=True) for cin in range(NCC)]
                        group(QS, [b_route], [b_xngf[par]], gload, 16 * NCC)
                        ntk = (HB + 127) // 128
                        for tk in range(ntk):
                            w = min(128, HB - tk * 128)
                            bky = (tk % 2)
                            def moe_mm(e, S, er=er, par=par, tk=tk, bky=bky, w=w):
                                for half in range(2):
                                    for cin in range(NCC):
                                        inst = e.matmul(
                                            pb[bky * 2 + half][0:w, 0:512],
                                            xngf(par, cin)[:, tk * 128:tk * 128 + w],
                                            wc(er % 8, cin)[:, half * 512:(half + 1) * 512],
                                            start=(cin == 0), stop=(cin == NCC - 1))
                                return inst
                            group(PE, [b_xngf[par], b_wc], [BK[bky * 2], BK[bky * 2 + 1]],
                                  moe_mm, 1)
                            def y_ev(e, S, tk=tk, bky=bky, w=w):
                                return [e.copy(out=y_sb[0:w, bky * C + half * 512:
                                                        bky * C + half * 512 + 512],
                                               in_=pb[bky * 2 + half][0:w, 0:512])
                                        for half in range(2)]
                            group(ACT, [BK[bky * 2], BK[bky * 2 + 1]],
                                  [b_y[bky], BK[bky * 2], BK[bky * 2 + 1]], y_ev, 2)
                            def y_st(e, S, er=er, hb=hb, tk=tk, bky=bky, w=w):
                                r0 = er * CAP + hb * HB + tk * 128
                                return e.dma_start(out=yext_d[r0:r0 + w, :],
                                                   in_=y_sb[0:w, bky * C:(bky + 1) * C])
                            group(QS, [b_y[bky]], [b_yd], y_st, 16)

                def ldB(c):
                    def ld(e, S, c=c):
                        L = [e.dma_start(out=xbt(c, tt),
                                         in_=x_d[c * TCB + tt * 128:c * TCB + (tt + 1) * 128, :])
                             for tt in range(NTB)]
                        L += [e.dma_start(
                            out=sbt(c, tt),
                            in_=s_d[c * TCB + tt * 128:c * TCB + (tt + 1) * 128, :])
                            for tt in range(NTB)]
                        return L
                    group(QS, [b_sdd[c * TCB // TCA]], [b_xb[c % 2], b_sb[c % 2]], ld,
                          16 * 2 * NTB)

                def xbt(c, tt):
                    return xb_sb[:, ((c % 2) * NTB + tt) * C:((c % 2) * NTB + tt + 1) * C]
                def sbt(c, tt):
                    return sb_sb[:, ((c % 2) * NTB + tt) * C:((c % 2) * NTB + tt + 1) * C]

                ldB(0)
                for c in range(NCHB):
                    if c + 1 < NCHB:
                        ldB(c + 1)
                    for tt in range(NTB):
                        gt = c * NTB + tt
                        def gf(e, S, gt=gt):
                            return [
                                e.indirect_dma_start(
                                    out=f1_sb[:], out_offset=None, in_=yext_d,
                                    in_offset=IndirectOffsetOnAxis(
                                        idx_i[:, gt:gt + 1], 0)),
                                e.indirect_dma_start(
                                    out=f2_sb[:], out_offset=None, in_=yext_d,
                                    in_offset=IndirectOffsetOnAxis(
                                        idx_i[:, NTT + gt:NTT + gt + 1], 0))]
                        group(QG, [b_yd, b_idx], [b_f1, b_f2], gf, 32)
                        group(DVE, [b_f1, b_f2], [b_acc],
                              lambda e, S: e.tensor_tensor(acc_sb[:], f1_sb[:],
                                                           f2_sb[:], ALU.add), 1)
                        group(ACT, [b_acc], [b_sg],
                              lambda e, S: e.activation(out=sg_sb[:], in_=acc_sb[:],
                                                        func=AF.Sigmoid), 1)
                        group(DVE, [b_sg, b_acc], [b_prod],
                              lambda e, S: e.tensor_tensor(prod_sb[:], acc_sb[:],
                                                           sg_sb[:], ALU.mult), 1)
                        group(DVE, [b_prod, b_sb[c % 2]], [b_prod],
                              lambda e, S, c=c, tt=tt: e.tensor_tensor(
                                  prod_sb[:], prod_sb[:], sbt(c, tt), ALU.mult), 1)
                        group(DVE, [b_prod, b_xb[c % 2]], [b_xb[c % 2]],
                              lambda e, S, c=c, tt=tt: e.tensor_tensor(
                                  xbt(c, tt), xbt(c, tt), prod_sb[:], ALU.add), 1)
                    def out_st(e, S, c=c):
                        return [e.dma_start(
                            out=out_d[c * TCB + tt * 128:c * TCB + (tt + 1) * 128, :],
                            in_=xbt(c, tt)) for tt in range(NTB)]
                    group(QG, [b_xb[c % 2]], [b_out[c]], out_st, 16 * NTB)

                if "xn" in taps:
                    group(QG, [b_xnd[NCHA - 1]], [],
                          lambda e, S: e.dma_start(out=tap_d["tap_xn"], in_=xn_d), 16)
                if "s" in taps:
                    group(QG, [b_sdd[NCHA - 1]], [],
                          lambda e, S: e.dma_start(out=tap_d["tap_s"], in_=s_d), 16)

                QG.emit(lambda e, S: e.wait_ge(S["QG"], QG.n))

                block = es.enter_context(nc.Block())
                S = sems
                block.sync(lambda e: [op(e, S) for op in QS.ops])
                block.tensor(lambda e: [op(e, S) for op in PE.ops])
                block.scalar(lambda e: [op(e, S) for op in ACT.ops])
                block.vector(lambda e: [op(e, S) for op in DVE.ops])
                block.gpsimd(lambda e: [op(e, S) for op in QG.ops])

    return nc


def kernel(**inputs):
    in_maps = [prep_core_inputs(b, inputs) for b in range(8)]
    nc = build(ncores=8)
    res = run_bass_kernel_spmd(nc, in_maps, core_ids=list(range(8)))
    return np.stack([res.results[b]["out"] for b in range(8)], axis=0)




# version 17
# speedup vs baseline: 1.0507x; 1.0507x over previous
import numpy as np
import ml_dtypes
import concourse.bass as bass
from concourse.bass import IndirectOffsetOnAxis
import concourse.mybir as mybir
from concourse.bass_utils import run_bass_kernel_spmd
from contextlib import ExitStack

F32 = mybir.dt.float32
BF16 = mybir.dt.bfloat16
AF = mybir.ActivationFunctionType
ALU = mybir.AluOpType
AX = mybir.AxisListType.X

T, C, H, D = 4096, 1024, 16, 64
NCC = 8
TCA = 256
NCHA = T // TCA
NTA = TCA // 128
TCB = 256
NCHB = T // TCB
NTB = TCB // 128
E = 8
J = 4096
NJG = J // 512
EXM05 = float(np.exp(np.float32(-0.5)))


def _cap(t):
    return max(128, -(-3 * t // 16 // 128) * 128)


CAP = _cap(T)


def set_dims(t=4096, j=4096):
    global T, NCHA, NCHB, J, NJG, CAP
    T, J = t, j
    NCHA = T // TCA
    NCHB = T // TCB
    NJG = J // 512
    CAP = _cap(T)


class Eng:

    def __init__(self, name, queue=False, ops=None):
        self.name = name
        self.queue = queue
        self.n = 0
        self.ops = [] if ops is None else ops
        self.seen = {}

    def self_wait(self):
        if self.n > 0:
            nm, vv = self.name, self.n
            self.ops.append(lambda e, S: e.wait_ge(S[nm], vv))

    def wait(self, other, val):
        if val <= 0:
            return
        if self.seen.get(other.name, 0) >= val:
            return
        self.seen[other.name] = val
        on, vv = other.name, int(val)
        self.ops.append(lambda e, S: e.wait_ge(S[on], vv))

    def emit(self, fn, tag=0):
        if tag:
            self.n += tag
            sname, step = self.name, tag

            def run(e, S, fn=fn, sname=sname, step=step):
                r = fn(e, S)
                if isinstance(r, (list, tuple)):
                    u = step // len(r)
                    assert u * len(r) == step, (step, len(r))
                    for inst in r:
                        inst.then_inc(S[sname], u)
                else:
                    r.then_inc(S[sname], step)

            self.ops.append(run)
        else:
            self.ops.append(lambda e, S, fn=fn: fn(e, S))
        return self.n


class Buf:

    def __init__(self, name=""):
        self.name = name
        self.w = (None, 0)
        self.r = []


def group(eng, reads, writes, fn, tag):
    if eng.queue:
        eng.self_wait()
    for b in reads:
        if b.w[0] is not None:
            eng.wait(b.w[0], b.w[1])
    for b in writes:
        for (oe, ov) in b.r:
            eng.wait(oe, ov)
        if b.w[0] is not None:
            eng.wait(b.w[0], b.w[1])
    eng.emit(fn, tag=tag)
    for b in reads:
        b.r.append((eng, eng.n))
    for b in writes:
        b.w = (eng, eng.n)
        b.r = []


def colmaj(v):
    return np.ascontiguousarray(np.asarray(v, np.float32).reshape(NCC, 128).T)


def prep_core_inputs(b, I):
    _ = (I["L0_attn_v0"], I["L0_attn_v1"], I["L0_attn_v2"])
    g = lambda k: np.asarray(I[k], np.float32)
    w0 = g("L0_attn_w0").reshape(H)
    a0 = g("L0_attn_a0").reshape(H)
    w2 = g("L0_attn_w2")
    a2 = g("L0_attn_a2")
    selx = np.kron(np.eye(2, dtype=np.float32), np.ones((64, 64), np.float32))
    m = {
        "x": np.ascontiguousarray(g("L0_attn_x")[b]),
        "wr": g("L0_attn_Wr"), "wk": g("L0_attn_Wk"),
        "wv": g("L0_attn_Wv"), "wo": g("L0_attn_Wo"),
        "w1": g("L0_attn_w1"), "a1": g("L0_attn_a1"), "g1": g("L0_attn_g1"),
        "g2": g("L0_attn_g2"),
        "w2x": np.ascontiguousarray(np.repeat(w2, D, axis=1)),
        "a2x": np.ascontiguousarray(np.repeat(a2, D, axis=1)),
        "gwt": np.ascontiguousarray(g("L0_ffn_gate_w").T),
        "wupt": np.ascontiguousarray(g("L0_ffn_w_up")[b % 8].T),
        "wdn": np.ascontiguousarray(g("L0_ffn_w_down")[b % 8]),
        "mixr": colmaj(g("L0_attn_x_r")), "mixw": colmaj(g("L0_attn_x_w")),
        "mixk": colmaj(g("L0_attn_x_k")), "mixv": colmaj(g("L0_attn_x_v")),
        "mixa": colmaj(g("L0_attn_x_a")), "mixg": colmaj(g("L0_attn_x_g")),
        "kkc": colmaj(g("L0_attn_k_k")),
        "lnw": colmaj(g("L0_attn_ln_w")), "lnb": colmaj(g("L0_attn_ln_b")),
        "rkc": colmaj(g("L0_attn_r_k").reshape(C)),
        "w0x": colmaj(np.repeat(w0, D)), "a0x": colmaj(np.repeat(a0, D)),
        "selx": selx.astype(ml_dtypes.bfloat16),
        "selx64": (selx / 64.0).astype(ml_dtypes.bfloat16),
        "w0row": np.ascontiguousarray(
            colmaj(np.repeat(w0, D)).T.reshape(1, C)).astype(ml_dtypes.bfloat16),
        "a0row": np.ascontiguousarray(
            colmaj(np.repeat(a0, D)).T.reshape(1, C)).astype(ml_dtypes.bfloat16),
        "kkw": np.ascontiguousarray(
            np.tile(g("L0_attn_k_k").reshape(1, C), (128, 1))).astype(ml_dtypes.bfloat16),
        "rkexp": np.ascontiguousarray(
            np.repeat(colmaj(g("L0_attn_r_k").reshape(C)), TCA, axis=1)).astype(
                ml_dtypes.bfloat16),
        "idnf": np.eye(128, dtype=np.float32),
        "basec": np.ascontiguousarray(
            np.tile((np.arange(16, dtype=np.float32) * CAP - 1.0)[None, :],
                    (128, 1))),
    }
    return m


def build(taps=(), ncores=8):
    nc = bass.Bass("TRN2", target_bir_lowering=False, debug=False)
    dt_in = {}
    def din(name, shape, dt=F32):
        dt_in[name] = nc.dram_tensor(name, list(shape), dt, kind="ExternalInput").ap()
        return dt_in[name]

    x_d = din("x", (T, C))
    wr_d = din("wr", (C, C)); wk_d = din("wk", (C, C))
    wv_d = din("wv", (C, C)); wo_d = din("wo", (C, C))
    w1_d = din("w1", (C, 64)); a1_d = din("a1", (C, 64)); g1_d = din("g1", (C, 64))
    g2_d = din("g2", (64, C)); w2x_d = din("w2x", (64, C)); a2x_d = din("a2x", (64, C))
    gwt_d = din("gwt", (C, E))
    wupt_d = din("wupt", (J, C)); wdn_d = din("wdn", (J, C))
    cols_d = {k: din(k, (128, NCC)) for k in
              ["mixr", "mixw", "mixk", "mixv", "mixa", "mixg",
               "kkc", "lnw", "lnb", "rkc", "w0x", "a0x"]}
    selx_d = din("selx", (128, 128), BF16)
    selx64_d = din("selx64", (128, 128), BF16)
    idnf_d = din("idnf", (128, 128))
    basec_d = din("basec", (128, 16))
    w0row_d = din("w0row", (1, C), BF16)
    a0row_d = din("a0row", (1, C), BF16)
    kkw_d = din("kkw", (128, C), BF16)
    rkexp_d = din("rkexp", (128, NCC * TCA), BF16)

    xn_d = nc.dram_tensor("xn_d", [T, C], BF16, kind="Internal").ap()
    wcown_d = nc.dram_tensor("wcown_d", [128, NCC * 1024], BF16, kind="Internal").ap()
    xng_d = nc.dram_tensor("xng_d", [16 * CAP, C], BF16, kind="Internal").ap()
    yext_d = nc.dram_tensor("yext_d", [16 * CAP, C], BF16, kind="Internal").ap()
    wcall_d = nc.dram_tensor("wcall_d", [128 * E, NCC * 1024], BF16, kind="Internal").ap()
    s_d = nc.dram_tensor("s_d", [T, C], BF16, kind="Internal").ap()
    out_d = nc.dram_tensor("out", [T, C], F32, kind="ExternalOutput").ap()
    tap_d = {}
    if "xn" in taps:
        tap_d["tap_xn"] = nc.dram_tensor("tap_xn", [T, C], BF16, kind="ExternalOutput").ap()
    if "s" in taps:
        tap_d["tap_s"] = nc.dram_tensor("tap_s", [T, C], BF16, kind="ExternalOutput").ap()
    if "wc" in taps:
        tap_d["tap_wc"] = nc.dram_tensor("tap_wc", [128, E * NCC * 1024], BF16,
                                         kind="ExternalOutput").ap()
    if "ffn" in taps:
        tap_d["tap_ffn"] = nc.dram_tensor("tap_ffn", [T, C], BF16,
                                          kind="ExternalOutput").ap()
    if "mask" in taps:
        tap_d["tap_mask"] = nc.dram_tensor("tap_mask", [T, E], F32,
                                           kind="ExternalOutput").ap()
    if "lg" in taps:
        tap_d["tap_lg"] = nc.dram_tensor("tap_lg", [T, E], F32,
                                         kind="ExternalOutput").ap()

    QS, QG, PE, ACT, DVE = (Eng("QS", queue=True), Eng("QG", queue=True), Eng("PE"), Eng("ACT"), Eng("DVE"))
    PL = Eng("PL", ops=QG.ops)
    CC = Eng("CC", ops=QG.ops)
    ENGS = [QS, QG, PL, CC, PE, ACT, DVE]

    def barrier(exclude=()):
        snap = [(e, e.n) for e in ENGS if e.name not in exclude]
        for e in ENGS:
            if e.name in exclude:
                continue
            if e.queue:
                e.self_wait()
            for (o, v) in snap:
                e.wait(o, v)

    es = ExitStack()
    sbuf = lambda name, shape, dt: es.enter_context(nc.sbuf_tensor(name, shape, dt))
    with es:
        sems = {}
        for e in ENGS:
            sems[e.name] = es.enter_context(nc.semaphore(e.name))

        cols_sb = {k: sbuf(k + "_sb", [128, NCC], F32) for k in cols_d}
        selx_sb = sbuf("selx_sb", [128, 128], BF16)
        selx64_sb = sbuf("selx64_sb", [128, 128], BF16)
        gwt_sb = sbuf("gwt_sb", [128, NCC * E], BF16)
        carry = sbuf("carry", [128, NCC], F32)
        prevc = sbuf("prevc", [128, NCC], F32)
        sumc = sbuf("sumc", [128, NTA], F32)
        sqc = sbuf("sqc", [128, NTA], F32)
        meanc = sbuf("meanc", [128, NTA], F32)
        varc = sbuf("varc", [128, NTA], F32)
        tmpc = sbuf("tmpc", [128, NTA], F32)
        rstdc = sbuf("rstdc", [128, NTA], F32)
        nmrc = sbuf("nmrc", [128, NTA], F32)
        eps5 = sbuf("eps5", [128, 1], F32)
        eps24 = sbuf("eps24", [128, 1], F32)
        mask12_sb = sbuf("mask12_sb", [128, (T // 128) * 16], BF16)
        lgs = sbuf("lgs", [128, 8], F32)
        m1s = sbuf("m1s", [128, 1], F32)
        eqs = sbuf("eqs", [128, 8], F32)
        lg2s = sbuf("lg2s", [128, 8], F32)
        m2s = sbuf("m2s", [128, 1], F32)

        pb = [es.enter_context(nc.psum_tensor(f"pb{i}", [128, 512], F32)) for i in range(8)]
        BK = [Buf(f"bank{i}") for i in range(8)]

        b_cols = Buf("cols"); b_selx = Buf("selx"); b_gwt = Buf("gwt")
        b_carry = Buf("carry"); b_prevc = Buf("prevc")
        b_stat = Buf("stat")
        b_xnd = [Buf(f"xnd{c}") for c in range(NCHA)]
        b_sdd = [Buf(f"sdd{c}") for c in range(NCHA)]

        def ld_smalls(e, S):
            L = [e.dma_start(out=cols_sb[k][:], in_=cols_d[k]) for k in cols_d]
            L.append(e.dma_start(out=selx_sb[:], in_=selx_d))
            L.append(e.dma_start(out=selx64_sb[:], in_=selx64_d))
            return L
        group(QS, [], [b_cols, b_selx], ld_smalls, tag=16 * (len(cols_d) + 2))
        group(QG, [], [b_gwt],
              lambda e, S: e.dma_start(
                  out=gwt_sb[:].rearrange("p (cc e) -> p cc e", e=E),
                  in_=gwt_d.rearrange("(cc p) e -> p cc e", p=128)), tag=16)
        b_eps = Buf("eps")
        b_m12 = Buf("m12"); b_lgx = Buf("lgx")
        group(DVE, [], [b_carry, b_prevc, b_eps],
              lambda e, S: [e.memset(carry[:], 0.0), e.memset(prevc[:], 0.0),
                            e.memset(eps5[:], 1e-5), e.memset(eps24[:], 1e-24)], tag=4)

        esW = ExitStack()
        sbw = lambda name, shape, dt: esW.enter_context(nc.sbuf_tensor(name, shape, dt))
        wr_sb = sbw("wr_sb", [128, NCC * C], BF16)
        wk_sb = sbw("wk_sb", [128, NCC * C], BF16)
        wv_sb = sbw("wv_sb", [128, NCC * C], BF16)
        wo_sb = sbw("wo_sb", [128, NCC * C], BF16)
        w1_sb = sbw("w1_sb", [128, NCC * 64], BF16)
        a1_sb = sbw("a1_sb", [128, NCC * 64], BF16)
        g1_sb = sbw("g1_sb", [128, NCC * 64], BF16)
        g2_sb = sbw("g2_sb", [64, C], BF16)
        w2x_sb = sbw("w2x_sb", [64, C], BF16)
        a2x_sb = sbw("a2x_sb", [64, C], BF16)
        x_sb = sbw("x_sb", [128, 2 * NTA * C], BF16)
        b_w = {k: Buf(k) for k in ["wr", "wk", "wv", "wo", "w1", "a1", "g1",
                                   "g2", "w2x", "a2x", "w1v", "a1v", "g1v",
                                   "rows", "kkw", "rkexp", "mxc"]}
        b_x = [Buf(f"x{p}") for p in range(2)]

        def wcast(dst, src_d, n):
            def fn(e, S, d=dst, s=src_d, n=n):
                return e.dma_start(
                    out=d[:].rearrange("p (cc n) -> p cc n", n=n),
                    in_=s.rearrange("(cc p) n -> p cc n", p=128))
            return fn
        group(QG, [], [b_w["w1"]], wcast(w1_sb, w1_d, 64), 16)
        group(QG, [], [b_w["a1"]], wcast(a1_sb, a1_d, 64), 16)
        group(QG, [], [b_w["g1"]], wcast(g1_sb, g1_d, 64), 16)
        group(QG, [], [b_w["g2"]], lambda e, S: e.dma_start(out=g2_sb[:], in_=g2_d), 16)
        group(QG, [], [b_w["w2x"]], lambda e, S: e.dma_start(out=w2x_sb[:], in_=w2x_d), 16)
        group(QG, [], [b_w["a2x"]], lambda e, S: e.dma_start(out=a2x_sb[:], in_=a2x_d), 16)

        def load_x(c):
            def ld(e, S, c=c):
                return [e.dma_start(
                    out=x_sb[:, ((c % 2) * NTA + tt) * C:((c % 2) * NTA + tt + 1) * C],
                    in_=x_d[c * TCA + tt * 128:c * TCA + (tt + 1) * 128, :])
                    for tt in range(NTA)]
            group(QG, [], [b_x[c % 2]], ld, 16 * NTA)
        for cc_ in range(min(2, NCHA)):
            load_x(cc_)
        group(QG, [], [b_w["wr"]], wcast(wr_sb, wr_d, C), 16)
        group(QG, [], [b_w["wk"]], wcast(wk_sb, wk_d, C), 16)
        group(QG, [], [b_w["wv"]], wcast(wv_sb, wv_d, C), 16)
        group(QG, [], [b_w["wo"]], wcast(wo_sb, wo_d, C), 16)

        b_wcall = Buf("wcall")
        esC = ExitStack()
        with esC:
            wcst = esC.enter_context(nc.sbuf_tensor("wcst", [128, NCC * 1024], BF16))
            wdn_bf = esC.enter_context(nc.sbuf_tensor("wdn_bf", [128, 32 * 1024], BF16))
            wup_q = esC.enter_context(nc.sbuf_tensor("wup_q", [128, 2 * 8 * 512], BF16))
            b_wupq = [Buf("wupq0"), Buf("wupq1")]
            b_wdnq = [Buf(f"wdnq{q}") for q in range(4)]
            b_wcown = Buf("wcown"); b_wcst = Buf("wcst")
            NJC = J // 128

            def c0_lddn(q):
                def fn(e, S, q=q):
                    jgs = NJC // 4
                    src_d2 = wdn_d.rearrange("(jg p) n -> jg p n", p=128)[
                        q * jgs:(q + 1) * jgs].rearrange("jg p n -> p jg n")
                    return e.dma_start(
                        out=wdn_bf[:, q * jgs * 1024:(q + 1) * jgs * 1024].rearrange(
                            "p (jg n) -> p jg n", n=1024),
                        in_=src_d2)
                group(QG, [], [b_wdnq[q]], fn, 16)

            for cinh in range(2):
                for q in range(4):
                    jgs = NJC // 4
                    p = q % 2
                    if cinh == 0:
                        c0_lddn(q)
                    def c0_ldup(e, S, cinh=cinh, q=q, p=p, jgs=jgs):
                        src_u = wupt_d.rearrange("(jg p) n -> jg p n", p=128)[
                            q * jgs:(q + 1) * jgs, :,
                            cinh * 512:(cinh + 1) * 512].rearrange("jg p n -> p jg n")
                        return e.dma_start(
                            out=wup_q[:, p * jgs * 512:(p + 1) * jgs * 512].rearrange(
                                "p (jg n) -> p jg n", n=512),
                            in_=src_u)
                    group(QG, [], [b_wupq[p]], c0_ldup, 16)
                    def c0_mm(e, S, cinh=cinh, q=q, p=p, jgs=jgs):
                        for jq in range(jgs):
                            jg = q * jgs + jq
                            for cc in range(4):
                                for half in range(2):
                                    inst = e.matmul(
                                        pb[cc * 2 + half][:, 0:512],
                                        wup_q[:, (p * jgs + jq) * 512 + cc * 128:
                                              (p * jgs + jq) * 512 + cc * 128 + 128],
                                        wdn_bf[:, jg * 1024 + half * 512:
                                               jg * 1024 + half * 512 + 512],
                                        start=(jg == 0), stop=(jg == NJC - 1))
                        return inst
                    group(PE, [b_wupq[p], b_wdnq[q]],
                          [BK[i] for i in range(8)] if q == 0 else [], c0_mm, 1)
                    if q != 0:
                        for i in range(8):
                            BK[i].w = (PE, PE.n)
                def c0_ev(e, S, cinh=cinh):
                    return [e.copy(
                        out=wcst[:, (cinh * 4 + cc) * 1024 + half * 512:
                                 (cinh * 4 + cc) * 1024 + half * 512 + 512],
                        in_=pb[cc * 2 + half][:, 0:512])
                        for cc in range(4) for half in range(2)]
                group(ACT, [BK[i] for i in range(8)],
                      [b_wcst] + [BK[i] for i in range(8)], c0_ev, 8)

            group(QG, [b_wcst], [b_wcown],
                  lambda e, S: e.dma_start(out=wcown_d, in_=wcst[:]), 16)
            group(CC, [b_wcown], [b_wcall],
                  lambda e, S: e.collective_compute(
                      "AllGather", ALU.bypass,
                      [list(range(ncores))], [wcown_d], [wcall_d]), 1)
        barrier(exclude=("CC",))

        TW = TCA + 16
        PW = 2 * TCA
        esA = ExitStack()
        sbufA = lambda name, shape, dt: esA.enter_context(nc.sbuf_tensor(name, shape, dt))
        with esA:
            w1v_sb = sbufA("w1v_sb", [128, 2 * NCC * 64], BF16)
            a1v_sb = sbufA("a1v_sb", [128, 2 * NCC * 64], BF16)
            g1v_sb = sbufA("g1v_sb", [128, 2 * NCC * 64], BF16)
            w0row_sb = sbufA("w0row_sb", [1, C], BF16)
            a0row_sb = sbufA("a0row_sb", [1, C], BF16)
            ones_sb = sbufA("ones_sb", [1, TCA], BF16)
            kkw_sb = sbufA("kkw_sb", [128, C], BF16)
            rkexp_sb = sbufA("rkexp_sb", [128, NCC * TCA], BF16)
            mxc_sb = sbufA("mxc_sb", [128, 6 * NCC], F32)
            xntm_sb = sbufA("xntm_sb", [128, 2 * NTA * C], BF16)
            xnfm_sb = sbufA("xnfm_sb", [128, 3 * NCC * TW], BF16)
            sx_sb = sbufA("sx_sb", [128, NCC * TCA], BF16)
            mix_sb = sbufA("mix_sb", [128, 2 * NCC * TCA], BF16)
            kscl_sb = sbufA("kscl_sb", [128, NCC * TCA], BF16)
            vh_sb = sbufA("vh_sb", [128, NCC * TCA], BF16)
            r_sb = sbufA("r_sb", [128, NCC * TCA], BF16)
            g_sb = sbufA("g_sb", [128, NCC * TCA], BF16)
            ab_sb = sbufA("ab_sb", [128, NCC * TCA], BF16)
            og_sb = sbufA("og_sb", [128, NCC * TCA], BF16)
            wb_sb = sbufA("wb_sb", [128, NCC * TCA], F32)
            tw_sb = sbufA("tw_sb", [64, TCA], BF16)
            ta_sb = sbufA("ta_sb", [64, TCA], BF16)
            tg_sb = sbufA("tg_sb", [64, TCA], BF16)
            lnscr = sbufA("lnscr", [128, C], F32)
            stm_sb = sbufA("stm_sb", [128, NTA * C], BF16)
            scr = {}
            for nm in ["rsn", "kk", "kv", "st", "t1", "xw"]:
                scr[nm] = sbufA("scr_" + nm, [128, 2 * PW], F32)
            scrb = {}
            for nm in ["z", "xwb", "xw2"]:
                scrb[nm] = sbufA("scrb_" + nm, [128, 2 * PW], BF16)

            b_xntm = [Buf(f"xntm{p}") for p in range(2)]
            b_xnfm = [Buf(f"xnfm{p}") for p in range(3)]
            b_sx = Buf("sx")
            b_mix = [Buf("mix0"), Buf("mix1")]
            b_kscl = Buf("kscl"); b_vh = Buf("vh"); b_r = Buf("r")
            b_g = Buf("g"); b_ab = Buf("ab"); b_og = Buf("og"); b_wb = Buf("wb")
            b_tw = Buf("t1lr"); b_ta = Buf("t2lr"); b_tg = Buf("t3lr")
            b_lnscr = Buf("lnscr")
            b_stm = [Buf("stm"), Buf("stm")]
            b_stm[1] = b_stm[0]
            b_scr = {nm: [Buf(nm + "0"), Buf(nm + "1")] for nm in
                     list(scr) + list(scrb)}

            group(QS, [], [b_w["rows"]], lambda e, S: [
                e.dma_start(out=w0row_sb[:], in_=w0row_d),
                e.dma_start(out=a0row_sb[:], in_=a0row_d),
                e.dma_start(out=kkw_sb[:], in_=kkw_d),
                e.dma_start(out=rkexp_sb[:], in_=rkexp_d)], 64)
            group(DVE, [], [b_w["rows"]],
                  lambda e, S: e.memset(ones_sb[:], 1.0), 1)

            def kfold(e, S):
                return [e.tensor_tensor(wk_sb[:, cc * C:(cc + 1) * C],
                                        wk_sb[:, cc * C:(cc + 1) * C], kkw_sb[:],
                                        ALU.mult) for cc in range(NCC)]
            group(DVE, [b_w["wk"], b_w["rows"]], [b_w["wk"]], kfold, NCC)
            def mxc(e, S):
                L = []
                for i, mk in enumerate(["mixw", "mixa", "mixg"]):
                    L.append(e.tensor_scalar(mxc_sb[:, (2 * i) * NCC:(2 * i + 1) * NCC],
                                             cols_sb[mk][:], 1.0, None, ALU.add))
                    L.append(e.tensor_scalar(mxc_sb[:, (2 * i + 1) * NCC:(2 * i + 2) * NCC],
                                             cols_sb[mk][:], -1.0, None, ALU.mult))
                return L
            group(DVE, [b_cols], [b_w["mxc"]], mxc, 6)
            def lrv(e, S):
                L = []
                for i, (wsb, vsb) in enumerate([(w1_sb, w1v_sb), (a1_sb, a1v_sb),
                                                (g1_sb, g1v_sb)]):
                    for v in range(2):
                        for cc in range(NCC):
                            L.append(e.tensor_scalar(
                                vsb[:, (v * NCC + cc) * 64:(v * NCC + cc + 1) * 64],
                                wsb[:, cc * 64:(cc + 1) * 64],
                                mxc_sb[:, (2 * i + v) * NCC + cc:
                                       (2 * i + v) * NCC + cc + 1],
                                None, ALU.mult))
                return L
            group(DVE, [b_w["w1"], b_w["a1"], b_w["g1"], b_w["mxc"]],
                  [b_w["w1v"], b_w["a1v"], b_w["g1v"]], lrv, 6 * NCC)

            def xt(c, tt):
                return x_sb[:, ((c % 2) * NTA + tt) * C:((c % 2) * NTA + tt + 1) * C]
            def xntm(c, tt):
                return xntm_sb[:, ((c % 2) * NTA + tt) * C:((c % 2) * NTA + tt + 1) * C]
            def fmtile(c, ci):
                off = ((c % 3) * NCC + ci) * TW
                return xnfm_sb[:, off:off + TW]
            def xnv(c, ci):
                return fmtile(c, ci)[:, 16:TW]
            def xnsh(c, ci):
                return fmtile(c, ci)[:, 15:TW - 1]
            def sxv(ci):
                return sx_sb[:, ci * TCA:(ci + 1) * TCA]
            def mixv(sl, ci):
                return mix_sb[:, (sl * NCC + ci) * TCA:(sl * NCC + ci + 1) * TCA]
            def bchunk(buf, ci, w=TCA):
                return buf[:, ci * w:(ci + 1) * w]
            def bpair(buf, pp):
                return buf[:, pp * PW:(pp + 1) * PW]
            def scrv(nm, pp):
                t = scr[nm] if nm in scr else scrb[nm]
                return t[:, (pp % 2) * PW:((pp % 2) + 1) * PW]
            def scrh(nm, pp, h):
                return scrv(nm, pp)[:, h * TCA:(h + 1) * TCA]
            def col(k, ci):
                return cols_sb[k][:, ci:ci + 1]
            def wslice(wsb, cin, j0, jw):
                return wsb[:, cin * C + j0:cin * C + j0 + jw]

            def ln0_tail(c):
                bx, bxn, bfm = b_x[c % 2], b_xntm[c % 2], b_xnfm[c % 3]
                bst = b_stat
                for tt in range(NTA):
                    group(ACT, [bx], [b_lnscr, bst],
                          lambda e, S, tt=tt: e.activation(
                              out=lnscr[:], in_=xt(c, tt), func=AF.Square,
                              accum_out=sqc[:, tt:tt + 1]), 1)
                    group(ACT, [bx], [b_lnscr, bst],
                          lambda e, S, tt=tt: e.activation(
                              out=lnscr[:], in_=xt(c, tt), func=AF.Identity,
                              accum_out=sumc[:, tt:tt + 1]), 1)
                group(DVE, [bst], [bst], lambda e, S: [
                    e.tensor_scalar(meanc[:], sumc[:], 1.0 / C, None, ALU.mult),
                    e.tensor_scalar(varc[:], sqc[:], 1.0 / C, None, ALU.mult)], 2)
                group(DVE, [bst], [bst],
                      lambda e, S: e.tensor_tensor(tmpc[:], meanc[:], meanc[:], ALU.mult), 1)
                group(DVE, [bst], [bst],
                      lambda e, S: e.tensor_tensor(varc[:], varc[:], tmpc[:], ALU.subtract), 1)
                group(ACT, [bst, b_eps], [bst],
                      lambda e, S: e.activation(out=tmpc[:], in_=varc[:], func=AF.Ln,
                                                bias=eps5[:, 0:1]), 1)
                group(ACT, [bst], [bst],
                      lambda e, S: e.activation(out=rstdc[:], in_=tmpc[:], func=AF.Exp,
                                                scale=-0.5), 1)
                group(DVE, [bst], [bst],
                      lambda e, S: e.scalar_tensor_tensor(
                          nmrc[:], meanc[:], -1.0, rstdc[:], ALU.mult, ALU.mult), 1)
                def apply(e, S):
                    return [e.activation(out=xntm(c, tt), in_=xt(c, tt), func=AF.Identity,
                                         scale=rstdc[:, tt:tt + 1], bias=nmrc[:, tt:tt + 1])
                            for tt in range(NTA)]
                group(ACT, [bx, bst], [bxn], apply, NTA)
                def store(e, S):
                    return [e.dma_start(out=xn_d[c * TCA + tt * 128:c * TCA + (tt + 1) * 128, :],
                                        in_=xntm(c, tt)) for tt in range(NTA)]
                group(QS, [bxn], [b_xnd[c]], store, 16 * NTA)
                def tpr(e, S):
                    return [e.dma_start(out=xnv(c, ci),
                                        in_=xn_d[c * TCA:(c + 1) * TCA, ci * 128:(ci + 1) * 128],
                                        transpose=True) for ci in range(NCC)]
                group(QS, [b_xnd[c]], [bfm], tpr, 16 * NCC)

            for cc_ in range(min(2, NCHA)):
                ln0_tail(cc_)

            for c in range(NCHA):
                if c + 2 < NCHA:
                    load_x(c + 2)
                bfm = b_xnfm[c % 3]
                if c == 0:
                    group(DVE, [], [bfm], lambda e, S: [
                        e.memset(fmtile(0, ci)[:, 0:16], 0.0) for ci in range(NCC)], NCC)
                else:
                    bprev = b_xnfm[(c - 1) % 3]
                    group(DVE, [bprev], [bfm], lambda e, S, c=c: [
                        e.tensor_copy(fmtile(c, ci)[:, 0:16],
                                      fmtile(c - 1, ci)[:, TW - 16:TW])
                        for ci in range(NCC)], NCC)
                group(DVE, [bfm], [b_sx], lambda e, S, c=c: [
                    e.tensor_tensor(sxv(ci), xnv(c, ci), xnsh(c, ci), ALU.subtract)
                    for ci in range(NCC)], NCC)

                for pi, (vsb, vbuf, tdst, tb) in enumerate(
                        [(w1v_sb, b_w["w1v"], tw_sb, b_tw),
                         (a1v_sb, b_w["a1v"], ta_sb, b_ta),
                         (g1v_sb, b_w["g1v"], tg_sb, b_tg)]):
                    def lr_mm(e, S, c=c, vsb=vsb):
                        L = []
                        for cin in range(NCC):
                            L.append(e.matmul(
                                pb[0][0:64, 0:TCA],
                                vsb[:, cin * 64:(cin + 1) * 64],
                                xnv(c, cin), start=(cin == 0), stop=False))
                            L.append(e.matmul(
                                pb[0][0:64, 0:TCA],
                                vsb[:, (NCC + cin) * 64:(NCC + cin + 1) * 64],
                                xnsh(c, cin), start=False,
                                stop=(cin == NCC - 1)))
                        return L[-1]
                    group(PE, [bfm, vbuf], [BK[0]], lr_mm, 1)
                    group(ACT, [BK[0]], [tb],
                          lambda e, S, tdst=tdst: e.activation(
                              out=tdst[:], in_=pb[0][0:64, 0:TCA], func=AF.Tanh), 1)
                for pi, (xsb, xb, tsrc, tsb, rowsb, dst, db) in enumerate(
                        [(w2x_sb, b_w["w2x"], tw_sb, b_tw, w0row_sb, wb_sb, b_wb),
                         (a2x_sb, b_w["a2x"], ta_sb, b_ta, a0row_sb, ab_sb, b_ab),
                         (g2_sb, b_w["g2"], tg_sb, b_tg, None, g_sb, b_g)]):
                    for pp in range(4):
                        bk = pp % 2
                        def s2_mm(e, S, pp=pp, bk=bk, xsb=xsb, tsrc=tsrc, rowsb=rowsb):
                            L = []
                            for h in range(2):
                                ci = 2 * pp + h
                                L.append(e.matmul(pb[bk][:, h * TCA:(h + 1) * TCA],
                                                  xsb[:, ci * 128:(ci + 1) * 128],
                                                  tsrc[:], start=True,
                                                  stop=(rowsb is None)))
                                if rowsb is not None:
                                    L.append(e.matmul(
                                        pb[bk][:, h * TCA:(h + 1) * TCA],
                                        rowsb[0:1, ci * 128:(ci + 1) * 128],
                                        ones_sb[0:1, :], start=False, stop=True))
                            return L[-1]
                        group(PE, [tsb, xb, b_w["rows"]], [BK[bk]], s2_mm, 1)
                        def sig_ev(e, S, pp=pp, bk=bk, dst=dst):
                            return e.activation(out=bpair(dst, pp), in_=pb[bk][:, 0:PW],
                                                func=AF.Sigmoid)
                        group(ACT, [BK[bk]], [db], sig_ev, 1)
                group(DVE, [b_wb], [b_wb],
                      lambda e, S: e.tensor_scalar(wb_sb[:], wb_sb[:], EXM05,
                                                   None, ALU.mult), 1)

                MIX3 = [("mixk", "k"), ("mixv", "v"), ("mixr", "r")]
                WMAP = {"k": (wk_sb, b_w["wk"], kscl_sb, b_kscl),
                        "v": (wv_sb, b_w["wv"], vh_sb, b_vh),
                        "r": (wr_sb, b_w["wr"], r_sb, b_r)}
                def emit_mix(mi):
                    mk, tag = MIX3[mi]
                    sl = mi % 2
                    def mix_fn(e, S, c=c, mk=mk, sl=sl):
                        return [e.scalar_tensor_tensor(
                            mixv(sl, ci), sxv(ci), col(mk, ci), xnv(c, ci),
                            ALU.mult, ALU.add) for ci in range(NCC)]
                    group(DVE, [b_sx, bfm, b_cols], [b_mix[sl]], mix_fn, NCC)
                def emit_bigmm(mi):
                    mk, tag = MIX3[mi]
                    sl = mi % 2
                    wsb, wbf, dst, db = WMAP[tag]
                    for pp in range(4):
                        bk = pp % 2
                        def big_mm(e, S, sl=sl, pp=pp, bk=bk, wsb=wsb):
                            for h in range(2):
                                co = 2 * pp + h
                                for cin in range(NCC):
                                    inst = e.matmul(pb[bk][:, h * TCA:(h + 1) * TCA],
                                                    wslice(wsb, cin, co * 128, 128),
                                                    mixv(sl, cin),
                                                    start=(cin == 0), stop=(cin == NCC - 1))
                            return inst
                        group(PE, [b_mix[sl], wbf], [BK[bk]], big_mm, 1)
                        def c_ev(e, S, pp=pp, bk=bk, dst=dst):
                            return e.copy(out=bpair(dst, pp), in_=pb[bk][:, 0:PW])
                        group(ACT, [BK[bk]], [db], c_ev, 1)
                emit_mix(0); emit_mix(1)
                emit_bigmm(0)
                emit_mix(2)
                emit_bigmm(1)
                emit_bigmm(2)

                def st_ksq(pp, s2, bkA, bkB, bkC):
                    group(ACT, [b_kscl], [b_scr["z"][s2]],
                          lambda e, S, pp=pp: e.activation(out=scrv("z", pp),
                                                           in_=bpair(kscl_sb, pp),
                                                           func=AF.Square), 1)
                def st_ssq(pp, s2, bkA, bkB, bkC):
                    group(PE, [b_scr["z"][s2], b_selx], [BK[bkA]],
                          lambda e, S, pp=pp, bkA=bkA: [
                              e.matmul(pb[bkA][:, h * TCA:(h + 1) * TCA], selx_sb[:],
                                       scrh("z", pp, h), start=True, stop=True)
                              for h in range(2)][-1], 1)
                def st_rsnln(pp, s2, bkA, bkB, bkC):
                    group(ACT, [BK[bkA], b_eps], [b_scr["rsn"][s2]],
                          lambda e, S, pp=pp, bkA=bkA: e.activation(
                              out=scrv("rsn", pp), in_=pb[bkA][:, 0:PW], func=AF.Ln,
                              bias=eps24[:, 0:1]), 1)
                def st_rsnexp(pp, s2, bkA, bkB, bkC):
                    group(ACT, [b_scr["rsn"][s2]], [b_scr["rsn"][s2]],
                          lambda e, S, pp=pp: e.activation(
                              out=scrv("rsn", pp), in_=scrv("rsn", pp),
                              func=AF.Exp, scale=-0.5), 1)
                def st_kk(pp, s2, bkA, bkB, bkC):
                    group(PL, [b_scr["rsn"][s2], b_kscl], [b_scr["kk"][s2]],
                          lambda e, S, pp=pp: e.tensor_tensor(
                              scrv("kk", pp), bpair(kscl_sb, pp), scrv("rsn", pp),
                              ALU.mult), 1)
                def st_kv1(pp, s2, bkA, bkB, bkC):
                    group(PL, [b_scr["kk"][s2], b_vh], [b_scr["kv"][s2]],
                          lambda e, S, pp=pp: e.tensor_tensor(
                              scrv("kv", pp), scrv("kk", pp), bpair(vh_sb, pp),
                              ALU.mult), 1)
                def st_kv2(pp, s2, bkA, bkB, bkC):
                    group(PL, [b_scr["kv"][s2], b_ab], [b_scr["kv"][s2]],
                          lambda e, S, pp=pp: e.tensor_tensor(
                              scrv("kv", pp), scrv("kv", pp), bpair(ab_sb, pp),
                              ALU.mult), 1)
                def st_scan(pp, s2, bkA, bkB, bkC):
                    for h in range(2):
                        ci = 2 * pp + h
                        group(DVE, [b_wb, b_scr["kv"][s2], b_carry], [b_scr["st"][s2]],
                              lambda e, S, pp=pp, h=h, ci=ci: e.tensor_tensor_scan(
                                  scrh("st", pp, h), bchunk(wb_sb, ci),
                                  scrh("kv", pp, h), carry[:, ci:ci + 1],
                                  ALU.mult, ALU.add), 1)
                        group(DVE, [b_scr["st"][s2]], [b_carry],
                              lambda e, S, pp=pp, h=h, ci=ci: e.tensor_copy(
                                  carry[:, ci:ci + 1],
                                  scrh("st", pp, h)[:, TCA - 1:TCA]), 1)
                def st_z1(pp, s2, bkA, bkB, bkC):
                    group(PL, [b_r, b_scr["kk"][s2]], [b_scr["z"][s2]],
                          lambda e, S, pp=pp: e.tensor_tensor(
                              scrv("z", pp), bpair(r_sb, pp), scrv("kk", pp),
                              ALU.mult), 1)
                def st_z2(pp, s2, bkA, bkB, bkC):
                    group(PL, [b_scr["z"][s2], b_w["rkexp"]], [b_scr["z"][s2]],
                          lambda e, S, pp=pp: e.tensor_tensor(
                              scrv("z", pp), scrv("z", pp), bpair(rkexp_sb, pp),
                              ALU.mult), 1)
                def st_bb(pp, s2, bkA, bkB, bkC):
                    group(PE, [b_scr["z"][s2], b_selx], [BK[bkA]],
                          lambda e, S, pp=pp, bkA=bkA: [
                              e.matmul(pb[bkA][:, h * TCA:(h + 1) * TCA], selx_sb[:],
                                       scrh("z", pp, h), start=True, stop=True)
                              for h in range(2)][-1], 1)
                def st_t1(pp, s2, bkA, bkB, bkC):
                    group(PL, [b_scr["st"][s2], b_r], [b_scr["t1"][s2]],
                          lambda e, S, pp=pp: e.tensor_tensor(
                              scrv("t1", pp), scrv("st", pp), bpair(r_sb, pp),
                              ALU.mult), 1)
                def st_xw1(pp, s2, bkA, bkB, bkC):
                    group(DVE, [b_vh, BK[bkA]], [b_scr["xw"][s2]],
                          lambda e, S, pp=pp, bkA=bkA: e.tensor_tensor(
                              scrv("xw", pp), bpair(vh_sb, pp), pb[bkA][:, 0:PW],
                              ALU.mult), 1)
                def st_xw2(pp, s2, bkA, bkB, bkC):
                    group(PL, [b_scr["t1"][s2], b_scr["xw"][s2]], [b_scr["xw"][s2]],
                          lambda e, S, pp=pp: e.tensor_tensor(
                              scrv("xw", pp), scrv("xw", pp), scrv("t1", pp),
                              ALU.add), 1)
                def st_xwb(pp, s2, bkA, bkB, bkC):
                    group(ACT, [b_scr["xw"][s2]], [b_scr["xwb"][s2], b_scr["xw2"][s2]],
                          lambda e, S, pp=pp: [
                              e.copy(out=scrv("xwb", pp), in_=scrv("xw", pp)),
                              e.activation(out=scrv("xw2", pp), in_=scrv("xw", pp),
                                           func=AF.Square)], 2)
                def st_mb(pp, s2, bkA, bkB, bkC):
                    group(PE, [b_scr["xwb"][s2], b_selx], [BK[bkB]],
                          lambda e, S, pp=pp, bkB=bkB: [
                              e.matmul(pb[bkB][:, h * TCA:(h + 1) * TCA], selx64_sb[:],
                                       scrh("xwb", pp, h), start=True, stop=True)
                              for h in range(2)][-1], 1)
                def st_s64(pp, s2, bkA, bkB, bkC):
                    group(PE, [b_scr["xw2"][s2], b_selx], [BK[bkC]],
                          lambda e, S, pp=pp, bkC=bkC: [
                              e.matmul(pb[bkC][:, h * TCA:(h + 1) * TCA], selx64_sb[:],
                                       scrh("xw2", pp, h), start=True, stop=True)
                              for h in range(2)][-1], 1)
                def st_mbsq(pp, s2, bkA, bkB, bkC):
                    group(ACT, [BK[bkB]], [b_scr["rsn"][s2]],
                          lambda e, S, pp=pp, bkB=bkB: e.activation(
                              out=scrv("rsn", pp), in_=pb[bkB][:, 0:PW],
                              func=AF.Square), 1)
                def st_var(pp, s2, bkA, bkB, bkC):
                    group(DVE, [BK[bkC], b_scr["rsn"][s2]], [b_scr["kv"][s2]],
                          lambda e, S, pp=pp, bkC=bkC: e.tensor_tensor(
                              scrv("kv", pp), pb[bkC][:, 0:PW], scrv("rsn", pp),
                              ALU.subtract), 1)
                def st_var2(pp, s2, bkA, bkB, bkC):
                    group(DVE, [b_scr["kv"][s2]], [b_scr["kv"][s2]],
                          lambda e, S, pp=pp: e.tensor_scalar(
                              scrv("kv", pp), scrv("kv", pp), 0.0, 1e-5,
                              ALU.max, ALU.add), 1)
                def st_gnln(pp, s2, bkA, bkB, bkC):
                    group(ACT, [b_scr["kv"][s2]], [b_scr["rsn"][s2]],
                          lambda e, S, pp=pp: e.activation(
                              out=scrv("rsn", pp), in_=scrv("kv", pp), func=AF.Ln), 1)
                def st_gnexp(pp, s2, bkA, bkB, bkC):
                    group(ACT, [b_scr["rsn"][s2]], [b_scr["kk"][s2]],
                          lambda e, S, pp=pp: e.activation(
                              out=scrv("kk", pp), in_=scrv("rsn", pp),
                              func=AF.Exp, scale=-0.5), 1)
                def st_gsub(pp, s2, bkA, bkB, bkC):
                    group(DVE, [b_scr["xw"][s2], BK[bkB]], [b_scr["t1"][s2]],
                          lambda e, S, pp=pp, bkB=bkB: e.tensor_tensor(
                              scrv("t1", pp), scrv("xw", pp), pb[bkB][:, 0:PW],
                              ALU.subtract), 1)
                def st_gmul(pp, s2, bkA, bkB, bkC):
                    group(PL, [b_scr["t1"][s2], b_scr["kk"][s2]], [b_scr["t1"][s2]],
                          lambda e, S, pp=pp: e.tensor_tensor(
                              scrv("t1", pp), scrv("t1", pp), scrv("kk", pp),
                              ALU.mult), 1)
                def st_lnr(pp, s2, bkA, bkB, bkC):
                    group(ACT, [b_scr["t1"][s2], b_cols], [b_scr["st"][s2]],
                          lambda e, S, pp=pp: [
                              e.activation(out=scrh("st", pp, h),
                                           in_=scrh("t1", pp, h),
                                           func=AF.Identity,
                                           scale=col("lnw", 2 * pp + h),
                                           bias=col("lnb", 2 * pp + h))
                              for h in range(2)], 2)
                def st_og(pp, s2, bkA, bkB, bkC):
                    group(PL, [b_scr["st"][s2], b_g], [b_og],
                          lambda e, S, pp=pp: e.tensor_tensor(
                              bpair(og_sb, pp), scrv("st", pp), bpair(g_sb, pp),
                              ALU.mult), 1)

                STAGES = [st_ksq, st_ssq, st_rsnln, st_rsnexp, st_kk, st_kv1,
                          st_kv2, st_scan, st_z1, st_z2, st_bb, st_t1, st_xw1,
                          st_xw2, st_xwb, st_mb, st_s64, st_mbsq, st_var,
                          st_var2, st_gnln, st_gnexp, st_gsub, st_gmul,
                          st_lnr, st_og]
                for wv in range(2):
                    for st_fn in STAGES:
                        for pp in (2 * wv, 2 * wv + 1):
                            s2 = pp % 2
                            st_fn(pp, s2, [2, 5][s2], [3, 6][s2], [4, 7][s2])

                for tt in range(NTA):
                    for half in range(2):
                        bk = [2, 5, 3, 6][(tt * 2 + half) % 4]
                        def s_mm(e, S, tt=tt, half=half, bk=bk):
                            for cin in range(NCC):
                                inst = e.matmul(pb[bk][:, 0:512],
                                                bchunk(og_sb, cin)[:, tt * 128:(tt + 1) * 128],
                                                wslice(wo_sb, cin, half * 512, 512),
                                                start=(cin == 0), stop=(cin == NCC - 1))
                            return inst
                        group(PE, [b_og, b_w["wo"]], [BK[bk]], s_mm, 1)
                        def s_ev(e, S, c=c, tt=tt, half=half, bk=bk):
                            return e.copy(
                                out=stm_sb[:, tt * C + half * 512:
                                           tt * C + half * 512 + 512],
                                in_=pb[bk][:, 0:512])
                        group(ACT, [BK[bk]], [b_stm[c % 2]], s_ev, 1)
                def s_store(e, S, c=c):
                    return [e.dma_start(
                        out=s_d[c * TCA + tt * 128:c * TCA + (tt + 1) * 128, :],
                        in_=stm_sb[:, tt * C:(tt + 1) * C]) for tt in range(NTA)]
                group(QS, [b_stm[c % 2]], [b_sdd[c]], s_store, 16 * NTA)

                for tt in range(NTA):
                    gt = c * NTA + tt
                    def lg_mm(e, S, c=c, tt=tt):
                        for cin in range(NCC):
                            inst = e.matmul(pb[1][:, 0:E],
                                            xnv(c, cin)[:, tt * 128:(tt + 1) * 128],
                                            gwt_sb[:, cin * E:(cin + 1) * E],
                                            start=(cin == 0), stop=(cin == NCC - 1))
                        return inst
                    group(PE, [bfm, b_gwt], [BK[1]], lg_mm, 1)
                    group(DVE, [BK[1]], [b_lgx, BK[1]],
                          lambda e, S: e.tensor_copy(lgs[:], pb[1][:, 0:E]), 1)
                    group(DVE, [b_lgx], [b_lgx],
                          lambda e, S: e.reduce_max(m1s[:, 0:1], lgs[:], axis=AX), 1)
                    group(DVE, [b_lgx], [b_lgx],
                          lambda e, S: e.tensor_scalar(eqs[:], lgs[:],
                                                       m1s[:, 0:1], None, ALU.is_ge), 1)
                    group(DVE, [b_lgx], [b_lgx],
                          lambda e, S: e.scalar_tensor_tensor(
                              lg2s[:], eqs[:], -1e30, lgs[:], ALU.mult, ALU.add), 1)
                    group(DVE, [b_lgx], [b_lgx],
                          lambda e, S: e.reduce_max(m2s[:, 0:1], lg2s[:], axis=AX), 1)
                    def m12_fn(e, S, gt=gt):
                        return [
                            e.tensor_copy(mask12_sb[:, gt * 16:gt * 16 + 8], eqs[:]),
                            e.tensor_scalar(mask12_sb[:, gt * 16 + 8:gt * 16 + 16],
                                            lgs[:], m2s[:, 0:1], None, ALU.is_ge)]
                    group(DVE, [b_lgx], [b_m12, b_lgx], m12_fn, 2)
                    group(DVE, [b_lgx, b_m12], [b_m12],
                          lambda e, S, gt=gt: e.tensor_tensor(
                              mask12_sb[:, gt * 16 + 8:gt * 16 + 16],
                              mask12_sb[:, gt * 16 + 8:gt * 16 + 16],
                              eqs[:], ALU.subtract), 1)

                if c + 2 < NCHA:
                    ln0_tail(c + 2)

            barrier()

        esW.close()
        esCB = ExitStack()
        with esCB:
            wc_sb = esCB.enter_context(nc.sbuf_tensor("wc_sb", [128, E * NCC * 1024], BF16))
            b_wc = Buf("wc")

            def wc(e_, cin):
                off = (e_ * NCC + cin) * 1024
                return wc_sb[:, off:off + 1024]

            def wc_ld(e, S):
                return [e.dma_start(
                    out=wc_sb[:, ee * NCC * 1024:(ee + 1) * NCC * 1024],
                    in_=wcall_d[ee * 128:(ee + 1) * 128, :]) for ee in range(E)]
            group(QS, [b_wcall], [b_wc], wc_ld, 16 * E)

            if "wc" in taps:
                group(QG, [b_wc], [],
                      lambda e, S: e.dma_start(out=tap_d["tap_wc"], in_=wc_sb[:]), 16)

            idx_i = esCB.enter_context(nc.sbuf_tensor("idx_i", [128, 2 * (T // 128)],
                                                      mybir.dt.int32))
            esR = ExitStack()
            idnf_sb = esR.enter_context(nc.sbuf_tensor("idnf_sb", [128, 128], F32))
            basec_sb = esR.enter_context(nc.sbuf_tensor("basec_sb", [128, 16], F32))
            ind_fm = esR.enter_context(nc.sbuf_tensor("ind_fm", [16, T], F32))
            zer_fm = esR.enter_context(nc.sbuf_tensor("zer_fm", [16, T], BF16))
            cnt_fm = esR.enter_context(nc.sbuf_tensor("cnt_fm", [16, T], F32))
            cnt_tm = esR.enter_context(nc.sbuf_tensor("cnt_tm", [128, (T // 128) * 16], F32))
            idxf = esR.enter_context(nc.sbuf_tensor("idxf", [128, 16], F32))
            idxr = esR.enter_context(nc.sbuf_tensor("idxr", [128, 2], F32))
            mstg = esR.enter_context(nc.sbuf_tensor("mstg", [128, 16], F32))
            b_route = Buf("route"); b_idx = Buf("idx")
            NTT = T // 128

            group(QS, [], [b_route], lambda e, S: [
                e.dma_start(out=idnf_sb[:], in_=idnf_d),
                e.dma_start(out=basec_sb[:], in_=basec_d)], 32)
            group(DVE, [], [b_route],
                  lambda e, S: e.memset(zer_fm[:], 0.0), 1)
            for gt in range(NTT):
                bk = gt % 2
                group(DVE, [b_m12], [b_route],
                      lambda e, S, gt=gt: e.tensor_copy(
                          mstg[:], mask12_sb[:, gt * 16:(gt + 1) * 16]), 1)
                group(PE, [b_route], [BK[bk]],
                      lambda e, S, bk=bk: e.transpose(
                          pb[bk][0:16, 0:128], mstg[:], idnf_sb[:]), 1)
                group(ACT, [BK[bk]], [b_route, BK[bk]],
                      lambda e, S, gt=gt, bk=bk: e.copy(
                          out=ind_fm[:, gt * 128:(gt + 1) * 128],
                          in_=pb[bk][0:16, 0:128]), 1)
            group(DVE, [b_route], [b_route],
                  lambda e, S: e.tensor_tensor_scan(
                      cnt_fm[:], zer_fm[:], ind_fm[:], 0.0, ALU.add, ALU.add), 1)
            group(DVE, [b_route], [b_route],
                  lambda e, S: e.tensor_scalar(cnt_fm[:], cnt_fm[:], float(CAP - 1),
                                               None, ALU.min), 1)
            for gt in range(NTT):
                bk = gt % 2
                group(PE, [b_route], [BK[bk]],
                      lambda e, S, gt=gt, bk=bk: e.transpose(
                          pb[bk][:, 0:16], cnt_fm[:, gt * 128:(gt + 1) * 128],
                          idnf_sb[0:16, 0:16]), 1)
                group(ACT, [BK[bk]], [b_route, BK[bk]],
                      lambda e, S, gt=gt, bk=bk: e.copy(
                          out=cnt_tm[:, gt * 16:(gt + 1) * 16], in_=pb[bk][:, 0:16]), 1)
            for gt in range(NTT):
                group(DVE, [b_route, b_m12], [b_route],
                      lambda e, S, gt=gt: e.tensor_tensor(
                          idxf[:], cnt_tm[:, gt * 16:(gt + 1) * 16], basec_sb[:],
                          ALU.add), 1)
                group(DVE, [b_route, b_m12], [b_route],
                      lambda e, S, gt=gt: e.tensor_tensor(
                          idxf[:], idxf[:], mask12_sb[:, gt * 16:(gt + 1) * 16],
                          ALU.mult), 1)
                group(DVE, [b_route], [b_route],
                      lambda e, S, gt=gt: [
                          e.reduce_sum(idxr[:, 0:1], idxf[:, 0:8], axis=AX),
                          e.reduce_sum(idxr[:, 1:2], idxf[:, 8:16], axis=AX)], 2)
                group(DVE, [b_route], [b_idx, b_route],
                      lambda e, S, gt=gt: [
                          e.tensor_copy(idx_i[:, gt:gt + 1], idxr[:, 0:1]),
                          e.tensor_copy(idx_i[:, NTT + gt:NTT + gt + 1],
                                        idxr[:, 1:2])], 2)
            xstg = esR.enter_context(nc.sbuf_tensor("xstg", [128, 2 * C], BF16))
            b_xstg = [Buf("xstg0"), Buf("xstg1")]
            for gt in range(NTT):
                par = gt % 2
                group(QS, [b_xnd[NCHA - 1]], [b_xstg[par]],
                      lambda e, S, gt=gt, par=par: e.dma_start(
                          out=xstg[:, par * C:(par + 1) * C],
                          in_=xn_d[gt * 128:(gt + 1) * 128, :]), 16)
                def xsc(e, S, gt=gt, par=par):
                    return [e.indirect_dma_start(
                        out=xng_d,
                        out_offset=IndirectOffsetOnAxis(
                            idx_i[:, rk * NTT + gt:rk * NTT + gt + 1], 0),
                        in_=xstg[:, par * C:(par + 1) * C],
                        in_offset=None) for rk in range(2)]
                group(QG, [b_idx, b_xstg[par]], [b_route], xsc, 32)

            barrier(exclude=("CC",))
            esR.close()

            esB = ExitStack()
            with esB:
                sbufB = lambda name, shape, dt: esB.enter_context(nc.sbuf_tensor(name, shape, dt))
                xngf_sb = sbufB("xngf_sb", [128, 2 * NCC * CAP // 2], BF16)
                y_sb = sbufB("y_sb", [128, 2 * 3 * C], BF16)
                xb_sb = sbufB("xb_sb", [128, 2 * NTB * C], F32)
                sb_sb = sbufB("sb_sb", [128, 2 * NTB * C], BF16)
                f1_sb = sbufB("f1_sb", [128, 2 * 2 * C], BF16)
                f2_sb = sbufB("f2_sb", [128, 2 * 2 * C], BF16)
                acc_sb = sbufB("acc_sb", [128, C], BF16)
                sg_sb = sbufB("sg_sb", [128, C], BF16)
                prod_sb = sbufB("prod_sb", [128, C], BF16)
                b_xngf = [Buf("xngf0"), Buf("xngf1")]
                b_y = [Buf("y0"), Buf("y1")]
                b_xb = [Buf(f"xb{p}") for p in range(2)]
                b_sb = [Buf(f"sb{p}") for p in range(2)]
                b_f1 = [Buf("f1a"), Buf("f1b")]; b_f2 = [Buf("f2a"), Buf("f2b")]
                b_acc = Buf("accB"); b_sg = Buf("sg"); b_prod = Buf("prod")
                b_yd = Buf("yd")
                b_out = [Buf(f"outb{c}") for c in range(NCHB)]
                NTE = CAP // 128

                HB = CAP // 2

                def xngf(par, cin):
                    off = (par * NCC + cin) * HB
                    return xngf_sb[:, off:off + HB]

                for er in range(16):
                    for hb in range(2):
                        par = (er * 2 + hb) % 2
                        def gload(e, S, er=er, hb=hb, par=par):
                            return [e.dma_start(
                                out=xngf(par, cin),
                                in_=xng_d[er * CAP + hb * HB:er * CAP + (hb + 1) * HB,
                                          cin * 128:(cin + 1) * 128],
                                transpose=True) for cin in range(NCC)]
                        group(QS, [b_route], [b_xngf[par]], gload, 16 * NCC)
                        ntk = (HB + 127) // 128
                        for tk in range(ntk):
                            w = min(128, HB - tk * 128)
                            bky = (tk % 2)
                            def moe_mm(e, S, er=er, par=par, tk=tk, bky=bky, w=w):
                                for half in range(2):
                                    for cin in range(NCC):
                                        inst = e.matmul(
                                            pb[bky * 2 + half][0:w, 0:512],
                                            xngf(par, cin)[:, tk * 128:tk * 128 + w],
                                            wc(er % 8, cin)[:, half * 512:(half + 1) * 512],
                                            start=(cin == 0), stop=(cin == NCC - 1))
                                return inst
                            group(PE, [b_xngf[par], b_wc], [BK[bky * 2], BK[bky * 2 + 1]],
                                  moe_mm, 1)
                            def y_ev(e, S, par=par, tk=tk, bky=bky, w=w):
                                return [e.copy(out=y_sb[0:w, (par * 3 + tk) * C + half * 512:
                                                        (par * 3 + tk) * C + half * 512 + 512],
                                               in_=pb[bky * 2 + half][0:w, 0:512])
                                        for half in range(2)]
                            group(ACT, [BK[bky * 2], BK[bky * 2 + 1]],
                                  [b_y[par], BK[bky * 2], BK[bky * 2 + 1]], y_ev, 2)
                        def y_st(e, S, er=er, hb=hb, par=par):
                            r0 = er * CAP + hb * HB
                            if HB < 128:
                                return e.dma_start(
                                    out=yext_d[r0:r0 + HB, :],
                                    in_=y_sb[0:HB, par * 3 * C:par * 3 * C + C])
                            return e.dma_start(
                                out=yext_d[r0:r0 + HB, :].rearrange(
                                    "(k p) n -> p k n", p=128),
                                in_=y_sb[:, par * 3 * C:(par * 3 + HB // 128) * C].rearrange(
                                    "p (k n) -> p k n", n=C))
                        group(QS, [b_y[par]], [b_yd], y_st, 16)

                def ldB(c):
                    def ld(e, S, c=c):
                        L = [e.dma_start(out=xbt(c, tt),
                                         in_=x_d[c * TCB + tt * 128:c * TCB + (tt + 1) * 128, :])
                             for tt in range(NTB)]
                        L += [e.dma_start(
                            out=sbt(c, tt),
                            in_=s_d[c * TCB + tt * 128:c * TCB + (tt + 1) * 128, :])
                            for tt in range(NTB)]
                        return L
                    group(QS, [b_sdd[c * TCB // TCA]], [b_xb[c % 2], b_sb[c % 2]], ld,
                          16 * 2 * NTB)

                def xbt(c, tt):
                    return xb_sb[:, ((c % 2) * NTB + tt) * C:((c % 2) * NTB + tt + 1) * C]
                def sbt(c, tt):
                    return sb_sb[:, ((c % 2) * NTB + tt) * C:((c % 2) * NTB + tt + 1) * C]

                def gf_chunk(c):
                    def gf(e, S, c=c):
                        L = []
                        for tt in range(NTB):
                            gt = c * NTB + tt
                            L.append(e.indirect_dma_start(
                                out=f1_sb[:, ((c % 2) * NTB + tt) * C:
                                          ((c % 2) * NTB + tt + 1) * C],
                                out_offset=None, in_=yext_d,
                                in_offset=IndirectOffsetOnAxis(
                                    idx_i[:, gt:gt + 1], 0)))
                            L.append(e.indirect_dma_start(
                                out=f2_sb[:, ((c % 2) * NTB + tt) * C:
                                          ((c % 2) * NTB + tt + 1) * C],
                                out_offset=None, in_=yext_d,
                                in_offset=IndirectOffsetOnAxis(
                                    idx_i[:, NTT + gt:NTT + gt + 1], 0)))
                        return L
                    group(QG, [b_yd, b_idx], [b_f1[c % 2], b_f2[c % 2]], gf, 16 * 2 * NTB)

                ldB(0)
                gf_chunk(0)
                for c in range(NCHB):
                    if c + 1 < NCHB:
                        ldB(c + 1)
                    for tt in range(NTB):
                        group(DVE, [b_f1[c % 2], b_f2[c % 2]], [b_acc],
                              lambda e, S, c=c, tt=tt: e.tensor_tensor(
                                  acc_sb[:],
                                  f1_sb[:, ((c % 2) * NTB + tt) * C:
                                        ((c % 2) * NTB + tt + 1) * C],
                                  f2_sb[:, ((c % 2) * NTB + tt) * C:
                                        ((c % 2) * NTB + tt + 1) * C], ALU.add), 1)
                        group(ACT, [b_acc], [b_sg],
                              lambda e, S: e.activation(out=sg_sb[:], in_=acc_sb[:],
                                                        func=AF.Sigmoid), 1)
                        group(DVE, [b_sg, b_acc], [b_prod],
                              lambda e, S: e.tensor_tensor(prod_sb[:], acc_sb[:],
                                                           sg_sb[:], ALU.mult), 1)
                        group(DVE, [b_prod, b_sb[c % 2]], [b_prod],
                              lambda e, S, c=c, tt=tt: e.tensor_tensor(
                                  prod_sb[:], prod_sb[:], sbt(c, tt), ALU.mult), 1)
                        group(DVE, [b_prod, b_xb[c % 2]], [b_xb[c % 2]],
                              lambda e, S, c=c, tt=tt: e.tensor_tensor(
                                  xbt(c, tt), xbt(c, tt), prod_sb[:], ALU.add), 1)
                    if c + 1 < NCHB:
                        gf_chunk(c + 1)
                    def out_st(e, S, c=c):
                        return [e.dma_start(
                            out=out_d[c * TCB + tt * 128:c * TCB + (tt + 1) * 128, :],
                            in_=xbt(c, tt)) for tt in range(NTB)]
                    group(QS, [b_xb[c % 2]], [b_out[c]], out_st, 16 * NTB)

                if "xn" in taps:
                    group(QG, [b_xnd[NCHA - 1]], [],
                          lambda e, S: e.dma_start(out=tap_d["tap_xn"], in_=xn_d), 16)
                if "s" in taps:
                    group(QG, [b_sdd[NCHA - 1]], [],
                          lambda e, S: e.dma_start(out=tap_d["tap_s"], in_=s_d), 16)

                QG.emit(lambda e, S: e.wait_ge(S["QG"], QG.n))

                block = es.enter_context(nc.Block())
                S = sems
                block.sync(lambda e: [op(e, S) for op in QS.ops])
                block.tensor(lambda e: [op(e, S) for op in PE.ops])
                block.scalar(lambda e: [op(e, S) for op in ACT.ops])
                block.vector(lambda e: [op(e, S) for op in DVE.ops])
                block.gpsimd(lambda e: [op(e, S) for op in QG.ops])

    return nc


def kernel(**inputs):
    in_maps = [prep_core_inputs(b, inputs) for b in range(8)]
    nc = build(ncores=8)
    res = run_bass_kernel_spmd(nc, in_maps, core_ids=list(range(8)))
    return np.stack([res.results[b]["out"] for b in range(8)], axis=0)




# version 18
# speedup vs baseline: 1.0909x; 1.0383x over previous
import numpy as np
import ml_dtypes
import concourse.bass as bass
import concourse.mybir as mybir
from concourse.bass_utils import run_bass_kernel_spmd
from contextlib import ExitStack

F32 = mybir.dt.float32
BF16 = mybir.dt.bfloat16
AF = mybir.ActivationFunctionType
ALU = mybir.AluOpType
AX = mybir.AxisListType.X

T, C, H, D = 4096, 1024, 16, 64
NCC = 8
TCA = 256
NCHA = T // TCA
NTA = TCA // 128
TCB = 256
NCHB = T // TCB
NTB = TCB // 128
E = 8
J = 4096
NJG = J // 512
EXM05 = float(np.exp(np.float32(-0.5)))


def set_dims(t=4096, j=4096):
    global T, NCHA, NCHB, J, NJG
    T, J = t, j
    NCHA = T // TCA
    NCHB = T // TCB
    NJG = J // 512


class Eng:

    def __init__(self, name, queue=False, ops=None):
        self.name = name
        self.queue = queue
        self.n = 0
        self.ops = [] if ops is None else ops
        self.seen = {}

    def self_wait(self):
        if self.n > 0:
            nm, vv = self.name, self.n
            self.ops.append(lambda e, S: e.wait_ge(S[nm], vv))

    def wait(self, other, val):
        if val <= 0:
            return
        if self.seen.get(other.name, 0) >= val:
            return
        self.seen[other.name] = val
        on, vv = other.name, int(val)
        self.ops.append(lambda e, S: e.wait_ge(S[on], vv))

    def emit(self, fn, tag=0):
        if tag:
            self.n += tag
            sname, step = self.name, tag

            def run(e, S, fn=fn, sname=sname, step=step):
                r = fn(e, S)
                if isinstance(r, (list, tuple)):
                    u = step // len(r)
                    assert u * len(r) == step, (step, len(r))
                    for inst in r:
                        inst.then_inc(S[sname], u)
                else:
                    r.then_inc(S[sname], step)

            self.ops.append(run)
        else:
            self.ops.append(lambda e, S, fn=fn: fn(e, S))
        return self.n


class Buf:

    def __init__(self, name=""):
        self.name = name
        self.w = (None, 0)
        self.r = []


def group(eng, reads, writes, fn, tag):
    if eng.queue:
        eng.self_wait()
    for b in reads:
        if b.w[0] is not None:
            eng.wait(b.w[0], b.w[1])
    for b in writes:
        for (oe, ov) in b.r:
            eng.wait(oe, ov)
        if b.w[0] is not None:
            eng.wait(b.w[0], b.w[1])
    eng.emit(fn, tag=tag)
    for b in reads:
        b.r.append((eng, eng.n))
    for b in writes:
        b.w = (eng, eng.n)
        b.r = []


def colmaj(v):
    return np.ascontiguousarray(np.asarray(v, np.float32).reshape(NCC, 128).T)


def prep_core_inputs(b, I):
    _ = (I["L0_attn_v0"], I["L0_attn_v1"], I["L0_attn_v2"])
    g = lambda k: np.asarray(I[k], np.float32)
    w0 = g("L0_attn_w0").reshape(H)
    a0 = g("L0_attn_a0").reshape(H)
    w2 = g("L0_attn_w2")
    a2 = g("L0_attn_a2")
    selx = np.kron(np.eye(2, dtype=np.float32), np.ones((64, 64), np.float32))
    m = {
        "x": np.ascontiguousarray(g("L0_attn_x")[b]),
        "wr": g("L0_attn_Wr"), "wk": g("L0_attn_Wk"),
        "wv": g("L0_attn_Wv"), "wo": g("L0_attn_Wo"),
        "w1": g("L0_attn_w1"), "a1": g("L0_attn_a1"), "g1": g("L0_attn_g1"),
        "g2": g("L0_attn_g2"),
        "w2x": np.ascontiguousarray(np.repeat(w2, D, axis=1)),
        "a2x": np.ascontiguousarray(np.repeat(a2, D, axis=1)),
        "gwt": np.ascontiguousarray(g("L0_ffn_gate_w").T),
        "wupt": np.ascontiguousarray(g("L0_ffn_w_up")[b % 8].T),
        "wdn": np.ascontiguousarray(g("L0_ffn_w_down")[b % 8]),
        "mixr": colmaj(g("L0_attn_x_r")), "mixw": colmaj(g("L0_attn_x_w")),
        "mixk": colmaj(g("L0_attn_x_k")), "mixv": colmaj(g("L0_attn_x_v")),
        "mixa": colmaj(g("L0_attn_x_a")), "mixg": colmaj(g("L0_attn_x_g")),
        "kkc": colmaj(g("L0_attn_k_k")),
        "lnw": colmaj(g("L0_attn_ln_w")), "lnb": colmaj(g("L0_attn_ln_b")),
        "rkc": colmaj(g("L0_attn_r_k").reshape(C)),
        "w0x": colmaj(np.repeat(w0, D)), "a0x": colmaj(np.repeat(a0, D)),
        "selx": selx.astype(ml_dtypes.bfloat16),
        "selx64": (selx / 64.0).astype(ml_dtypes.bfloat16),
        "w0row": np.ascontiguousarray(
            colmaj(np.repeat(w0, D)).T.reshape(1, C)).astype(ml_dtypes.bfloat16),
        "a0row": np.ascontiguousarray(
            colmaj(np.repeat(a0, D)).T.reshape(1, C)).astype(ml_dtypes.bfloat16),
        "kkw": np.ascontiguousarray(
            np.tile(g("L0_attn_k_k").reshape(1, C), (128, 1))).astype(ml_dtypes.bfloat16),
        "rkexp": np.ascontiguousarray(
            np.repeat(colmaj(g("L0_attn_r_k").reshape(C)), TCA, axis=1)).astype(
                ml_dtypes.bfloat16),
    }
    return m


def build(taps=(), ncores=8):
    nc = bass.Bass("TRN2", target_bir_lowering=False, debug=False)
    dt_in = {}
    def din(name, shape, dt=F32):
        dt_in[name] = nc.dram_tensor(name, list(shape), dt, kind="ExternalInput").ap()
        return dt_in[name]

    x_d = din("x", (T, C))
    wr_d = din("wr", (C, C)); wk_d = din("wk", (C, C))
    wv_d = din("wv", (C, C)); wo_d = din("wo", (C, C))
    w1_d = din("w1", (C, 64)); a1_d = din("a1", (C, 64)); g1_d = din("g1", (C, 64))
    g2_d = din("g2", (64, C)); w2x_d = din("w2x", (64, C)); a2x_d = din("a2x", (64, C))
    gwt_d = din("gwt", (C, E))
    wupt_d = din("wupt", (J, C)); wdn_d = din("wdn", (J, C))
    cols_d = {k: din(k, (128, NCC)) for k in
              ["mixr", "mixw", "mixk", "mixv", "mixa", "mixg",
               "kkc", "lnw", "lnb", "rkc", "w0x", "a0x"]}
    selx_d = din("selx", (128, 128), BF16)
    selx64_d = din("selx64", (128, 128), BF16)
    w0row_d = din("w0row", (1, C), BF16)
    a0row_d = din("a0row", (1, C), BF16)
    kkw_d = din("kkw", (128, C), BF16)
    rkexp_d = din("rkexp", (128, NCC * TCA), BF16)

    xn_d = nc.dram_tensor("xn_d", [T, C], BF16, kind="Internal").ap()
    wcown_d = nc.dram_tensor("wcown_d", [128, NCC * 1024], BF16, kind="Internal").ap()
    wcall_d = nc.dram_tensor("wcall_d", [128 * E, NCC * 1024], BF16, kind="Internal").ap()
    s_d = nc.dram_tensor("s_d", [T, C], BF16, kind="Internal").ap()
    out_d = nc.dram_tensor("out", [T, C], F32, kind="ExternalOutput").ap()
    tap_d = {}
    if "xn" in taps:
        tap_d["tap_xn"] = nc.dram_tensor("tap_xn", [T, C], BF16, kind="ExternalOutput").ap()
    if "s" in taps:
        tap_d["tap_s"] = nc.dram_tensor("tap_s", [T, C], BF16, kind="ExternalOutput").ap()
    if "wc" in taps:
        tap_d["tap_wc"] = nc.dram_tensor("tap_wc", [128, E * NCC * 1024], BF16,
                                         kind="ExternalOutput").ap()
    if "ffn" in taps:
        tap_d["tap_ffn"] = nc.dram_tensor("tap_ffn", [T, C], BF16,
                                          kind="ExternalOutput").ap()
    if "mask" in taps:
        tap_d["tap_mask"] = nc.dram_tensor("tap_mask", [T, E], F32,
                                           kind="ExternalOutput").ap()
    if "lg" in taps:
        tap_d["tap_lg"] = nc.dram_tensor("tap_lg", [T, E], F32,
                                         kind="ExternalOutput").ap()

    QS, QG, PE, ACT, DVE = (Eng("QS", queue=True), Eng("QG", queue=True), Eng("PE"), Eng("ACT"), Eng("DVE"))
    PL = Eng("PL", ops=QG.ops)
    CC = Eng("CC", ops=QG.ops)
    ENGS = [QS, QG, PL, CC, PE, ACT, DVE]

    def barrier(exclude=()):
        snap = [(e, e.n) for e in ENGS if e.name not in exclude]
        for e in ENGS:
            if e.name in exclude:
                continue
            if e.queue:
                e.self_wait()
            for (o, v) in snap:
                e.wait(o, v)

    es = ExitStack()
    sbuf = lambda name, shape, dt: es.enter_context(nc.sbuf_tensor(name, shape, dt))
    with es:
        sems = {}
        for e in ENGS:
            sems[e.name] = es.enter_context(nc.semaphore(e.name))

        cols_sb = {k: sbuf(k + "_sb", [128, NCC], F32) for k in cols_d}
        selx_sb = sbuf("selx_sb", [128, 128], BF16)
        selx64_sb = sbuf("selx64_sb", [128, 128], BF16)
        gwt_sb = sbuf("gwt_sb", [128, NCC * E], BF16)
        carry = sbuf("carry", [128, NCC], F32)
        prevc = sbuf("prevc", [128, NCC], F32)
        sumc = sbuf("sumc", [128, NTA], F32)
        sqc = sbuf("sqc", [128, NTA], F32)
        meanc = sbuf("meanc", [128, NTA], F32)
        varc = sbuf("varc", [128, NTA], F32)
        tmpc = sbuf("tmpc", [128, NTA], F32)
        rstdc = sbuf("rstdc", [128, NTA], F32)
        nmrc = sbuf("nmrc", [128, NTA], F32)
        eps5 = sbuf("eps5", [128, 1], F32)
        eps24 = sbuf("eps24", [128, 1], F32)

        pb = [es.enter_context(nc.psum_tensor(f"pb{i}", [128, 512], F32)) for i in range(8)]
        BK = [Buf(f"bank{i}") for i in range(8)]

        b_cols = Buf("cols"); b_selx = Buf("selx"); b_gwt = Buf("gwt")
        b_carry = Buf("carry"); b_prevc = Buf("prevc")
        b_stat = Buf("stat")
        b_xnd = [Buf(f"xnd{c}") for c in range(NCHA)]
        b_sdd = [Buf(f"sdd{c}") for c in range(NCHA)]

        def ld_smalls(e, S):
            L = [e.dma_start(out=cols_sb[k][:], in_=cols_d[k]) for k in cols_d]
            L.append(e.dma_start(out=selx_sb[:], in_=selx_d))
            L.append(e.dma_start(out=selx64_sb[:], in_=selx64_d))
            return L
        group(QS, [], [b_cols, b_selx], ld_smalls, tag=16 * (len(cols_d) + 2))
        group(QG, [], [b_gwt],
              lambda e, S: e.dma_start(
                  out=gwt_sb[:].rearrange("p (cc e) -> p cc e", e=E),
                  in_=gwt_d.rearrange("(cc p) e -> p cc e", p=128)), tag=16)
        b_eps = Buf("eps")
        group(DVE, [], [b_carry, b_prevc, b_eps],
              lambda e, S: [e.memset(carry[:], 0.0), e.memset(prevc[:], 0.0),
                            e.memset(eps5[:], 1e-5), e.memset(eps24[:], 1e-24)], tag=4)

        esW = ExitStack()
        sbw = lambda name, shape, dt: esW.enter_context(nc.sbuf_tensor(name, shape, dt))
        wr_sb = sbw("wr_sb", [128, NCC * C], BF16)
        wk_sb = sbw("wk_sb", [128, NCC * C], BF16)
        wv_sb = sbw("wv_sb", [128, NCC * C], BF16)
        wo_sb = sbw("wo_sb", [128, NCC * C], BF16)
        w1_sb = sbw("w1_sb", [128, NCC * 64], BF16)
        a1_sb = sbw("a1_sb", [128, NCC * 64], BF16)
        g1_sb = sbw("g1_sb", [128, NCC * 64], BF16)
        g2_sb = sbw("g2_sb", [64, C], BF16)
        w2x_sb = sbw("w2x_sb", [64, C], BF16)
        a2x_sb = sbw("a2x_sb", [64, C], BF16)
        x_sb = sbw("x_sb", [128, 3 * NTA * C], BF16)
        b_w = {k: Buf(k) for k in ["wr", "wk", "wv", "wo", "w1", "a1", "g1",
                                   "g2", "w2x", "a2x", "w1v", "a1v", "g1v",
                                   "rows", "kkw", "rkexp", "mxc"]}
        b_x = [Buf(f"x{p}") for p in range(3)]

        def wcast(dst, src_d, n):
            def fn(e, S, d=dst, s=src_d, n=n):
                return e.dma_start(
                    out=d[:].rearrange("p (cc n) -> p cc n", n=n),
                    in_=s.rearrange("(cc p) n -> p cc n", p=128))
            return fn
        group(QG, [], [b_w["w1"]], wcast(w1_sb, w1_d, 64), 16)
        group(QG, [], [b_w["a1"]], wcast(a1_sb, a1_d, 64), 16)
        group(QG, [], [b_w["g1"]], wcast(g1_sb, g1_d, 64), 16)
        group(QG, [], [b_w["g2"]], lambda e, S: e.dma_start(out=g2_sb[:], in_=g2_d), 16)
        group(QG, [], [b_w["w2x"]], lambda e, S: e.dma_start(out=w2x_sb[:], in_=w2x_d), 16)
        group(QG, [], [b_w["a2x"]], lambda e, S: e.dma_start(out=a2x_sb[:], in_=a2x_d), 16)

        def load_x(c):
            def ld(e, S, c=c):
                return [e.dma_start(
                    out=x_sb[:, ((c % 3) * NTA + tt) * C:((c % 3) * NTA + tt + 1) * C],
                    in_=x_d[c * TCA + tt * 128:c * TCA + (tt + 1) * 128, :])
                    for tt in range(NTA)]
            group(QG, [], [b_x[c % 3]], ld, 16 * NTA)
        for cc_ in range(min(3, NCHA)):
            load_x(cc_)
        group(QG, [], [b_w["wr"]], wcast(wr_sb, wr_d, C), 16)
        group(QG, [], [b_w["wk"]], wcast(wk_sb, wk_d, C), 16)
        group(QG, [], [b_w["wv"]], wcast(wv_sb, wv_d, C), 16)
        group(QG, [], [b_w["wo"]], wcast(wo_sb, wo_d, C), 16)

        b_wcall = Buf("wcall")
        esC = ExitStack()
        with esC:
            wcst = esC.enter_context(nc.sbuf_tensor("wcst", [128, NCC * 1024], BF16))
            wdn_bf = esC.enter_context(nc.sbuf_tensor("wdn_bf", [128, 32 * 1024], BF16))
            wup_q = esC.enter_context(nc.sbuf_tensor("wup_q", [128, 2 * 8 * 512], BF16))
            b_wupq = [Buf("wupq0"), Buf("wupq1")]
            b_wdnq = [Buf(f"wdnq{q}") for q in range(4)]
            b_wcown = Buf("wcown"); b_wcst = Buf("wcst")
            NJC = J // 128

            def c0_lddn(q):
                def fn(e, S, q=q):
                    jgs = NJC // 4
                    src_d2 = wdn_d.rearrange("(jg p) n -> jg p n", p=128)[
                        q * jgs:(q + 1) * jgs].rearrange("jg p n -> p jg n")
                    return e.dma_start(
                        out=wdn_bf[:, q * jgs * 1024:(q + 1) * jgs * 1024].rearrange(
                            "p (jg n) -> p jg n", n=1024),
                        in_=src_d2)
                group(QG, [], [b_wdnq[q]], fn, 16)

            for cinh in range(2):
                for q in range(4):
                    jgs = NJC // 4
                    p = q % 2
                    if cinh == 0:
                        c0_lddn(q)
                    def c0_ldup(e, S, cinh=cinh, q=q, p=p, jgs=jgs):
                        src_u = wupt_d.rearrange("(jg p) n -> jg p n", p=128)[
                            q * jgs:(q + 1) * jgs, :,
                            cinh * 512:(cinh + 1) * 512].rearrange("jg p n -> p jg n")
                        return e.dma_start(
                            out=wup_q[:, p * jgs * 512:(p + 1) * jgs * 512].rearrange(
                                "p (jg n) -> p jg n", n=512),
                            in_=src_u)
                    group(QG, [], [b_wupq[p]], c0_ldup, 16)
                    def c0_mm(e, S, cinh=cinh, q=q, p=p, jgs=jgs):
                        for jq in range(jgs):
                            jg = q * jgs + jq
                            for cc in range(4):
                                for half in range(2):
                                    inst = e.matmul(
                                        pb[cc * 2 + half][:, 0:512],
                                        wup_q[:, (p * jgs + jq) * 512 + cc * 128:
                                              (p * jgs + jq) * 512 + cc * 128 + 128],
                                        wdn_bf[:, jg * 1024 + half * 512:
                                               jg * 1024 + half * 512 + 512],
                                        start=(jg == 0), stop=(jg == NJC - 1))
                        return inst
                    group(PE, [b_wupq[p], b_wdnq[q]],
                          [BK[i] for i in range(8)] if q == 0 else [], c0_mm, 1)
                    if q != 0:
                        for i in range(8):
                            BK[i].w = (PE, PE.n)
                def c0_ev(e, S, cinh=cinh):
                    return [e.copy(
                        out=wcst[:, (cinh * 4 + cc) * 1024 + half * 512:
                                 (cinh * 4 + cc) * 1024 + half * 512 + 512],
                        in_=pb[cc * 2 + half][:, 0:512])
                        for cc in range(4) for half in range(2)]
                group(ACT, [BK[i] for i in range(8)],
                      [b_wcst] + [BK[i] for i in range(8)], c0_ev, 8)

            group(QG, [b_wcst], [b_wcown],
                  lambda e, S: e.dma_start(out=wcown_d, in_=wcst[:]), 16)
            group(CC, [b_wcown], [b_wcall],
                  lambda e, S: e.collective_compute(
                      "AllGather", ALU.bypass,
                      [list(range(ncores))], [wcown_d], [wcall_d]), 1)
        barrier(exclude=("CC",))

        TW = TCA + 16
        PW = 2 * TCA
        esA = ExitStack()
        sbufA = lambda name, shape, dt: esA.enter_context(nc.sbuf_tensor(name, shape, dt))
        with esA:
            w1v_sb = sbufA("w1v_sb", [128, 2 * NCC * 64], BF16)
            a1v_sb = sbufA("a1v_sb", [128, 2 * NCC * 64], BF16)
            g1v_sb = sbufA("g1v_sb", [128, 2 * NCC * 64], BF16)
            w0row_sb = sbufA("w0row_sb", [1, C], BF16)
            a0row_sb = sbufA("a0row_sb", [1, C], BF16)
            ones_sb = sbufA("ones_sb", [1, TCA], BF16)
            kkw_sb = sbufA("kkw_sb", [128, C], BF16)
            rkexp_sb = sbufA("rkexp_sb", [128, NCC * TCA], BF16)
            mxc_sb = sbufA("mxc_sb", [128, 6 * NCC], F32)
            xntm_sb = sbufA("xntm_sb", [128, 2 * NTA * C], BF16)
            xnfm_sb = sbufA("xnfm_sb", [128, 3 * NCC * TW], BF16)
            sx_sb = sbufA("sx_sb", [128, NCC * TCA], BF16)
            mix_sb = sbufA("mix_sb", [128, 2 * NCC * TCA], BF16)
            kscl_sb = sbufA("kscl_sb", [128, NCC * TCA], BF16)
            vh_sb = sbufA("vh_sb", [128, NCC * TCA], BF16)
            r_sb = sbufA("r_sb", [128, NCC * TCA], BF16)
            g_sb = sbufA("g_sb", [128, NCC * TCA], BF16)
            ab_sb = sbufA("ab_sb", [128, NCC * TCA], BF16)
            og_sb = sbufA("og_sb", [128, NCC * TCA], BF16)
            wb_sb = sbufA("wb_sb", [128, NCC * TCA], F32)
            tw_sb = sbufA("tw_sb", [64, TCA], BF16)
            ta_sb = sbufA("ta_sb", [64, TCA], BF16)
            tg_sb = sbufA("tg_sb", [64, TCA], BF16)
            lnscr = sbufA("lnscr", [128, C], F32)
            stm_sb = sbufA("stm_sb", [128, NTA * C], BF16)
            scr = {}
            for nm in ["rsn", "kk", "kv", "st", "t1", "xw"]:
                scr[nm] = sbufA("scr_" + nm, [128, 2 * PW], F32)
            scrb = {}
            for nm in ["z", "xwb", "xw2"]:
                scrb[nm] = sbufA("scrb_" + nm, [128, 2 * PW], BF16)

            b_xntm = [Buf(f"xntm{p}") for p in range(2)]
            b_xnfm = [Buf(f"xnfm{p}") for p in range(3)]
            b_sx = Buf("sx")
            b_mix = [Buf("mix0"), Buf("mix1")]
            b_kscl = Buf("kscl"); b_vh = Buf("vh"); b_r = Buf("r")
            b_g = Buf("g"); b_ab = Buf("ab"); b_og = Buf("og"); b_wb = Buf("wb")
            b_tw = Buf("tw"); b_ta = Buf("ta"); b_tg = Buf("tg")
            b_lnscr = Buf("lnscr")
            b_stm = [Buf("stm"), Buf("stm")]
            b_stm[1] = b_stm[0]
            b_scr = {nm: [Buf(nm + "0"), Buf(nm + "1")] for nm in
                     list(scr) + list(scrb)}

            group(QS, [], [b_w["rows"]], lambda e, S: [
                e.dma_start(out=w0row_sb[:], in_=w0row_d),
                e.dma_start(out=a0row_sb[:], in_=a0row_d),
                e.dma_start(out=kkw_sb[:], in_=kkw_d),
                e.dma_start(out=rkexp_sb[:], in_=rkexp_d)], 64)
            group(DVE, [], [b_w["rows"]],
                  lambda e, S: e.memset(ones_sb[:], 1.0), 1)

            def kfold(e, S):
                return [e.tensor_tensor(wk_sb[:, cc * C:(cc + 1) * C],
                                        wk_sb[:, cc * C:(cc + 1) * C], kkw_sb[:],
                                        ALU.mult) for cc in range(NCC)]
            group(DVE, [b_w["wk"], b_w["rows"]], [b_w["wk"]], kfold, NCC)
            def mxc(e, S):
                L = []
                for i, mk in enumerate(["mixw", "mixa", "mixg"]):
                    L.append(e.tensor_scalar(mxc_sb[:, (2 * i) * NCC:(2 * i + 1) * NCC],
                                             cols_sb[mk][:], 1.0, None, ALU.add))
                    L.append(e.tensor_scalar(mxc_sb[:, (2 * i + 1) * NCC:(2 * i + 2) * NCC],
                                             cols_sb[mk][:], -1.0, None, ALU.mult))
                return L
            group(DVE, [b_cols], [b_w["mxc"]], mxc, 6)
            def lrv(e, S):
                L = []
                for i, (wsb, vsb) in enumerate([(w1_sb, w1v_sb), (a1_sb, a1v_sb),
                                                (g1_sb, g1v_sb)]):
                    for v in range(2):
                        for cc in range(NCC):
                            L.append(e.tensor_scalar(
                                vsb[:, (v * NCC + cc) * 64:(v * NCC + cc + 1) * 64],
                                wsb[:, cc * 64:(cc + 1) * 64],
                                mxc_sb[:, (2 * i + v) * NCC + cc:
                                       (2 * i + v) * NCC + cc + 1],
                                None, ALU.mult))
                return L
            group(DVE, [b_w["w1"], b_w["a1"], b_w["g1"], b_w["mxc"]],
                  [b_w["w1v"], b_w["a1v"], b_w["g1v"]], lrv, 6 * NCC)

            def xt(c, tt):
                return x_sb[:, ((c % 3) * NTA + tt) * C:((c % 3) * NTA + tt + 1) * C]
            def xntm(c, tt):
                return xntm_sb[:, ((c % 2) * NTA + tt) * C:((c % 2) * NTA + tt + 1) * C]
            def fmtile(c, ci):
                off = ((c % 3) * NCC + ci) * TW
                return xnfm_sb[:, off:off + TW]
            def xnv(c, ci):
                return fmtile(c, ci)[:, 16:TW]
            def xnsh(c, ci):
                return fmtile(c, ci)[:, 15:TW - 1]
            def sxv(ci):
                return sx_sb[:, ci * TCA:(ci + 1) * TCA]
            def mixv(sl, ci):
                return mix_sb[:, (sl * NCC + ci) * TCA:(sl * NCC + ci + 1) * TCA]
            def bchunk(buf, ci, w=TCA):
                return buf[:, ci * w:(ci + 1) * w]
            def bpair(buf, pp):
                return buf[:, pp * PW:(pp + 1) * PW]
            def scrv(nm, pp):
                t = scr[nm] if nm in scr else scrb[nm]
                return t[:, (pp % 2) * PW:((pp % 2) + 1) * PW]
            def scrh(nm, pp, h):
                return scrv(nm, pp)[:, h * TCA:(h + 1) * TCA]
            def col(k, ci):
                return cols_sb[k][:, ci:ci + 1]
            def wslice(wsb, cin, j0, jw):
                return wsb[:, cin * C + j0:cin * C + j0 + jw]

            def ln0_tail(c):
                bx, bxn, bfm = b_x[c % 3], b_xntm[c % 2], b_xnfm[c % 3]
                bst = b_stat
                for tt in range(NTA):
                    group(ACT, [bx], [b_lnscr, bst],
                          lambda e, S, tt=tt: e.activation(
                              out=lnscr[:], in_=xt(c, tt), func=AF.Square,
                              accum_out=sqc[:, tt:tt + 1]), 1)
                    group(ACT, [bx], [b_lnscr, bst],
                          lambda e, S, tt=tt: e.activation(
                              out=lnscr[:], in_=xt(c, tt), func=AF.Identity,
                              accum_out=sumc[:, tt:tt + 1]), 1)
                group(DVE, [bst], [bst], lambda e, S: [
                    e.tensor_scalar(meanc[:], sumc[:], 1.0 / C, None, ALU.mult),
                    e.tensor_scalar(varc[:], sqc[:], 1.0 / C, None, ALU.mult)], 2)
                group(DVE, [bst], [bst],
                      lambda e, S: e.tensor_tensor(tmpc[:], meanc[:], meanc[:], ALU.mult), 1)
                group(DVE, [bst], [bst],
                      lambda e, S: e.tensor_tensor(varc[:], varc[:], tmpc[:], ALU.subtract), 1)
                group(ACT, [bst, b_eps], [bst],
                      lambda e, S: e.activation(out=tmpc[:], in_=varc[:], func=AF.Ln,
                                                bias=eps5[:, 0:1]), 1)
                group(ACT, [bst], [bst],
                      lambda e, S: e.activation(out=rstdc[:], in_=tmpc[:], func=AF.Exp,
                                                scale=-0.5), 1)
                group(DVE, [bst], [bst],
                      lambda e, S: e.scalar_tensor_tensor(
                          nmrc[:], meanc[:], -1.0, rstdc[:], ALU.mult, ALU.mult), 1)
                def apply(e, S):
                    return [e.activation(out=xntm(c, tt), in_=xt(c, tt), func=AF.Identity,
                                         scale=rstdc[:, tt:tt + 1], bias=nmrc[:, tt:tt + 1])
                            for tt in range(NTA)]
                group(ACT, [bx, bst], [bxn], apply, NTA)
                def store(e, S):
                    return [e.dma_start(out=xn_d[c * TCA + tt * 128:c * TCA + (tt + 1) * 128, :],
                                        in_=xntm(c, tt)) for tt in range(NTA)]
                group(QS, [bxn], [b_xnd[c]], store, 16 * NTA)
                def tpr(e, S):
                    return [e.dma_start(out=xnv(c, ci),
                                        in_=xn_d[c * TCA:(c + 1) * TCA, ci * 128:(ci + 1) * 128],
                                        transpose=True) for ci in range(NCC)]
                group(QS, [b_xnd[c]], [bfm], tpr, 16 * NCC)

            for cc_ in range(min(2, NCHA)):
                ln0_tail(cc_)

            for c in range(NCHA):
                if c + 3 < NCHA:
                    load_x(c + 3)
                bfm = b_xnfm[c % 3]
                if c == 0:
                    group(DVE, [], [bfm], lambda e, S: [
                        e.memset(fmtile(0, ci)[:, 0:16], 0.0) for ci in range(NCC)], NCC)
                else:
                    bprev = b_xnfm[(c - 1) % 3]
                    group(DVE, [bprev], [bfm], lambda e, S, c=c: [
                        e.tensor_copy(fmtile(c, ci)[:, 0:16],
                                      fmtile(c - 1, ci)[:, TW - 16:TW])
                        for ci in range(NCC)], NCC)
                group(DVE, [bfm], [b_sx], lambda e, S, c=c: [
                    e.tensor_tensor(sxv(ci), xnv(c, ci), xnsh(c, ci), ALU.subtract)
                    for ci in range(NCC)], NCC)

                for pi, (vsb, vbuf, tdst, tb) in enumerate(
                        [(w1v_sb, b_w["w1v"], tw_sb, b_tw),
                         (a1v_sb, b_w["a1v"], ta_sb, b_ta),
                         (g1v_sb, b_w["g1v"], tg_sb, b_tg)]):
                    def lr_mm(e, S, c=c, vsb=vsb):
                        L = []
                        for cin in range(NCC):
                            L.append(e.matmul(
                                pb[0][0:64, 0:TCA],
                                vsb[:, cin * 64:(cin + 1) * 64],
                                xnv(c, cin), start=(cin == 0), stop=False))
                            L.append(e.matmul(
                                pb[0][0:64, 0:TCA],
                                vsb[:, (NCC + cin) * 64:(NCC + cin + 1) * 64],
                                xnsh(c, cin), start=False,
                                stop=(cin == NCC - 1)))
                        return L[-1]
                    group(PE, [bfm, vbuf], [BK[0]], lr_mm, 1)
                    group(ACT, [BK[0]], [tb],
                          lambda e, S, tdst=tdst: e.activation(
                              out=tdst[:], in_=pb[0][0:64, 0:TCA], func=AF.Tanh), 1)
                for pi, (xsb, xb, tsrc, tsb, rowsb, dst, db) in enumerate(
                        [(w2x_sb, b_w["w2x"], tw_sb, b_tw, w0row_sb, wb_sb, b_wb),
                         (a2x_sb, b_w["a2x"], ta_sb, b_ta, a0row_sb, ab_sb, b_ab),
                         (g2_sb, b_w["g2"], tg_sb, b_tg, None, g_sb, b_g)]):
                    for pp in range(4):
                        bk = pp % 2
                        def s2_mm(e, S, pp=pp, bk=bk, xsb=xsb, tsrc=tsrc, rowsb=rowsb):
                            L = []
                            for h in range(2):
                                ci = 2 * pp + h
                                L.append(e.matmul(pb[bk][:, h * TCA:(h + 1) * TCA],
                                                  xsb[:, ci * 128:(ci + 1) * 128],
                                                  tsrc[:], start=True,
                                                  stop=(rowsb is None)))
                                if rowsb is not None:
                                    L.append(e.matmul(
                                        pb[bk][:, h * TCA:(h + 1) * TCA],
                                        rowsb[0:1, ci * 128:(ci + 1) * 128],
                                        ones_sb[0:1, :], start=False, stop=True))
                            return L[-1]
                        group(PE, [tsb, xb, b_w["rows"]], [BK[bk]], s2_mm, 1)
                        def sig_ev(e, S, pp=pp, bk=bk, dst=dst):
                            return e.activation(out=bpair(dst, pp), in_=pb[bk][:, 0:PW],
                                                func=AF.Sigmoid)
                        group(ACT, [BK[bk]], [db], sig_ev, 1)
                group(DVE, [b_wb], [b_wb],
                      lambda e, S: e.tensor_scalar(wb_sb[:], wb_sb[:], EXM05,
                                                   None, ALU.mult), 1)

                MIX3 = [("mixk", "k"), ("mixv", "v"), ("mixr", "r")]
                WMAP = {"k": (wk_sb, b_w["wk"], kscl_sb, b_kscl),
                        "v": (wv_sb, b_w["wv"], vh_sb, b_vh),
                        "r": (wr_sb, b_w["wr"], r_sb, b_r)}
                def emit_mix(mi):
                    mk, tag = MIX3[mi]
                    sl = mi % 2
                    def mix_fn(e, S, c=c, mk=mk, sl=sl):
                        return [e.scalar_tensor_tensor(
                            mixv(sl, ci), sxv(ci), col(mk, ci), xnv(c, ci),
                            ALU.mult, ALU.add) for ci in range(NCC)]
                    group(DVE, [b_sx, bfm, b_cols], [b_mix[sl]], mix_fn, NCC)
                def emit_bigmm(mi):
                    mk, tag = MIX3[mi]
                    sl = mi % 2
                    wsb, wbf, dst, db = WMAP[tag]
                    for pp in range(4):
                        bk = pp % 2
                        def big_mm(e, S, sl=sl, pp=pp, bk=bk, wsb=wsb):
                            for h in range(2):
                                co = 2 * pp + h
                                for cin in range(NCC):
                                    inst = e.matmul(pb[bk][:, h * TCA:(h + 1) * TCA],
                                                    wslice(wsb, cin, co * 128, 128),
                                                    mixv(sl, cin),
                                                    start=(cin == 0), stop=(cin == NCC - 1))
                            return inst
                        group(PE, [b_mix[sl], wbf], [BK[bk]], big_mm, 1)
                        def c_ev(e, S, pp=pp, bk=bk, dst=dst):
                            return e.tensor_copy(bpair(dst, pp), pb[bk][:, 0:PW])
                        group(DVE, [BK[bk]], [db], c_ev, 1)
                emit_mix(0); emit_mix(1)
                emit_bigmm(0)
                emit_mix(2)
                emit_bigmm(1)
                emit_bigmm(2)

                def st_ksq(pp, s2, bkA, bkB, bkC):
                    group(ACT, [b_kscl], [b_scr["z"][s2]],
                          lambda e, S, pp=pp: e.activation(out=scrv("z", pp),
                                                           in_=bpair(kscl_sb, pp),
                                                           func=AF.Square), 1)
                def st_ssq(pp, s2, bkA, bkB, bkC):
                    group(PE, [b_scr["z"][s2], b_selx], [BK[bkA]],
                          lambda e, S, pp=pp, bkA=bkA: [
                              e.matmul(pb[bkA][:, h * TCA:(h + 1) * TCA], selx_sb[:],
                                       scrh("z", pp, h), start=True, stop=True)
                              for h in range(2)][-1], 1)
                def st_rsnln(pp, s2, bkA, bkB, bkC):
                    group(ACT, [BK[bkA], b_eps], [b_scr["rsn"][s2]],
                          lambda e, S, pp=pp, bkA=bkA: e.activation(
                              out=scrv("rsn", pp), in_=pb[bkA][:, 0:PW], func=AF.Ln,
                              bias=eps24[:, 0:1]), 1)
                def st_rsnexp(pp, s2, bkA, bkB, bkC):
                    group(ACT, [b_scr["rsn"][s2]], [b_scr["rsn"][s2]],
                          lambda e, S, pp=pp: e.activation(
                              out=scrv("rsn", pp), in_=scrv("rsn", pp),
                              func=AF.Exp, scale=-0.5), 1)
                def st_kk(pp, s2, bkA, bkB, bkC):
                    group(PL, [b_scr["rsn"][s2], b_kscl], [b_scr["kk"][s2]],
                          lambda e, S, pp=pp: e.tensor_tensor(
                              scrv("kk", pp), bpair(kscl_sb, pp), scrv("rsn", pp),
                              ALU.mult), 1)
                def st_kv1(pp, s2, bkA, bkB, bkC):
                    group(PL, [b_scr["kk"][s2], b_vh], [b_scr["kv"][s2]],
                          lambda e, S, pp=pp: e.tensor_tensor(
                              scrv("kv", pp), scrv("kk", pp), bpair(vh_sb, pp),
                              ALU.mult), 1)
                def st_kv2(pp, s2, bkA, bkB, bkC):
                    group(PL, [b_scr["kv"][s2], b_ab], [b_scr["kv"][s2]],
                          lambda e, S, pp=pp: e.tensor_tensor(
                              scrv("kv", pp), scrv("kv", pp), bpair(ab_sb, pp),
                              ALU.mult), 1)
                def st_scan(pp, s2, bkA, bkB, bkC):
                    for h in range(2):
                        ci = 2 * pp + h
                        group(DVE, [b_wb, b_scr["kv"][s2], b_carry], [b_scr["st"][s2]],
                              lambda e, S, pp=pp, h=h, ci=ci: e.tensor_tensor_scan(
                                  scrh("st", pp, h), bchunk(wb_sb, ci),
                                  scrh("kv", pp, h), carry[:, ci:ci + 1],
                                  ALU.mult, ALU.add), 1)
                        group(DVE, [b_scr["st"][s2]], [b_carry],
                              lambda e, S, pp=pp, h=h, ci=ci: e.tensor_copy(
                                  carry[:, ci:ci + 1],
                                  scrh("st", pp, h)[:, TCA - 1:TCA]), 1)
                def st_z1(pp, s2, bkA, bkB, bkC):
                    group(PL, [b_r, b_scr["kk"][s2]], [b_scr["z"][s2]],
                          lambda e, S, pp=pp: e.tensor_tensor(
                              scrv("z", pp), bpair(r_sb, pp), scrv("kk", pp),
                              ALU.mult), 1)
                def st_z2(pp, s2, bkA, bkB, bkC):
                    group(PL, [b_scr["z"][s2], b_w["rkexp"]], [b_scr["z"][s2]],
                          lambda e, S, pp=pp: e.tensor_tensor(
                              scrv("z", pp), scrv("z", pp), bpair(rkexp_sb, pp),
                              ALU.mult), 1)
                def st_bb(pp, s2, bkA, bkB, bkC):
                    group(PE, [b_scr["z"][s2], b_selx], [BK[bkA]],
                          lambda e, S, pp=pp, bkA=bkA: [
                              e.matmul(pb[bkA][:, h * TCA:(h + 1) * TCA], selx_sb[:],
                                       scrh("z", pp, h), start=True, stop=True)
                              for h in range(2)][-1], 1)
                def st_t1(pp, s2, bkA, bkB, bkC):
                    group(PL, [b_scr["st"][s2], b_r], [b_scr["t1"][s2]],
                          lambda e, S, pp=pp: e.tensor_tensor(
                              scrv("t1", pp), scrv("st", pp), bpair(r_sb, pp),
                              ALU.mult), 1)
                def st_xw1(pp, s2, bkA, bkB, bkC):
                    group(DVE, [b_vh, BK[bkA]], [b_scr["xw"][s2]],
                          lambda e, S, pp=pp, bkA=bkA: e.tensor_tensor(
                              scrv("xw", pp), bpair(vh_sb, pp), pb[bkA][:, 0:PW],
                              ALU.mult), 1)
                def st_xw2(pp, s2, bkA, bkB, bkC):
                    group(PL, [b_scr["t1"][s2], b_scr["xw"][s2]], [b_scr["xw"][s2]],
                          lambda e, S, pp=pp: e.tensor_tensor(
                              scrv("xw", pp), scrv("xw", pp), scrv("t1", pp),
                              ALU.add), 1)
                def st_xwb(pp, s2, bkA, bkB, bkC):
                    group(ACT, [b_scr["xw"][s2]], [b_scr["xwb"][s2], b_scr["xw2"][s2]],
                          lambda e, S, pp=pp: [
                              e.copy(out=scrv("xwb", pp), in_=scrv("xw", pp)),
                              e.activation(out=scrv("xw2", pp), in_=scrv("xw", pp),
                                           func=AF.Square)], 2)
                def st_mb(pp, s2, bkA, bkB, bkC):
                    group(PE, [b_scr["xwb"][s2], b_selx], [BK[bkB]],
                          lambda e, S, pp=pp, bkB=bkB: [
                              e.matmul(pb[bkB][:, h * TCA:(h + 1) * TCA], selx64_sb[:],
                                       scrh("xwb", pp, h), start=True, stop=True)
                              for h in range(2)][-1], 1)
                def st_s64(pp, s2, bkA, bkB, bkC):
                    group(PE, [b_scr["xw2"][s2], b_selx], [BK[bkC]],
                          lambda e, S, pp=pp, bkC=bkC: [
                              e.matmul(pb[bkC][:, h * TCA:(h + 1) * TCA], selx64_sb[:],
                                       scrh("xw2", pp, h), start=True, stop=True)
                              for h in range(2)][-1], 1)
                def st_mbsq(pp, s2, bkA, bkB, bkC):
                    group(ACT, [BK[bkB]], [b_scr["rsn"][s2]],
                          lambda e, S, pp=pp, bkB=bkB: e.activation(
                              out=scrv("rsn", pp), in_=pb[bkB][:, 0:PW],
                              func=AF.Square), 1)
                def st_var(pp, s2, bkA, bkB, bkC):
                    group(DVE, [BK[bkC], b_scr["rsn"][s2]], [b_scr["kv"][s2]],
                          lambda e, S, pp=pp, bkC=bkC: e.tensor_tensor(
                              scrv("kv", pp), pb[bkC][:, 0:PW], scrv("rsn", pp),
                              ALU.subtract), 1)
                def st_var2(pp, s2, bkA, bkB, bkC):
                    group(DVE, [b_scr["kv"][s2]], [b_scr["kv"][s2]],
                          lambda e, S, pp=pp: e.tensor_scalar(
                              scrv("kv", pp), scrv("kv", pp), 0.0, 1e-5,
                              ALU.max, ALU.add), 1)
                def st_gnln(pp, s2, bkA, bkB, bkC):
                    group(ACT, [b_scr["kv"][s2]], [b_scr["rsn"][s2]],
                          lambda e, S, pp=pp: e.activation(
                              out=scrv("rsn", pp), in_=scrv("kv", pp), func=AF.Ln), 1)
                def st_gnexp(pp, s2, bkA, bkB, bkC):
                    group(ACT, [b_scr["rsn"][s2]], [b_scr["kk"][s2]],
                          lambda e, S, pp=pp: e.activation(
                              out=scrv("kk", pp), in_=scrv("rsn", pp),
                              func=AF.Exp, scale=-0.5), 1)
                def st_gsub(pp, s2, bkA, bkB, bkC):
                    group(DVE, [b_scr["xw"][s2], BK[bkB]], [b_scr["t1"][s2]],
                          lambda e, S, pp=pp, bkB=bkB: e.tensor_tensor(
                              scrv("t1", pp), scrv("xw", pp), pb[bkB][:, 0:PW],
                              ALU.subtract), 1)
                def st_gmul(pp, s2, bkA, bkB, bkC):
                    group(PL, [b_scr["t1"][s2], b_scr["kk"][s2]], [b_scr["t1"][s2]],
                          lambda e, S, pp=pp: e.tensor_tensor(
                              scrv("t1", pp), scrv("t1", pp), scrv("kk", pp),
                              ALU.mult), 1)
                def st_lnr(pp, s2, bkA, bkB, bkC):
                    group(ACT, [b_scr["t1"][s2], b_cols], [b_scr["st"][s2]],
                          lambda e, S, pp=pp: [
                              e.activation(out=scrh("st", pp, h),
                                           in_=scrh("t1", pp, h),
                                           func=AF.Identity,
                                           scale=col("lnw", 2 * pp + h),
                                           bias=col("lnb", 2 * pp + h))
                              for h in range(2)], 2)
                def st_og(pp, s2, bkA, bkB, bkC):
                    group(PL, [b_scr["st"][s2], b_g], [b_og],
                          lambda e, S, pp=pp: e.tensor_tensor(
                              bpair(og_sb, pp), scrv("st", pp), bpair(g_sb, pp),
                              ALU.mult), 1)

                STAGES = [st_ksq, st_ssq, st_rsnln, st_rsnexp, st_kk, st_kv1,
                          st_kv2, st_scan, st_z1, st_z2, st_bb, st_t1, st_xw1,
                          st_xw2, st_xwb, st_mb, st_s64, st_mbsq, st_var,
                          st_var2, st_gnln, st_gnexp, st_gsub, st_gmul,
                          st_lnr, st_og]
                for wv in range(2):
                    for st_fn in STAGES:
                        for pp in (2 * wv, 2 * wv + 1):
                            s2 = pp % 2
                            st_fn(pp, s2, [2, 5][s2], [3, 6][s2], [4, 7][s2])

                for tt in range(NTA):
                    for half in range(2):
                        bk = [2, 5, 3, 6][(tt * 2 + half) % 4]
                        def s_mm(e, S, tt=tt, half=half, bk=bk):
                            for cin in range(NCC):
                                inst = e.matmul(pb[bk][:, 0:512],
                                                bchunk(og_sb, cin)[:, tt * 128:(tt + 1) * 128],
                                                wslice(wo_sb, cin, half * 512, 512),
                                                start=(cin == 0), stop=(cin == NCC - 1))
                            return inst
                        group(PE, [b_og, b_w["wo"]], [BK[bk]], s_mm, 1)
                        def s_ev(e, S, c=c, tt=tt, half=half, bk=bk):
                            return e.copy(
                                out=stm_sb[:, tt * C + half * 512:
                                           tt * C + half * 512 + 512],
                                in_=pb[bk][:, 0:512])
                        group(ACT, [BK[bk]], [b_stm[c % 2]], s_ev, 1)
                def s_store(e, S, c=c):
                    return [e.dma_start(
                        out=s_d[c * TCA + tt * 128:c * TCA + (tt + 1) * 128, :],
                        in_=stm_sb[:, tt * C:(tt + 1) * C]) for tt in range(NTA)]
                group(QS, [b_stm[c % 2]], [b_sdd[c]], s_store, 16 * NTA)

                if c + 2 < NCHA:
                    ln0_tail(c + 2)

            barrier()

        esW.close()
        esCB = ExitStack()
        with esCB:
            wc_sb = esCB.enter_context(nc.sbuf_tensor("wc_sb", [128, E * NCC * 1024], BF16))
            b_wc = Buf("wc")

            def wc(e_, cin):
                off = (e_ * NCC + cin) * 1024
                return wc_sb[:, off:off + 1024]

            def wc_ld(e, S):
                return [e.dma_start(
                    out=wc_sb[:, ee * NCC * 1024:(ee + 1) * NCC * 1024],
                    in_=wcall_d[ee * 128:(ee + 1) * 128, :]) for ee in range(E)]
            group(QS, [b_wcall], [b_wc], wc_ld, 16 * E)

            if "wc" in taps:
                group(QG, [b_wc], [],
                      lambda e, S: e.dma_start(out=tap_d["tap_wc"], in_=wc_sb[:]), 16)

            esB = ExitStack()
            with esB:
                sbufB = lambda name, shape, dt: esB.enter_context(nc.sbuf_tensor(name, shape, dt))
                xb_sb = sbufB("xb_sb", [128, 2 * NTB * C], F32)
                sb_sb = sbufB("sb_sb", [128, 2 * NTB * C], BF16)
                xnfmb_sb = sbufB("xnfmb_sb", [128, 2 * NCC * TCB], BF16)
                acc_sb = sbufB("acc_sb", [128, 2 * NTB * C], BF16)
                tmpa_sb = sbufB("tmpa_sb", [128, C], BF16)
                tmpb_sb = sbufB("tmpb_sb", [128, C], BF16)
                sg_sb = sbufB("sg_sb", [128, C], BF16)
                prod_sb = sbufB("prod_sb", [128, C], BF16)
                lg_sb = sbufB("lg_sb", [128, E], F32)
                m1_sb = sbufB("m1_sb", [128, 1], F32)
                eq_sb = sbufB("eq_sb", [128, E], F32)
                lg2_sb = sbufB("lg2_sb", [128, E], F32)
                m2_sb = sbufB("m2_sb", [128, 1], F32)
                mask_sb = sbufB("mask_sb", [128, 2 * NTB * E], F32)

                b_xb = [Buf(f"xb{p}") for p in range(2)]
                b_sb = [Buf(f"sb{p}") for p in range(2)]
                b_fmb = [Buf(f"fmb{p}") for p in range(2)]
                b_acc = [Buf(f"acc{p}") for p in range(2)]
                b_tmpa = Buf("tmpa"); b_tmpb = Buf("tmpb")
                b_sg = Buf("sg"); b_prod = Buf("prod")
                b_lg = Buf("lg"); b_mask = Buf("maskb")
                b_out = [Buf(f"outb{c}") for c in range(NCHB)]

                def xbt(c, tt):
                    return xb_sb[:, ((c % 2) * NTB + tt) * C:((c % 2) * NTB + tt + 1) * C]
                def sbt(c, tt):
                    return sb_sb[:, ((c % 2) * NTB + tt) * C:((c % 2) * NTB + tt + 1) * C]
                def fmb(c, ci):
                    off = ((c % 2) * NCC + ci) * TCB
                    return xnfmb_sb[:, off:off + TCB]
                def acct(c, tt):
                    return acc_sb[:, ((c % 2) * NTB + tt) * C:((c % 2) * NTB + tt + 1) * C]

                def ldB(c):
                    def ld(e, S, c=c):
                        L = [e.dma_start(out=xbt(c, tt),
                                         in_=x_d[c * TCB + tt * 128:c * TCB + (tt + 1) * 128, :])
                             for tt in range(NTB)]
                        L += [e.dma_start(
                            out=sbt(c, tt),
                            in_=s_d[c * TCB + tt * 128:c * TCB + (tt + 1) * 128, :])
                            for tt in range(NTB)]
                        return L
                    group(QS, [b_sdd[c * TCB // TCA]], [b_xb[c % 2], b_sb[c % 2]], ld, 16 * 2 * NTB)
                    def tpr(e, S, c=c):
                        return [e.dma_start(
                            out=fmb(c, ci),
                            in_=xn_d[c * TCB:(c + 1) * TCB, ci * 128:(ci + 1) * 128],
                            transpose=True) for ci in range(NCC)]
                    group(QS, [b_xnd[c * TCB // TCA]], [b_fmb[c % 2]], tpr, 16 * NCC)

                ldB(0)

                def emit_masks(c):
                    for tt in range(NTB):
                        def lg_mm(e, S, c=c, tt=tt):
                            for cin in range(NCC):
                                inst = e.matmul(pb[0][:, 0:E],
                                                fmb(c, cin)[:, tt * 128:(tt + 1) * 128],
                                                gwt_sb[:, cin * E:(cin + 1) * E],
                                                start=(cin == 0), stop=(cin == NCC - 1))
                            return inst
                        group(PE, [b_fmb[c % 2], b_gwt], [BK[0]], lg_mm, 1)
                        group(DVE, [BK[0]], [b_lg, BK[0]],
                              lambda e, S: e.tensor_copy(lg_sb[:], pb[0][:, 0:E]), 1)
                        if "lg" in taps:
                            group(QG, [b_lg], [],
                                  lambda e, S, c=c, tt=tt: e.dma_start(
                                      out=tap_d["tap_lg"][c * TCB + tt * 128:
                                                          c * TCB + (tt + 1) * 128, :],
                                      in_=lg_sb[:]), 16)
                        group(DVE, [b_lg], [b_lg],
                              lambda e, S: e.reduce_max(m1_sb[:, 0:1], lg_sb[:], axis=AX), 1)
                        group(DVE, [b_lg], [b_lg],
                              lambda e, S: e.tensor_scalar(eq_sb[:], lg_sb[:],
                                                           m1_sb[:, 0:1], None, ALU.is_ge), 1)
                        group(DVE, [b_lg], [b_lg],
                              lambda e, S: e.scalar_tensor_tensor(
                                  lg2_sb[:], eq_sb[:], -1e30, lg_sb[:],
                                  ALU.mult, ALU.add), 1)
                        group(DVE, [b_lg], [b_lg],
                              lambda e, S: e.reduce_max(m2_sb[:, 0:1], lg2_sb[:], axis=AX), 1)
                        group(DVE, [b_lg], [b_mask],
                              lambda e, S, c=c, tt=tt: e.tensor_scalar(
                                  mask_sb[:, ((c % 2) * NTB + tt) * E:
                                          ((c % 2) * NTB + tt + 1) * E],
                                  lg_sb[:], m2_sb[:, 0:1], None, ALU.is_ge), 1)

                emit_masks(0)
                for c in range(NCHB):
                    if c + 1 < NCHB:
                        ldB(c + 1)
                    for tt in range(NTB):
                        for pr in range(4):
                            g0 = (pr % 2) * 4
                            e_lo, e_hi = 2 * pr, 2 * pr + 1
                            def moe_mm(e, S, c=c, tt=tt, pr=pr, g0=g0, e_lo=e_lo, e_hi=e_hi):
                                for cin in range(NCC):
                                    st_ = (cin == 0); sp = (cin == NCC - 1)
                                    lhs = fmb(c, cin)[:, tt * 128:(tt + 1) * 128]
                                    e.matmul(pb[g0 + 0][:, 0:512], lhs,
                                             wc(e_lo, cin)[:, 0:512], start=st_, stop=sp)
                                    e.matmul(pb[g0 + 1][:, 0:512], lhs,
                                             wc(e_lo, cin)[:, 512:1024], start=st_, stop=sp)
                                    e.matmul(pb[g0 + 2][:, 0:512], lhs,
                                             wc(e_hi, cin)[:, 0:512], start=st_, stop=sp)
                                    inst = e.matmul(pb[g0 + 3][:, 0:512], lhs,
                                                    wc(e_hi, cin)[:, 512:1024], start=st_,
                                                    stop=sp)
                                return inst
                            group(PE, [b_fmb[c % 2], b_wc],
                                  [BK[g0 + i] for i in range(4)], moe_mm, 1)
                            def mcol(c, tt, ee):
                                off = ((c % 2) * NTB + tt) * E + ee
                                return mask_sb[:, off:off + 1]
                            def moe_ev_a(e, S, c=c, tt=tt, g0=g0, e_lo=e_lo):
                                return [
                                    e.activation(out=tmpa_sb[:, 0:512],
                                                 in_=pb[g0 + 0][:, 0:512],
                                                 func=AF.Identity, scale=mcol(c, tt, e_lo)),
                                    e.activation(out=tmpa_sb[:, 512:1024],
                                                 in_=pb[g0 + 1][:, 0:512],
                                                 func=AF.Identity, scale=mcol(c, tt, e_lo))]
                            group(ACT, [BK[g0], BK[g0 + 1], b_mask],
                                  [b_tmpa, BK[g0], BK[g0 + 1]], moe_ev_a, 2)
                            def moe_ev_b(e, S, c=c, tt=tt, g0=g0, e_hi=e_hi):
                                return [
                                    e.tensor_scalar(tmpb_sb[:, 0:512],
                                                    pb[g0 + 2][:, 0:512],
                                                    mcol(c, tt, e_hi), None, ALU.mult),
                                    e.tensor_scalar(tmpb_sb[:, 512:1024],
                                                    pb[g0 + 3][:, 0:512],
                                                    mcol(c, tt, e_hi), None, ALU.mult)]
                            group(DVE, [BK[g0 + 2], BK[g0 + 3], b_mask],
                                  [b_tmpb, BK[g0 + 2], BK[g0 + 3]], moe_ev_b, 2)
                            if pr == 0:
                                def acc0(e, S, c=c, tt=tt):
                                    return e.tensor_tensor(acct(c, tt), tmpa_sb[:],
                                                           tmpb_sb[:], ALU.add)
                                group(DVE, [b_tmpa, b_tmpb], [b_acc[c % 2]], acc0, 1)
                            else:
                                group(DVE, [b_tmpa, b_acc[c % 2]], [b_acc[c % 2]],
                                      lambda e, S, c=c, tt=tt: e.tensor_tensor(
                                          acct(c, tt), acct(c, tt), tmpa_sb[:], ALU.add), 1)
                                group(DVE, [b_tmpb, b_acc[c % 2]], [b_acc[c % 2]],
                                      lambda e, S, c=c, tt=tt: e.tensor_tensor(
                                          acct(c, tt), acct(c, tt), tmpb_sb[:], ALU.add), 1)

                        if "ffn" in taps:
                            group(QG, [b_acc[c % 2]], [],
                                  lambda e, S, c=c, tt=tt: e.dma_start(
                                      out=tap_d["tap_ffn"][c * TCB + tt * 128:
                                                           c * TCB + (tt + 1) * 128, :],
                                      in_=acct(c, tt)), 16)
                        if "mask" in taps:
                            group(QG, [b_mask], [],
                                  lambda e, S, c=c, tt=tt: e.dma_start(
                                      out=tap_d["tap_mask"][c * TCB + tt * 128:
                                                            c * TCB + (tt + 1) * 128, :],
                                      in_=mask_sb[:, ((c % 2) * NTB + tt) * E:
                                                  ((c % 2) * NTB + tt + 1) * E]), 16)
                        group(ACT, [b_acc[c % 2]], [b_sg],
                              lambda e, S, c=c, tt=tt: e.activation(
                                  out=sg_sb[:], in_=acct(c, tt), func=AF.Sigmoid), 1)
                        group(DVE, [b_sg, b_acc[c % 2]], [b_prod],
                              lambda e, S, c=c, tt=tt: e.tensor_tensor(
                                  prod_sb[:], acct(c, tt), sg_sb[:], ALU.mult), 1)
                        group(DVE, [b_prod, b_sb[c % 2]], [b_prod],
                              lambda e, S, c=c, tt=tt: e.tensor_tensor(
                                  prod_sb[:], prod_sb[:], sbt(c, tt), ALU.mult), 1)
                        group(DVE, [b_prod, b_xb[c % 2]], [b_xb[c % 2]],
                              lambda e, S, c=c, tt=tt: e.tensor_tensor(
                                  xbt(c, tt), xbt(c, tt), prod_sb[:], ALU.add), 1)
                    def out_st(e, S, c=c):
                        return [e.dma_start(
                            out=out_d[c * TCB + tt * 128:c * TCB + (tt + 1) * 128, :],
                            in_=xbt(c, tt)) for tt in range(NTB)]
                    group(QG, [b_xb[c % 2]], [b_out[c]], out_st, 16 * NTB)
                    if c + 1 < NCHB:
                        emit_masks(c + 1)

                if "xn" in taps:
                    group(QG, [b_xnd[NCHA - 1]], [],
                          lambda e, S: e.dma_start(out=tap_d["tap_xn"], in_=xn_d), 16)
                if "s" in taps:
                    group(QG, [b_sdd[NCHA - 1]], [],
                          lambda e, S: e.dma_start(out=tap_d["tap_s"], in_=s_d), 16)

                QG.emit(lambda e, S: e.wait_ge(S["QG"], QG.n))

                block = es.enter_context(nc.Block())
                S = sems
                block.sync(lambda e: [op(e, S) for op in QS.ops])
                block.tensor(lambda e: [op(e, S) for op in PE.ops])
                block.scalar(lambda e: [op(e, S) for op in ACT.ops])
                block.vector(lambda e: [op(e, S) for op in DVE.ops])
                block.gpsimd(lambda e: [op(e, S) for op in QG.ops])

    return nc


def kernel(**inputs):
    in_maps = [prep_core_inputs(b, inputs) for b in range(8)]
    nc = build(ncores=8)
    res = run_bass_kernel_spmd(nc, in_maps, core_ids=list(range(8)))
    return np.stack([res.results[b]["out"] for b in range(8)], axis=0)




# version 19
# speedup vs baseline: 1.1325x; 1.0381x over previous
import numpy as np
import ml_dtypes
import concourse.bass as bass
import concourse.mybir as mybir
from concourse.bass_utils import run_bass_kernel_spmd
from contextlib import ExitStack

F32 = mybir.dt.float32
BF16 = mybir.dt.bfloat16
AF = mybir.ActivationFunctionType
ALU = mybir.AluOpType
AX = mybir.AxisListType.X

T, C, H, D = 4096, 1024, 16, 64
NCC = 8
TCA = 256
NCHA = T // TCA
NTA = TCA // 128
TCB = 256
NCHB = T // TCB
NTB = TCB // 128
E = 8
J = 4096
NJG = J // 512
EXM05 = float(np.exp(np.float32(-0.5)))


def set_dims(t=4096, j=4096):
    global T, NCHA, NCHB, J, NJG
    T, J = t, j
    NCHA = T // TCA
    NCHB = T // TCB
    NJG = J // 512


class Eng:

    def __init__(self, name, queue=False, ops=None):
        self.name = name
        self.queue = queue
        self.n = 0
        self.ops = [] if ops is None else ops
        self.seen = {}

    def self_wait(self):
        if self.n > 0:
            nm, vv = self.name, self.n
            self.ops.append(lambda e, S: e.wait_ge(S[nm], vv))

    def wait(self, other, val):
        if val <= 0:
            return
        if self.seen.get(other.name, 0) >= val:
            return
        self.seen[other.name] = val
        on, vv = other.name, int(val)
        self.ops.append(lambda e, S: e.wait_ge(S[on], vv))

    def emit(self, fn, tag=0):
        if tag:
            self.n += tag
            sname, step = self.name, tag

            def run(e, S, fn=fn, sname=sname, step=step):
                r = fn(e, S)
                if isinstance(r, (list, tuple)):
                    u = step // len(r)
                    assert u * len(r) == step, (step, len(r))
                    for inst in r:
                        inst.then_inc(S[sname], u)
                else:
                    r.then_inc(S[sname], step)

            self.ops.append(run)
        else:
            self.ops.append(lambda e, S, fn=fn: fn(e, S))
        return self.n


class Buf:

    def __init__(self, name=""):
        self.name = name
        self.w = (None, 0)
        self.r = []


def group(eng, reads, writes, fn, tag):
    if eng.queue:
        eng.self_wait()
    for b in reads:
        if b.w[0] is not None:
            eng.wait(b.w[0], b.w[1])
    for b in writes:
        for (oe, ov) in b.r:
            eng.wait(oe, ov)
        if b.w[0] is not None:
            eng.wait(b.w[0], b.w[1])
    eng.emit(fn, tag=tag)
    for b in reads:
        b.r.append((eng, eng.n))
    for b in writes:
        b.w = (eng, eng.n)
        b.r = []


def colmaj(v):
    return np.ascontiguousarray(np.asarray(v, np.float32).reshape(NCC, 128).T)


def prep_core_inputs(b, I):
    _ = (I["L0_attn_v0"], I["L0_attn_v1"], I["L0_attn_v2"])
    g = lambda k: np.asarray(I[k], np.float32)
    w0 = g("L0_attn_w0").reshape(H)
    a0 = g("L0_attn_a0").reshape(H)
    w2 = g("L0_attn_w2")
    a2 = g("L0_attn_a2")
    selx = np.kron(np.eye(2, dtype=np.float32), np.ones((64, 64), np.float32))
    m = {
        "x": np.ascontiguousarray(g("L0_attn_x")[b]),
        "wr": g("L0_attn_Wr"), "wk": g("L0_attn_Wk"),
        "wv": g("L0_attn_Wv"), "wo": g("L0_attn_Wo"),
        "w1": g("L0_attn_w1"), "a1": g("L0_attn_a1"), "g1": g("L0_attn_g1"),
        "g2": g("L0_attn_g2"),
        "w2x": np.ascontiguousarray(np.repeat(w2, D, axis=1)),
        "a2x": np.ascontiguousarray(np.repeat(a2, D, axis=1)),
        "gwt": np.ascontiguousarray(g("L0_ffn_gate_w").T),
        "wupt": np.ascontiguousarray(g("L0_ffn_w_up")[b % 8].T),
        "wdn": np.ascontiguousarray(g("L0_ffn_w_down")[b % 8]),
        "mixr": colmaj(g("L0_attn_x_r")), "mixw": colmaj(g("L0_attn_x_w")),
        "mixk": colmaj(g("L0_attn_x_k")), "mixv": colmaj(g("L0_attn_x_v")),
        "mixa": colmaj(g("L0_attn_x_a")), "mixg": colmaj(g("L0_attn_x_g")),
        "kkc": colmaj(g("L0_attn_k_k")),
        "lnw": colmaj(g("L0_attn_ln_w")), "lnb": colmaj(g("L0_attn_ln_b")),
        "rkc": colmaj(g("L0_attn_r_k").reshape(C)),
        "w0x": colmaj(np.repeat(w0, D)), "a0x": colmaj(np.repeat(a0, D)),
        "selx": selx.astype(ml_dtypes.bfloat16),
        "selx64": (selx / 64.0).astype(ml_dtypes.bfloat16),
        "w0row": np.ascontiguousarray(
            colmaj(np.repeat(w0, D)).T.reshape(1, C)).astype(ml_dtypes.bfloat16),
        "a0row": np.ascontiguousarray(
            colmaj(np.repeat(a0, D)).T.reshape(1, C)).astype(ml_dtypes.bfloat16),
        "kkw": np.ascontiguousarray(
            np.tile(g("L0_attn_k_k").reshape(1, C), (128, 1))).astype(ml_dtypes.bfloat16),
        "rkexp": np.ascontiguousarray(
            np.repeat(colmaj(g("L0_attn_r_k").reshape(C)), TCA, axis=1)).astype(
                ml_dtypes.bfloat16),
    }
    return m


def build(taps=(), ncores=8):
    nc = bass.Bass("TRN2", target_bir_lowering=False, debug=False)
    dt_in = {}
    def din(name, shape, dt=F32):
        dt_in[name] = nc.dram_tensor(name, list(shape), dt, kind="ExternalInput").ap()
        return dt_in[name]

    x_d = din("x", (T, C))
    wr_d = din("wr", (C, C)); wk_d = din("wk", (C, C))
    wv_d = din("wv", (C, C)); wo_d = din("wo", (C, C))
    w1_d = din("w1", (C, 64)); a1_d = din("a1", (C, 64)); g1_d = din("g1", (C, 64))
    g2_d = din("g2", (64, C)); w2x_d = din("w2x", (64, C)); a2x_d = din("a2x", (64, C))
    gwt_d = din("gwt", (C, E))
    wupt_d = din("wupt", (J, C)); wdn_d = din("wdn", (J, C))
    cols_d = {k: din(k, (128, NCC)) for k in
              ["mixr", "mixw", "mixk", "mixv", "mixa", "mixg",
               "kkc", "lnw", "lnb", "rkc", "w0x", "a0x"]}
    selx_d = din("selx", (128, 128), BF16)
    selx64_d = din("selx64", (128, 128), BF16)
    w0row_d = din("w0row", (1, C), BF16)
    a0row_d = din("a0row", (1, C), BF16)
    kkw_d = din("kkw", (128, C), BF16)
    rkexp_d = din("rkexp", (128, NCC * TCA), BF16)

    xn_d = nc.dram_tensor("xn_d", [T, C], BF16, kind="Internal").ap()
    wcown_d = nc.dram_tensor("wcown_d", [128, NCC * 1024], BF16, kind="Internal").ap()
    wcall_d = nc.dram_tensor("wcall_d", [128 * E, NCC * 1024], BF16, kind="Internal").ap()
    s_d = nc.dram_tensor("s_d", [T, C], BF16, kind="Internal").ap()
    out_d = nc.dram_tensor("out", [T, C], F32, kind="ExternalOutput").ap()
    tap_d = {}
    if "xn" in taps:
        tap_d["tap_xn"] = nc.dram_tensor("tap_xn", [T, C], BF16, kind="ExternalOutput").ap()
    if "s" in taps:
        tap_d["tap_s"] = nc.dram_tensor("tap_s", [T, C], BF16, kind="ExternalOutput").ap()
    if "wc" in taps:
        tap_d["tap_wc"] = nc.dram_tensor("tap_wc", [128, E * NCC * 1024], BF16,
                                         kind="ExternalOutput").ap()
    if "ffn" in taps:
        tap_d["tap_ffn"] = nc.dram_tensor("tap_ffn", [T, C], BF16,
                                          kind="ExternalOutput").ap()
    if "mask" in taps:
        tap_d["tap_mask"] = nc.dram_tensor("tap_mask", [T, E], F32,
                                           kind="ExternalOutput").ap()
    if "lg" in taps:
        tap_d["tap_lg"] = nc.dram_tensor("tap_lg", [T, E], F32,
                                         kind="ExternalOutput").ap()

    QS, QG, PE, ACT, DVE = (Eng("QS", queue=True), Eng("QG", queue=True), Eng("PE"), Eng("ACT"), Eng("DVE"))
    PL = Eng("PL", ops=QG.ops)
    CC = Eng("CC", ops=QG.ops)
    ENGS = [QS, QG, PL, CC, PE, ACT, DVE]

    def barrier(exclude=()):
        snap = [(e, e.n) for e in ENGS if e.name not in exclude]
        for e in ENGS:
            if e.name in exclude:
                continue
            if e.queue:
                e.self_wait()
            for (o, v) in snap:
                e.wait(o, v)

    es = ExitStack()
    sbuf = lambda name, shape, dt: es.enter_context(nc.sbuf_tensor(name, shape, dt))
    with es:
        sems = {}
        for e in ENGS:
            sems[e.name] = es.enter_context(nc.semaphore(e.name))

        cols_sb = {k: sbuf(k + "_sb", [128, NCC], F32) for k in cols_d}
        selx_sb = sbuf("selx_sb", [128, 128], BF16)
        selx64_sb = sbuf("selx64_sb", [128, 128], BF16)
        gwt_sb = sbuf("gwt_sb", [128, NCC * E], BF16)
        carry = sbuf("carry", [128, NCC], F32)
        prevc = sbuf("prevc", [128, NCC], F32)
        sumc = sbuf("sumc", [128, NTA], F32)
        sqc = sbuf("sqc", [128, NTA], F32)
        meanc = sbuf("meanc", [128, NTA], F32)
        varc = sbuf("varc", [128, NTA], F32)
        tmpc = sbuf("tmpc", [128, NTA], F32)
        rstdc = sbuf("rstdc", [128, NTA], F32)
        nmrc = sbuf("nmrc", [128, NTA], F32)
        eps5 = sbuf("eps5", [128, 1], F32)
        eps24 = sbuf("eps24", [128, 1], F32)

        pb = [es.enter_context(nc.psum_tensor(f"pb{i}", [128, 512], F32)) for i in range(8)]
        BK = [Buf(f"bank{i}") for i in range(8)]

        b_cols = Buf("cols"); b_selx = Buf("selx"); b_gwt = Buf("gwt")
        b_carry = Buf("carry"); b_prevc = Buf("prevc")
        b_stat = Buf("stat")
        b_xnd = [Buf(f"xnd{c}") for c in range(NCHA)]
        b_sdd = [Buf(f"sdd{c}") for c in range(NCHA)]

        def ld_smalls(e, S):
            L = [e.dma_start(out=cols_sb[k][:], in_=cols_d[k]) for k in cols_d]
            L.append(e.dma_start(out=selx_sb[:], in_=selx_d))
            L.append(e.dma_start(out=selx64_sb[:], in_=selx64_d))
            return L
        group(QS, [], [b_cols, b_selx], ld_smalls, tag=16 * (len(cols_d) + 2))
        group(QG, [], [b_gwt],
              lambda e, S: e.dma_start(
                  out=gwt_sb[:].rearrange("p (cc e) -> p cc e", e=E),
                  in_=gwt_d.rearrange("(cc p) e -> p cc e", p=128)), tag=16)
        b_eps = Buf("eps")
        group(DVE, [], [b_carry, b_prevc, b_eps],
              lambda e, S: [e.memset(carry[:], 0.0), e.memset(prevc[:], 0.0),
                            e.memset(eps5[:], 1e-5), e.memset(eps24[:], 1e-24)], tag=4)

        esW = ExitStack()
        sbw = lambda name, shape, dt: esW.enter_context(nc.sbuf_tensor(name, shape, dt))
        wr_sb = sbw("wr_sb", [128, NCC * C], BF16)
        wk_sb = sbw("wk_sb", [128, NCC * C], BF16)
        wv_sb = sbw("wv_sb", [128, NCC * C], BF16)
        wo_sb = sbw("wo_sb", [128, NCC * C], BF16)
        w1_sb = sbw("w1_sb", [128, NCC * 64], BF16)
        a1_sb = sbw("a1_sb", [128, NCC * 64], BF16)
        g1_sb = sbw("g1_sb", [128, NCC * 64], BF16)
        g2_sb = sbw("g2_sb", [64, C], BF16)
        w2x_sb = sbw("w2x_sb", [64, C], BF16)
        a2x_sb = sbw("a2x_sb", [64, C], BF16)
        x_sb = sbw("x_sb", [128, 3 * NTA * C], BF16)
        b_w = {k: Buf(k) for k in ["wr", "wk", "wv", "wo", "w1", "a1", "g1",
                                   "g2", "w2x", "a2x", "w1v", "a1v", "g1v",
                                   "rows", "kkw", "rkexp", "mxc"]}
        b_x = [Buf(f"x{p}") for p in range(3)]

        def wcast(dst, src_d, n):
            def fn(e, S, d=dst, s=src_d, n=n):
                return e.dma_start(
                    out=d[:].rearrange("p (cc n) -> p cc n", n=n),
                    in_=s.rearrange("(cc p) n -> p cc n", p=128))
            return fn
        group(QG, [], [b_w["w1"]], wcast(w1_sb, w1_d, 64), 16)
        group(QG, [], [b_w["a1"]], wcast(a1_sb, a1_d, 64), 16)
        group(QG, [], [b_w["g1"]], wcast(g1_sb, g1_d, 64), 16)
        group(QG, [], [b_w["g2"]], lambda e, S: e.dma_start(out=g2_sb[:], in_=g2_d), 16)
        group(QG, [], [b_w["w2x"]], lambda e, S: e.dma_start(out=w2x_sb[:], in_=w2x_d), 16)
        group(QG, [], [b_w["a2x"]], lambda e, S: e.dma_start(out=a2x_sb[:], in_=a2x_d), 16)

        def load_x(c):
            def ld(e, S, c=c):
                return [e.dma_start(
                    out=x_sb[:, ((c % 3) * NTA + tt) * C:((c % 3) * NTA + tt + 1) * C],
                    in_=x_d[c * TCA + tt * 128:c * TCA + (tt + 1) * 128, :])
                    for tt in range(NTA)]
            group(QG, [], [b_x[c % 3]], ld, 16 * NTA)
        for cc_ in range(min(3, NCHA)):
            load_x(cc_)
        group(QG, [], [b_w["wr"]], wcast(wr_sb, wr_d, C), 16)
        group(QG, [], [b_w["wk"]], wcast(wk_sb, wk_d, C), 16)
        group(QG, [], [b_w["wv"]], wcast(wv_sb, wv_d, C), 16)
        group(QG, [], [b_w["wo"]], wcast(wo_sb, wo_d, C), 16)

        b_wcall = Buf("wcall")
        esC = ExitStack()
        with esC:
            wcst = esC.enter_context(nc.sbuf_tensor("wcst", [128, NCC * 1024], BF16))
            wdn_bf = esC.enter_context(nc.sbuf_tensor("wdn_bf", [128, 32 * 1024], BF16))
            wup_q = esC.enter_context(nc.sbuf_tensor("wup_q", [128, 2 * 8 * 512], BF16))
            b_wupq = [Buf("wupq0"), Buf("wupq1")]
            b_wdnq = [Buf(f"wdnq{q}") for q in range(4)]
            b_wcown = Buf("wcown"); b_wcst = Buf("wcst")
            NJC = J // 128

            def c0_lddn(q):
                def fn(e, S, q=q):
                    jgs = NJC // 4
                    src_d2 = wdn_d.rearrange("(jg p) n -> jg p n", p=128)[
                        q * jgs:(q + 1) * jgs].rearrange("jg p n -> p jg n")
                    return e.dma_start(
                        out=wdn_bf[:, q * jgs * 1024:(q + 1) * jgs * 1024].rearrange(
                            "p (jg n) -> p jg n", n=1024),
                        in_=src_d2)
                group(QG, [], [b_wdnq[q]], fn, 16)

            for cinh in range(2):
                for q in range(4):
                    jgs = NJC // 4
                    p = q % 2
                    if cinh == 0:
                        c0_lddn(q)
                    def c0_ldup(e, S, cinh=cinh, q=q, p=p, jgs=jgs):
                        src_u = wupt_d.rearrange("(jg p) n -> jg p n", p=128)[
                            q * jgs:(q + 1) * jgs, :,
                            cinh * 512:(cinh + 1) * 512].rearrange("jg p n -> p jg n")
                        return e.dma_start(
                            out=wup_q[:, p * jgs * 512:(p + 1) * jgs * 512].rearrange(
                                "p (jg n) -> p jg n", n=512),
                            in_=src_u)
                    group(QG, [], [b_wupq[p]], c0_ldup, 16)
                    def c0_mm(e, S, cinh=cinh, q=q, p=p, jgs=jgs):
                        for jq in range(jgs):
                            jg = q * jgs + jq
                            for cc in range(4):
                                for half in range(2):
                                    inst = e.matmul(
                                        pb[cc * 2 + half][:, 0:512],
                                        wup_q[:, (p * jgs + jq) * 512 + cc * 128:
                                              (p * jgs + jq) * 512 + cc * 128 + 128],
                                        wdn_bf[:, jg * 1024 + half * 512:
                                               jg * 1024 + half * 512 + 512],
                                        start=(jg == 0), stop=(jg == NJC - 1))
                        return inst
                    group(PE, [b_wupq[p], b_wdnq[q]],
                          [BK[i] for i in range(8)] if q == 0 else [], c0_mm, 1)
                    if q != 0:
                        for i in range(8):
                            BK[i].w = (PE, PE.n)
                def c0_ev(e, S, cinh=cinh):
                    return [e.copy(
                        out=wcst[:, (cinh * 4 + cc) * 1024 + half * 512:
                                 (cinh * 4 + cc) * 1024 + half * 512 + 512],
                        in_=pb[cc * 2 + half][:, 0:512])
                        for cc in range(4) for half in range(2)]
                group(ACT, [BK[i] for i in range(8)],
                      [b_wcst] + [BK[i] for i in range(8)], c0_ev, 8)

            group(QG, [b_wcst], [b_wcown],
                  lambda e, S: e.dma_start(out=wcown_d, in_=wcst[:]), 16)
            group(CC, [b_wcown], [b_wcall],
                  lambda e, S: e.collective_compute(
                      "AllGather", ALU.bypass,
                      [list(range(ncores))], [wcown_d], [wcall_d]), 1)
        barrier(exclude=("CC",))

        TW = TCA + 16
        PW = 2 * TCA
        esA = ExitStack()
        sbufA = lambda name, shape, dt: esA.enter_context(nc.sbuf_tensor(name, shape, dt))
        with esA:
            w1v_sb = sbufA("w1v_sb", [128, 2 * NCC * 64], BF16)
            a1v_sb = sbufA("a1v_sb", [128, 2 * NCC * 64], BF16)
            g1v_sb = sbufA("g1v_sb", [128, 2 * NCC * 64], BF16)
            w0row_sb = sbufA("w0row_sb", [1, C], BF16)
            a0row_sb = sbufA("a0row_sb", [1, C], BF16)
            ones_sb = sbufA("ones_sb", [1, TCA], BF16)
            kkw_sb = sbufA("kkw_sb", [128, C], BF16)
            rkexp_sb = sbufA("rkexp_sb", [128, NCC * TCA], BF16)
            mxc_sb = sbufA("mxc_sb", [128, 6 * NCC], F32)
            xntm_sb = sbufA("xntm_sb", [128, NTA * C], BF16)
            xnfm_sb = sbufA("xnfm_sb", [128, 3 * NCC * TW], BF16)
            sx_sb = sbufA("sx_sb", [128, NCC * TCA], BF16)
            mix_sb = sbufA("mix_sb", [128, 2 * NCC * TCA], BF16)
            kscl_sb = sbufA("kscl_sb", [128, NCC * TCA], BF16)
            vh_sb = sbufA("vh_sb", [128, NCC * TCA], BF16)
            r_sb = sbufA("r_sb", [128, NCC * TCA], BF16)
            g_sb = sbufA("g_sb", [128, 2 * NCC * TCA], BF16)
            ab_sb = sbufA("ab_sb", [128, NCC * TCA], BF16)
            og_sb = sbufA("og_sb", [128, NCC * TCA], BF16)
            wb_sb = sbufA("wb_sb", [128, NCC * TCA], F32)
            tw_sb = sbufA("tw_sb", [64, TCA], BF16)
            ta_sb = sbufA("ta_sb", [64, TCA], BF16)
            tg_sb = sbufA("tg_sb", [64, TCA], BF16)
            lnscr = sbufA("lnscr", [128, C], F32)
            stm_sb = sbufA("stm_sb", [128, NTA * C], BF16)
            scr = {}
            for nm in ["rsn", "kk", "kv", "st", "t1", "xw"]:
                scr[nm] = sbufA("scr_" + nm, [128, 2 * PW], F32)
            scrb = {}
            for nm in ["z", "xwb", "xw2"]:
                scrb[nm] = sbufA("scrb_" + nm, [128, 2 * PW], BF16)

            b_xntm = [Buf("xntm"), Buf("xntm")]
            b_xntm[1] = b_xntm[0]
            b_xnfm = [Buf(f"xnfm{p}") for p in range(3)]
            b_sx = Buf("sx")
            b_mix = [Buf("mix0"), Buf("mix1")]
            b_kscl = Buf("kscl"); b_vh = Buf("vh"); b_r = Buf("r")
            b_g = [Buf("g0"), Buf("g1")]
            b_ab = Buf("ab"); b_og = Buf("og"); b_wb = Buf("wb")
            b_tw = Buf("tw"); b_ta = Buf("ta"); b_tg = Buf("tg")
            b_lnscr = Buf("lnscr")
            b_stm = [Buf("stm"), Buf("stm")]
            b_stm[1] = b_stm[0]
            b_scr = {nm: [Buf(nm + "0"), Buf(nm + "1")] for nm in
                     list(scr) + list(scrb)}

            group(QS, [], [b_w["rows"]], lambda e, S: [
                e.dma_start(out=w0row_sb[:], in_=w0row_d),
                e.dma_start(out=a0row_sb[:], in_=a0row_d),
                e.dma_start(out=kkw_sb[:], in_=kkw_d),
                e.dma_start(out=rkexp_sb[:], in_=rkexp_d)], 64)
            group(DVE, [], [b_w["rows"]],
                  lambda e, S: e.memset(ones_sb[:], 1.0), 1)

            def kfold(e, S):
                return [e.tensor_tensor(wk_sb[:, cc * C:(cc + 1) * C],
                                        wk_sb[:, cc * C:(cc + 1) * C], kkw_sb[:],
                                        ALU.mult) for cc in range(NCC)]
            group(DVE, [b_w["wk"], b_w["rows"]], [b_w["wk"]], kfold, NCC)
            def mxc(e, S):
                L = []
                for i, mk in enumerate(["mixw", "mixa", "mixg"]):
                    L.append(e.tensor_scalar(mxc_sb[:, (2 * i) * NCC:(2 * i + 1) * NCC],
                                             cols_sb[mk][:], 1.0, None, ALU.add))
                    L.append(e.tensor_scalar(mxc_sb[:, (2 * i + 1) * NCC:(2 * i + 2) * NCC],
                                             cols_sb[mk][:], -1.0, None, ALU.mult))
                return L
            group(DVE, [b_cols], [b_w["mxc"]], mxc, 6)
            def lrv(e, S):
                L = []
                for i, (wsb, vsb) in enumerate([(w1_sb, w1v_sb), (a1_sb, a1v_sb),
                                                (g1_sb, g1v_sb)]):
                    for v in range(2):
                        for cc in range(NCC):
                            L.append(e.tensor_scalar(
                                vsb[:, (v * NCC + cc) * 64:(v * NCC + cc + 1) * 64],
                                wsb[:, cc * 64:(cc + 1) * 64],
                                mxc_sb[:, (2 * i + v) * NCC + cc:
                                       (2 * i + v) * NCC + cc + 1],
                                None, ALU.mult))
                return L
            group(DVE, [b_w["w1"], b_w["a1"], b_w["g1"], b_w["mxc"]],
                  [b_w["w1v"], b_w["a1v"], b_w["g1v"]], lrv, 6 * NCC)

            def xt(c, tt):
                return x_sb[:, ((c % 3) * NTA + tt) * C:((c % 3) * NTA + tt + 1) * C]
            def xntm(c, tt):
                return xntm_sb[:, tt * C:(tt + 1) * C]
            def fmtile(c, ci):
                off = ((c % 3) * NCC + ci) * TW
                return xnfm_sb[:, off:off + TW]
            def xnv(c, ci):
                return fmtile(c, ci)[:, 16:TW]
            def xnsh(c, ci):
                return fmtile(c, ci)[:, 15:TW - 1]
            def sxv(ci):
                return sx_sb[:, ci * TCA:(ci + 1) * TCA]
            def mixv(sl, ci):
                return mix_sb[:, (sl * NCC + ci) * TCA:(sl * NCC + ci + 1) * TCA]
            def bchunk(buf, ci, w=TCA):
                return buf[:, ci * w:(ci + 1) * w]
            def bpair(buf, pp):
                return buf[:, pp * PW:(pp + 1) * PW]
            def scrv(nm, pp):
                t = scr[nm] if nm in scr else scrb[nm]
                return t[:, (pp % 2) * PW:((pp % 2) + 1) * PW]
            def scrh(nm, pp, h):
                return scrv(nm, pp)[:, h * TCA:(h + 1) * TCA]
            def col(k, ci):
                return cols_sb[k][:, ci:ci + 1]
            def wslice(wsb, cin, j0, jw):
                return wsb[:, cin * C + j0:cin * C + j0 + jw]

            def ln0_tail(c):
                bx, bxn, bfm = b_x[c % 3], b_xntm[c % 2], b_xnfm[c % 3]
                bst = b_stat
                for tt in range(NTA):
                    group(ACT, [bx], [b_lnscr, bst],
                          lambda e, S, tt=tt: e.activation(
                              out=lnscr[:], in_=xt(c, tt), func=AF.Square,
                              accum_out=sqc[:, tt:tt + 1]), 1)
                    group(ACT, [bx], [b_lnscr, bst],
                          lambda e, S, tt=tt: e.activation(
                              out=lnscr[:], in_=xt(c, tt), func=AF.Identity,
                              accum_out=sumc[:, tt:tt + 1]), 1)
                group(DVE, [bst], [bst], lambda e, S: [
                    e.tensor_scalar(meanc[:], sumc[:], 1.0 / C, None, ALU.mult),
                    e.tensor_scalar(varc[:], sqc[:], 1.0 / C, None, ALU.mult)], 2)
                group(DVE, [bst], [bst],
                      lambda e, S: e.tensor_tensor(tmpc[:], meanc[:], meanc[:], ALU.mult), 1)
                group(DVE, [bst], [bst],
                      lambda e, S: e.tensor_tensor(varc[:], varc[:], tmpc[:], ALU.subtract), 1)
                group(ACT, [bst, b_eps], [bst],
                      lambda e, S: e.activation(out=tmpc[:], in_=varc[:], func=AF.Ln,
                                                bias=eps5[:, 0:1]), 1)
                group(ACT, [bst], [bst],
                      lambda e, S: e.activation(out=rstdc[:], in_=tmpc[:], func=AF.Exp,
                                                scale=-0.5), 1)
                group(DVE, [bst], [bst],
                      lambda e, S: e.scalar_tensor_tensor(
                          nmrc[:], meanc[:], -1.0, rstdc[:], ALU.mult, ALU.mult), 1)
                def apply(e, S):
                    return [e.activation(out=xntm(c, tt), in_=xt(c, tt), func=AF.Identity,
                                         scale=rstdc[:, tt:tt + 1], bias=nmrc[:, tt:tt + 1])
                            for tt in range(NTA)]
                group(ACT, [bx, bst], [bxn], apply, NTA)
                def store(e, S):
                    return [e.dma_start(out=xn_d[c * TCA + tt * 128:c * TCA + (tt + 1) * 128, :],
                                        in_=xntm(c, tt)) for tt in range(NTA)]
                group(QS, [bxn], [b_xnd[c]], store, 16 * NTA)
                def tpr(e, S):
                    return [e.dma_start(out=xnv(c, ci),
                                        in_=xn_d[c * TCA:(c + 1) * TCA, ci * 128:(ci + 1) * 128],
                                        transpose=True) for ci in range(NCC)]
                group(QS, [b_xnd[c]], [bfm], tpr, 16 * NCC)

            for cc_ in range(min(2, NCHA)):
                ln0_tail(cc_)

            for c in range(NCHA):
                if c + 3 < NCHA:
                    load_x(c + 3)
                bfm = b_xnfm[c % 3]
                if c == 0:
                    group(DVE, [], [bfm], lambda e, S: [
                        e.memset(fmtile(0, ci)[:, 0:16], 0.0) for ci in range(NCC)], NCC)
                else:
                    bprev = b_xnfm[(c - 1) % 3]
                    group(DVE, [bprev], [bfm], lambda e, S, c=c: [
                        e.tensor_copy(fmtile(c, ci)[:, 0:16],
                                      fmtile(c - 1, ci)[:, TW - 16:TW])
                        for ci in range(NCC)], NCC)
                group(DVE, [bfm], [b_sx], lambda e, S, c=c: [
                    e.tensor_tensor(sxv(ci), xnv(c, ci), xnsh(c, ci), ALU.subtract)
                    for ci in range(NCC)], NCC)

                for pi, (vsb, vbuf, tdst, tb) in enumerate(
                        [(w1v_sb, b_w["w1v"], tw_sb, b_tw),
                         (a1v_sb, b_w["a1v"], ta_sb, b_ta),
                         (g1v_sb, b_w["g1v"], tg_sb, b_tg)]):
                    def lr_mm(e, S, c=c, vsb=vsb):
                        L = []
                        for cin in range(NCC):
                            L.append(e.matmul(
                                pb[0][0:64, 0:TCA],
                                vsb[:, cin * 64:(cin + 1) * 64],
                                xnv(c, cin), start=(cin == 0), stop=False))
                            L.append(e.matmul(
                                pb[0][0:64, 0:TCA],
                                vsb[:, (NCC + cin) * 64:(NCC + cin + 1) * 64],
                                xnsh(c, cin), start=False,
                                stop=(cin == NCC - 1)))
                        return L[-1]
                    group(PE, [bfm, vbuf], [BK[0]], lr_mm, 1)
                    group(ACT, [BK[0]], [tb],
                          lambda e, S, tdst=tdst: e.activation(
                              out=tdst[:], in_=pb[0][0:64, 0:TCA], func=AF.Tanh), 1)
                gofs = (c % 2) * NCC * TCA
                for pi, (xsb, xb, tsrc, tsb, rowsb, dst, db) in enumerate(
                        [(w2x_sb, b_w["w2x"], tw_sb, b_tw, w0row_sb, wb_sb, b_wb),
                         (a2x_sb, b_w["a2x"], ta_sb, b_ta, a0row_sb, ab_sb, b_ab),
                         (g2_sb, b_w["g2"], tg_sb, b_tg, None, None, b_g[c % 2])]):
                    for pp in range(4):
                        bk = pp % 2
                        def s2_mm(e, S, pp=pp, bk=bk, xsb=xsb, tsrc=tsrc, rowsb=rowsb):
                            L = []
                            for h in range(2):
                                ci = 2 * pp + h
                                L.append(e.matmul(pb[bk][:, h * TCA:(h + 1) * TCA],
                                                  xsb[:, ci * 128:(ci + 1) * 128],
                                                  tsrc[:], start=True,
                                                  stop=(rowsb is None)))
                                if rowsb is not None:
                                    L.append(e.matmul(
                                        pb[bk][:, h * TCA:(h + 1) * TCA],
                                        rowsb[0:1, ci * 128:(ci + 1) * 128],
                                        ones_sb[0:1, :], start=False, stop=True))
                            return L[-1]
                        group(PE, [tsb, xb, b_w["rows"]], [BK[bk]], s2_mm, 1)
                        def sig_ev(e, S, pp=pp, bk=bk, dst=dst, gofs=gofs):
                            d = g_sb[:, gofs + pp * PW:gofs + (pp + 1) * PW] \
                                if dst is None else bpair(dst, pp)
                            return e.activation(out=d, in_=pb[bk][:, 0:PW],
                                                func=AF.Sigmoid)
                        group(ACT, [BK[bk]], [db], sig_ev, 1)
                group(DVE, [b_wb], [b_wb],
                      lambda e, S: e.tensor_scalar(wb_sb[:], wb_sb[:], EXM05,
                                                   None, ALU.mult), 1)

                MIX3 = [("mixk", "k"), ("mixv", "v"), ("mixr", "r")]
                WMAP = {"k": (wk_sb, b_w["wk"], kscl_sb, b_kscl),
                        "v": (wv_sb, b_w["wv"], vh_sb, b_vh),
                        "r": (wr_sb, b_w["wr"], r_sb, b_r)}
                def emit_mix(mi):
                    mk, tag = MIX3[mi]
                    sl = mi % 2
                    def mix_fn(e, S, c=c, mk=mk, sl=sl):
                        return [e.scalar_tensor_tensor(
                            mixv(sl, ci), sxv(ci), col(mk, ci), xnv(c, ci),
                            ALU.mult, ALU.add) for ci in range(NCC)]
                    group(DVE, [b_sx, bfm, b_cols], [b_mix[sl]], mix_fn, NCC)
                def emit_bigmm(mi):
                    mk, tag = MIX3[mi]
                    sl = mi % 2
                    wsb, wbf, dst, db = WMAP[tag]
                    for pp in range(4):
                        bk = pp % 2
                        def big_mm(e, S, sl=sl, pp=pp, bk=bk, wsb=wsb):
                            for h in range(2):
                                co = 2 * pp + h
                                for cin in range(NCC):
                                    inst = e.matmul(pb[bk][:, h * TCA:(h + 1) * TCA],
                                                    wslice(wsb, cin, co * 128, 128),
                                                    mixv(sl, cin),
                                                    start=(cin == 0), stop=(cin == NCC - 1))
                            return inst
                        group(PE, [b_mix[sl], wbf], [BK[bk]], big_mm, 1)
                        def c_ev(e, S, pp=pp, bk=bk, dst=dst):
                            return e.tensor_copy(bpair(dst, pp), pb[bk][:, 0:PW])
                        group(DVE, [BK[bk]], [db], c_ev, 1)
                emit_mix(0); emit_mix(1)
                emit_bigmm(0)
                emit_mix(2)
                emit_bigmm(1)
                emit_bigmm(2)

                def st_ksq(pp, s2, bkA, bkB, bkC):
                    group(ACT, [b_kscl], [b_scr["z"][s2]],
                          lambda e, S, pp=pp: e.activation(out=scrv("z", pp),
                                                           in_=bpair(kscl_sb, pp),
                                                           func=AF.Square), 1)
                def st_ssq(pp, s2, bkA, bkB, bkC):
                    group(PE, [b_scr["z"][s2], b_selx], [BK[bkA]],
                          lambda e, S, pp=pp, bkA=bkA: [
                              e.matmul(pb[bkA][:, h * TCA:(h + 1) * TCA], selx_sb[:],
                                       scrh("z", pp, h), start=True, stop=True)
                              for h in range(2)][-1], 1)
                def st_rsnln(pp, s2, bkA, bkB, bkC):
                    group(ACT, [BK[bkA], b_eps], [b_scr["rsn"][s2]],
                          lambda e, S, pp=pp, bkA=bkA: e.activation(
                              out=scrv("rsn", pp), in_=pb[bkA][:, 0:PW], func=AF.Ln,
                              bias=eps24[:, 0:1]), 1)
                def st_rsnexp(pp, s2, bkA, bkB, bkC):
                    group(ACT, [b_scr["rsn"][s2]], [b_scr["rsn"][s2]],
                          lambda e, S, pp=pp: e.activation(
                              out=scrv("rsn", pp), in_=scrv("rsn", pp),
                              func=AF.Exp, scale=-0.5), 1)
                def st_kk(pp, s2, bkA, bkB, bkC):
                    group(PL, [b_scr["rsn"][s2], b_kscl], [b_scr["kk"][s2]],
                          lambda e, S, pp=pp: e.tensor_tensor(
                              scrv("kk", pp), bpair(kscl_sb, pp), scrv("rsn", pp),
                              ALU.mult), 1)
                def st_kv1(pp, s2, bkA, bkB, bkC):
                    group(PL, [b_scr["kk"][s2], b_vh], [b_scr["kv"][s2]],
                          lambda e, S, pp=pp: e.tensor_tensor(
                              scrv("kv", pp), scrv("kk", pp), bpair(vh_sb, pp),
                              ALU.mult), 1)
                def st_kv2(pp, s2, bkA, bkB, bkC):
                    group(PL, [b_scr["kv"][s2], b_ab], [b_scr["kv"][s2]],
                          lambda e, S, pp=pp: e.tensor_tensor(
                              scrv("kv", pp), scrv("kv", pp), bpair(ab_sb, pp),
                              ALU.mult), 1)
                def st_scan(pp, s2, bkA, bkB, bkC):
                    for h in range(2):
                        ci = 2 * pp + h
                        group(DVE, [b_wb, b_scr["kv"][s2], b_carry], [b_scr["st"][s2]],
                              lambda e, S, pp=pp, h=h, ci=ci: e.tensor_tensor_scan(
                                  scrh("st", pp, h), bchunk(wb_sb, ci),
                                  scrh("kv", pp, h), carry[:, ci:ci + 1],
                                  ALU.mult, ALU.add), 1)
                        group(DVE, [b_scr["st"][s2]], [b_carry],
                              lambda e, S, pp=pp, h=h, ci=ci: e.tensor_copy(
                                  carry[:, ci:ci + 1],
                                  scrh("st", pp, h)[:, TCA - 1:TCA]), 1)
                def st_z1(pp, s2, bkA, bkB, bkC):
                    group(PL, [b_r, b_scr["kk"][s2]], [b_scr["z"][s2]],
                          lambda e, S, pp=pp: e.tensor_tensor(
                              scrv("z", pp), bpair(r_sb, pp), scrv("kk", pp),
                              ALU.mult), 1)
                def st_z2(pp, s2, bkA, bkB, bkC):
                    group(PL, [b_scr["z"][s2], b_w["rkexp"]], [b_scr["z"][s2]],
                          lambda e, S, pp=pp: e.tensor_tensor(
                              scrv("z", pp), scrv("z", pp), bpair(rkexp_sb, pp),
                              ALU.mult), 1)
                def st_bb(pp, s2, bkA, bkB, bkC):
                    group(PE, [b_scr["z"][s2], b_selx], [BK[bkA]],
                          lambda e, S, pp=pp, bkA=bkA: [
                              e.matmul(pb[bkA][:, h * TCA:(h + 1) * TCA], selx_sb[:],
                                       scrh("z", pp, h), start=True, stop=True)
                              for h in range(2)][-1], 1)
                def st_t1(pp, s2, bkA, bkB, bkC):
                    group(PL, [b_scr["st"][s2], b_r], [b_scr["t1"][s2]],
                          lambda e, S, pp=pp: e.tensor_tensor(
                              scrv("t1", pp), scrv("st", pp), bpair(r_sb, pp),
                              ALU.mult), 1)
                def st_xw1(pp, s2, bkA, bkB, bkC):
                    group(DVE, [b_vh, BK[bkA]], [b_scr["xw"][s2]],
                          lambda e, S, pp=pp, bkA=bkA: e.tensor_tensor(
                              scrv("xw", pp), bpair(vh_sb, pp), pb[bkA][:, 0:PW],
                              ALU.mult), 1)
                def st_xw2(pp, s2, bkA, bkB, bkC):
                    group(PL, [b_scr["t1"][s2], b_scr["xw"][s2]], [b_scr["xw"][s2]],
                          lambda e, S, pp=pp: e.tensor_tensor(
                              scrv("xw", pp), scrv("xw", pp), scrv("t1", pp),
                              ALU.add), 1)
                def st_xwb(pp, s2, bkA, bkB, bkC):
                    group(ACT, [b_scr["xw"][s2]], [b_scr["xwb"][s2], b_scr["xw2"][s2]],
                          lambda e, S, pp=pp: [
                              e.copy(out=scrv("xwb", pp), in_=scrv("xw", pp)),
                              e.activation(out=scrv("xw2", pp), in_=scrv("xw", pp),
                                           func=AF.Square)], 2)
                def st_mb(pp, s2, bkA, bkB, bkC):
                    group(PE, [b_scr["xwb"][s2], b_selx], [BK[bkB]],
                          lambda e, S, pp=pp, bkB=bkB: [
                              e.matmul(pb[bkB][:, h * TCA:(h + 1) * TCA], selx64_sb[:],
                                       scrh("xwb", pp, h), start=True, stop=True)
                              for h in range(2)][-1], 1)
                def st_s64(pp, s2, bkA, bkB, bkC):
                    group(PE, [b_scr["xw2"][s2], b_selx], [BK[bkC]],
                          lambda e, S, pp=pp, bkC=bkC: [
                              e.matmul(pb[bkC][:, h * TCA:(h + 1) * TCA], selx64_sb[:],
                                       scrh("xw2", pp, h), start=True, stop=True)
                              for h in range(2)][-1], 1)
                def st_mbsq(pp, s2, bkA, bkB, bkC):
                    group(ACT, [BK[bkB]], [b_scr["rsn"][s2]],
                          lambda e, S, pp=pp, bkB=bkB: e.activation(
                              out=scrv("rsn", pp), in_=pb[bkB][:, 0:PW],
                              func=AF.Square), 1)
                def st_var(pp, s2, bkA, bkB, bkC):
                    group(DVE, [BK[bkC], b_scr["rsn"][s2]], [b_scr["kv"][s2]],
                          lambda e, S, pp=pp, bkC=bkC: e.tensor_tensor(
                              scrv("kv", pp), pb[bkC][:, 0:PW], scrv("rsn", pp),
                              ALU.subtract), 1)
                def st_var2(pp, s2, bkA, bkB, bkC):
                    group(DVE, [b_scr["kv"][s2]], [b_scr["kv"][s2]],
                          lambda e, S, pp=pp: e.tensor_scalar(
                              scrv("kv", pp), scrv("kv", pp), 0.0, 1e-5,
                              ALU.max, ALU.add), 1)
                def st_gnln(pp, s2, bkA, bkB, bkC):
                    group(ACT, [b_scr["kv"][s2]], [b_scr["rsn"][s2]],
                          lambda e, S, pp=pp: e.activation(
                              out=scrv("rsn", pp), in_=scrv("kv", pp), func=AF.Ln), 1)
                def st_gnexp(pp, s2, bkA, bkB, bkC):
                    group(ACT, [b_scr["rsn"][s2]], [b_scr["kk"][s2]],
                          lambda e, S, pp=pp: e.activation(
                              out=scrv("kk", pp), in_=scrv("rsn", pp),
                              func=AF.Exp, scale=-0.5), 1)
                def st_gsub(pp, s2, bkA, bkB, bkC):
                    group(DVE, [b_scr["xw"][s2], BK[bkB]], [b_scr["t1"][s2]],
                          lambda e, S, pp=pp, bkB=bkB: e.tensor_tensor(
                              scrv("t1", pp), scrv("xw", pp), pb[bkB][:, 0:PW],
                              ALU.subtract), 1)
                def st_gmul(pp, s2, bkA, bkB, bkC):
                    group(PL, [b_scr["t1"][s2], b_scr["kk"][s2]], [b_scr["t1"][s2]],
                          lambda e, S, pp=pp: e.tensor_tensor(
                              scrv("t1", pp), scrv("t1", pp), scrv("kk", pp),
                              ALU.mult), 1)
                def st_lnr(pp, s2, bkA, bkB, bkC):
                    group(ACT, [b_scr["t1"][s2], b_cols], [b_scr["st"][s2]],
                          lambda e, S, pp=pp: [
                              e.activation(out=scrh("st", pp, h),
                                           in_=scrh("t1", pp, h),
                                           func=AF.Identity,
                                           scale=col("lnw", 2 * pp + h),
                                           bias=col("lnb", 2 * pp + h))
                              for h in range(2)], 2)
                def st_og(pp, s2, bkA, bkB, bkC):
                    gofs2 = (c % 2) * NCC * TCA
                    group(PL, [b_scr["st"][s2], b_g[c % 2]], [b_og],
                          lambda e, S, pp=pp, gofs2=gofs2: e.tensor_tensor(
                              bpair(og_sb, pp), scrv("st", pp),
                              g_sb[:, gofs2 + pp * PW:gofs2 + (pp + 1) * PW],
                              ALU.mult), 1)

                STAGES = [st_ksq, st_ssq, st_rsnln, st_rsnexp, st_kk, st_kv1,
                          st_kv2, st_scan, st_z1, st_z2, st_bb, st_t1, st_xw1,
                          st_xw2, st_xwb, st_mb, st_s64, st_mbsq, st_var,
                          st_var2, st_gnln, st_gnexp, st_gsub, st_gmul,
                          st_lnr, st_og]
                for wv in range(2):
                    for st_fn in STAGES:
                        for pp in (2 * wv, 2 * wv + 1):
                            s2 = pp % 2
                            st_fn(pp, s2, [2, 5][s2], [3, 6][s2], [4, 7][s2])

                for tt in range(NTA):
                    for half in range(2):
                        bk = [2, 5, 3, 6][(tt * 2 + half) % 4]
                        def s_mm(e, S, tt=tt, half=half, bk=bk):
                            for cin in range(NCC):
                                inst = e.matmul(pb[bk][:, 0:512],
                                                bchunk(og_sb, cin)[:, tt * 128:(tt + 1) * 128],
                                                wslice(wo_sb, cin, half * 512, 512),
                                                start=(cin == 0), stop=(cin == NCC - 1))
                            return inst
                        group(PE, [b_og, b_w["wo"]], [BK[bk]], s_mm, 1)
                        def s_ev(e, S, c=c, tt=tt, half=half, bk=bk):
                            return e.copy(
                                out=stm_sb[:, tt * C + half * 512:
                                           tt * C + half * 512 + 512],
                                in_=pb[bk][:, 0:512])
                        group(ACT, [BK[bk]], [b_stm[c % 2]], s_ev, 1)
                def s_store(e, S, c=c):
                    return [e.dma_start(
                        out=s_d[c * TCA + tt * 128:c * TCA + (tt + 1) * 128, :],
                        in_=stm_sb[:, tt * C:(tt + 1) * C]) for tt in range(NTA)]
                group(QS, [b_stm[c % 2]], [b_sdd[c]], s_store, 16 * NTA)

                if c + 2 < NCHA:
                    ln0_tail(c + 2)

            barrier()

        esW.close()
        esCB = ExitStack()
        with esCB:
            wc_sb = esCB.enter_context(nc.sbuf_tensor("wc_sb", [128, E * NCC * 1024], BF16))
            b_wc = Buf("wc")

            def wc(e_, cin):
                off = (e_ * NCC + cin) * 1024
                return wc_sb[:, off:off + 1024]

            def wc_ld(e, S):
                return [e.dma_start(
                    out=wc_sb[:, ee * NCC * 1024:(ee + 1) * NCC * 1024],
                    in_=wcall_d[ee * 128:(ee + 1) * 128, :]) for ee in range(E)]
            group(QS, [b_wcall], [b_wc], wc_ld, 16 * E)

            if "wc" in taps:
                group(QG, [b_wc], [],
                      lambda e, S: e.dma_start(out=tap_d["tap_wc"], in_=wc_sb[:]), 16)

            esB = ExitStack()
            with esB:
                sbufB = lambda name, shape, dt: esB.enter_context(nc.sbuf_tensor(name, shape, dt))
                xb_sb = sbufB("xb_sb", [128, 2 * NTB * C], F32)
                sb_sb = sbufB("sb_sb", [128, 2 * NTB * C], BF16)
                xnfmb_sb = sbufB("xnfmb_sb", [128, 2 * NCC * TCB], BF16)
                acc_sb = sbufB("acc_sb", [128, 2 * NTB * C], BF16)
                tmpa_sb = sbufB("tmpa_sb", [128, C], BF16)
                tmpb_sb = sbufB("tmpb_sb", [128, C], BF16)
                sg_sb = sbufB("sg_sb", [128, C], BF16)
                prod_sb = sbufB("prod_sb", [128, C], BF16)
                lg_sb = sbufB("lg_sb", [128, E], F32)
                m1_sb = sbufB("m1_sb", [128, 1], F32)
                eq_sb = sbufB("eq_sb", [128, E], F32)
                lg2_sb = sbufB("lg2_sb", [128, E], F32)
                m2_sb = sbufB("m2_sb", [128, 1], F32)
                mask_sb = sbufB("mask_sb", [128, 2 * NTB * E], F32)

                b_xb = [Buf(f"xb{p}") for p in range(2)]
                b_sb = [Buf(f"sb{p}") for p in range(2)]
                b_fmb = [Buf(f"fmb{p}") for p in range(2)]
                b_acc = [Buf(f"acc{p}") for p in range(2)]
                b_tmpa = Buf("tmpa"); b_tmpb = Buf("tmpb")
                b_sg = Buf("sg"); b_prod = Buf("prod")
                b_lg = Buf("lg"); b_mask = Buf("maskb")
                b_out = [Buf(f"outb{c}") for c in range(NCHB)]

                def xbt(c, tt):
                    return xb_sb[:, ((c % 2) * NTB + tt) * C:((c % 2) * NTB + tt + 1) * C]
                def sbt(c, tt):
                    return sb_sb[:, ((c % 2) * NTB + tt) * C:((c % 2) * NTB + tt + 1) * C]
                def fmb(c, ci):
                    off = ((c % 2) * NCC + ci) * TCB
                    return xnfmb_sb[:, off:off + TCB]
                def acct(c, tt):
                    return acc_sb[:, ((c % 2) * NTB + tt) * C:((c % 2) * NTB + tt + 1) * C]

                def ldB(c):
                    def ld(e, S, c=c):
                        L = [e.dma_start(out=xbt(c, tt),
                                         in_=x_d[c * TCB + tt * 128:c * TCB + (tt + 1) * 128, :])
                             for tt in range(NTB)]
                        L += [e.dma_start(
                            out=sbt(c, tt),
                            in_=s_d[c * TCB + tt * 128:c * TCB + (tt + 1) * 128, :])
                            for tt in range(NTB)]
                        return L
                    group(QS, [b_sdd[c * TCB // TCA]], [b_xb[c % 2], b_sb[c % 2]], ld, 16 * 2 * NTB)
                    def tpr(e, S, c=c):
                        return [e.dma_start(
                            out=fmb(c, ci),
                            in_=xn_d[c * TCB:(c + 1) * TCB, ci * 128:(ci + 1) * 128],
                            transpose=True) for ci in range(NCC)]
                    group(QS, [b_xnd[c * TCB // TCA]], [b_fmb[c % 2]], tpr, 16 * NCC)

                ldB(0)

                def emit_masks(c):
                    for tt in range(NTB):
                        def lg_mm(e, S, c=c, tt=tt):
                            for cin in range(NCC):
                                inst = e.matmul(pb[0][:, 0:E],
                                                fmb(c, cin)[:, tt * 128:(tt + 1) * 128],
                                                gwt_sb[:, cin * E:(cin + 1) * E],
                                                start=(cin == 0), stop=(cin == NCC - 1))
                            return inst
                        group(PE, [b_fmb[c % 2], b_gwt], [BK[0]], lg_mm, 1)
                        group(DVE, [BK[0]], [b_lg, BK[0]],
                              lambda e, S: e.tensor_copy(lg_sb[:], pb[0][:, 0:E]), 1)
                        if "lg" in taps:
                            group(QG, [b_lg], [],
                                  lambda e, S, c=c, tt=tt: e.dma_start(
                                      out=tap_d["tap_lg"][c * TCB + tt * 128:
                                                          c * TCB + (tt + 1) * 128, :],
                                      in_=lg_sb[:]), 16)
                        group(DVE, [b_lg], [b_lg],
                              lambda e, S: e.reduce_max(m1_sb[:, 0:1], lg_sb[:], axis=AX), 1)
                        group(DVE, [b_lg], [b_lg],
                              lambda e, S: e.tensor_scalar(eq_sb[:], lg_sb[:],
                                                           m1_sb[:, 0:1], None, ALU.is_ge), 1)
                        group(DVE, [b_lg], [b_lg],
                              lambda e, S: e.scalar_tensor_tensor(
                                  lg2_sb[:], eq_sb[:], -1e30, lg_sb[:],
                                  ALU.mult, ALU.add), 1)
                        group(DVE, [b_lg], [b_lg],
                              lambda e, S: e.reduce_max(m2_sb[:, 0:1], lg2_sb[:], axis=AX), 1)
                        group(DVE, [b_lg], [b_mask],
                              lambda e, S, c=c, tt=tt: e.tensor_scalar(
                                  mask_sb[:, ((c % 2) * NTB + tt) * E:
                                          ((c % 2) * NTB + tt + 1) * E],
                                  lg_sb[:], m2_sb[:, 0:1], None, ALU.is_ge), 1)

                emit_masks(0)
                for c in range(NCHB):
                    if c + 1 < NCHB:
                        ldB(c + 1)
                    for tt in range(NTB):
                        for pr in range(4):
                            g0 = (pr % 2) * 4
                            e_lo, e_hi = 2 * pr, 2 * pr + 1
                            def moe_mm(e, S, c=c, tt=tt, pr=pr, g0=g0, e_lo=e_lo, e_hi=e_hi):
                                for cin in range(NCC):
                                    st_ = (cin == 0); sp = (cin == NCC - 1)
                                    lhs = fmb(c, cin)[:, tt * 128:(tt + 1) * 128]
                                    e.matmul(pb[g0 + 0][:, 0:512], lhs,
                                             wc(e_lo, cin)[:, 0:512], start=st_, stop=sp)
                                    e.matmul(pb[g0 + 1][:, 0:512], lhs,
                                             wc(e_lo, cin)[:, 512:1024], start=st_, stop=sp)
                                    e.matmul(pb[g0 + 2][:, 0:512], lhs,
                                             wc(e_hi, cin)[:, 0:512], start=st_, stop=sp)
                                    inst = e.matmul(pb[g0 + 3][:, 0:512], lhs,
                                                    wc(e_hi, cin)[:, 512:1024], start=st_,
                                                    stop=sp)
                                return inst
                            group(PE, [b_fmb[c % 2], b_wc],
                                  [BK[g0 + i] for i in range(4)], moe_mm, 1)
                            def mcol(c, tt, ee):
                                off = ((c % 2) * NTB + tt) * E + ee
                                return mask_sb[:, off:off + 1]
                            def moe_ev_a(e, S, c=c, tt=tt, g0=g0, e_lo=e_lo):
                                return [
                                    e.activation(out=tmpa_sb[:, 0:512],
                                                 in_=pb[g0 + 0][:, 0:512],
                                                 func=AF.Identity, scale=mcol(c, tt, e_lo)),
                                    e.activation(out=tmpa_sb[:, 512:1024],
                                                 in_=pb[g0 + 1][:, 0:512],
                                                 func=AF.Identity, scale=mcol(c, tt, e_lo))]
                            group(ACT, [BK[g0], BK[g0 + 1], b_mask],
                                  [b_tmpa, BK[g0], BK[g0 + 1]], moe_ev_a, 2)
                            def moe_ev_b(e, S, c=c, tt=tt, g0=g0, e_hi=e_hi):
                                return [
                                    e.tensor_scalar(tmpb_sb[:, 0:512],
                                                    pb[g0 + 2][:, 0:512],
                                                    mcol(c, tt, e_hi), None, ALU.mult),
                                    e.tensor_scalar(tmpb_sb[:, 512:1024],
                                                    pb[g0 + 3][:, 0:512],
                                                    mcol(c, tt, e_hi), None, ALU.mult)]
                            group(DVE, [BK[g0 + 2], BK[g0 + 3], b_mask],
                                  [b_tmpb, BK[g0 + 2], BK[g0 + 3]], moe_ev_b, 2)
                            if pr == 0:
                                def acc0(e, S, c=c, tt=tt):
                                    return e.tensor_tensor(acct(c, tt), tmpa_sb[:],
                                                           tmpb_sb[:], ALU.add)
                                group(DVE, [b_tmpa, b_tmpb], [b_acc[c % 2]], acc0, 1)
                            else:
                                group(DVE, [b_tmpa, b_acc[c % 2]], [b_acc[c % 2]],
                                      lambda e, S, c=c, tt=tt: e.tensor_tensor(
                                          acct(c, tt), acct(c, tt), tmpa_sb[:], ALU.add), 1)
                                group(DVE, [b_tmpb, b_acc[c % 2]], [b_acc[c % 2]],
                                      lambda e, S, c=c, tt=tt: e.tensor_tensor(
                                          acct(c, tt), acct(c, tt), tmpb_sb[:], ALU.add), 1)

                        if "ffn" in taps:
                            group(QG, [b_acc[c % 2]], [],
                                  lambda e, S, c=c, tt=tt: e.dma_start(
                                      out=tap_d["tap_ffn"][c * TCB + tt * 128:
                                                           c * TCB + (tt + 1) * 128, :],
                                      in_=acct(c, tt)), 16)
                        if "mask" in taps:
                            group(QG, [b_mask], [],
                                  lambda e, S, c=c, tt=tt: e.dma_start(
                                      out=tap_d["tap_mask"][c * TCB + tt * 128:
                                                            c * TCB + (tt + 1) * 128, :],
                                      in_=mask_sb[:, ((c % 2) * NTB + tt) * E:
                                                  ((c % 2) * NTB + tt + 1) * E]), 16)
                        group(ACT, [b_acc[c % 2]], [b_sg],
                              lambda e, S, c=c, tt=tt: e.activation(
                                  out=sg_sb[:], in_=acct(c, tt), func=AF.Sigmoid), 1)
                        group(DVE, [b_sg, b_acc[c % 2]], [b_prod],
                              lambda e, S, c=c, tt=tt: e.tensor_tensor(
                                  prod_sb[:], acct(c, tt), sg_sb[:], ALU.mult), 1)
                        group(DVE, [b_prod, b_sb[c % 2]], [b_prod],
                              lambda e, S, c=c, tt=tt: e.tensor_tensor(
                                  prod_sb[:], prod_sb[:], sbt(c, tt), ALU.mult), 1)
                        group(DVE, [b_prod, b_xb[c % 2]], [b_xb[c % 2]],
                              lambda e, S, c=c, tt=tt: e.tensor_tensor(
                                  xbt(c, tt), xbt(c, tt), prod_sb[:], ALU.add), 1)
                    def out_st(e, S, c=c):
                        return [e.dma_start(
                            out=out_d[c * TCB + tt * 128:c * TCB + (tt + 1) * 128, :],
                            in_=xbt(c, tt)) for tt in range(NTB)]
                    group(QG, [b_xb[c % 2]], [b_out[c]], out_st, 16 * NTB)
                    if c + 1 < NCHB:
                        emit_masks(c + 1)

                if "xn" in taps:
                    group(QG, [b_xnd[NCHA - 1]], [],
                          lambda e, S: e.dma_start(out=tap_d["tap_xn"], in_=xn_d), 16)
                if "s" in taps:
                    group(QG, [b_sdd[NCHA - 1]], [],
                          lambda e, S: e.dma_start(out=tap_d["tap_s"], in_=s_d), 16)

                QG.emit(lambda e, S: e.wait_ge(S["QG"], QG.n))

                block = es.enter_context(nc.Block())
                S = sems
                block.sync(lambda e: [op(e, S) for op in QS.ops])
                block.tensor(lambda e: [op(e, S) for op in PE.ops])
                block.scalar(lambda e: [op(e, S) for op in ACT.ops])
                block.vector(lambda e: [op(e, S) for op in DVE.ops])
                block.gpsimd(lambda e: [op(e, S) for op in QG.ops])

    return nc


def kernel(**inputs):
    in_maps = [prep_core_inputs(b, inputs) for b in range(8)]
    nc = build(ncores=8)
    res = run_bass_kernel_spmd(nc, in_maps, core_ids=list(range(8)))
    return np.stack([res.results[b]["out"] for b in range(8)], axis=0)


